# version 15
# baseline (speedup 1.0000x reference)
import numpy as np
from contextlib import ExitStack

import concourse.bass as bass
import concourse.mybir as mybir
from concourse.bass_utils import run_bass_kernel_spmd
from concourse.aot_env import allow_ici_all_to_all

F32 = mybir.dt.float32
BF16 = mybir.dt.bfloat16
AF = mybir.ActivationFunctionType
ALU = mybir.AluOpType
AX = mybir.AxisListType

NEG = -1e30
C_SM = float(1.0 / np.sqrt(128.0))


class Cfg:
    def __init__(self, D=2048, FF=8192, NH=16, S=4096, B=2):
        self.D, self.FF, self.NH, self.S, self.B = D, FF, NH, S, B
        self.DT = D // 128
        self.T = S // 4
        self.HPC = NH // 4
        self.QT = S // 512
        self.KB = S // 128
        self.FQ = FF // 8 // 128
        self.TH = self.T // 512
        self.VW = min(256, (self.NH // 4) * 128)


CFG = Cfg()



def _pack_lhsT(w_lhsT, nblk):
    K, M = w_lhsT.shape
    dt = K // 128
    a = w_lhsT.reshape(dt, 128, nblk, 128).transpose(2, 1, 0, 3)
    return np.ascontiguousarray(a.reshape(nblk, dt * 128 * 128))


def _pack_rhs(w, nchunk, width):
    K, M = w.shape
    dt = K // 128
    a = w.reshape(dt, 128, nchunk, width).transpose(2, 1, 0, 3)
    return np.ascontiguousarray(a.reshape(nchunk, dt * 128 * width))


def _pack_down(wd_T, ff, d):
    fq = ff // 8 // 128
    dbn = d // 128
    a = wd_T.reshape(8, fq, 128, dbn, 128).transpose(0, 3, 2, 1, 4)
    return np.ascontiguousarray(a.reshape(8 * dbn, 128 * fq * 128))


def _make_masks01():
    m = np.zeros((128, 4, 512), dtype=np.float32)
    p = np.arange(128)[:, None]
    qf = np.arange(512)[None, :]
    for s in range(4):
        m[:, s, :] = np.where(s * 128 + p <= qf, 1.0, 0.0)
    return np.ascontiguousarray(m.reshape(128, 4 * 512))


def host_inputs(cfg, inputs):
    f = np.float32
    d = {}
    pairs = (("A", np.ascontiguousarray(np.asarray(inputs["L0_attn_a_W_qkv"], f).T),
              np.asarray(inputs["L0_attn_a_b_qkv"], f)),
             ("B", np.ascontiguousarray(np.asarray(inputs["L0_attn_b_w_qkv"], f)),
              np.asarray(inputs["L0_attn_b_b_qkv"], f)))
    for tag, w, b in pairs:
        d[f"wqk{tag}"] = _pack_lhsT(w[:, : 2 * cfg.D], 2 * cfg.DT)
        d[f"wv{tag}"] = _pack_rhs(w[:, 2 * cfg.D:], cfg.D // cfg.VW, cfg.VW)
        d[f"bqk{tag}"] = np.ascontiguousarray(b[: 2 * cfg.D].reshape(2 * cfg.DT, 128).T)
        d[f"bv{tag}"] = np.ascontiguousarray(b[2 * cfg.D:].reshape(cfg.NH, 128).T)
    d["woA"] = _pack_lhsT(np.ascontiguousarray(np.asarray(inputs["L0_attn_a_W_out"], f).T), cfg.DT)
    d["woB"] = _pack_lhsT(np.ascontiguousarray(np.asarray(inputs["L0_attn_b_w_proj"], f)), cfg.DT)
    d["boA"] = np.ascontiguousarray(np.asarray(inputs["L0_attn_a_b_out"], f).reshape(cfg.DT, 128).T)
    d["boB"] = np.ascontiguousarray(np.asarray(inputs["L0_attn_b_b_proj"], f).reshape(cfg.DT, 128).T)
    d["wg"] = _pack_lhsT(np.ascontiguousarray(np.asarray(inputs["L0_ffn_w_gate"], f).T), cfg.FF // 128)
    d["wu"] = _pack_lhsT(np.ascontiguousarray(np.asarray(inputs["L0_ffn_w_up"], f).T), cfg.FF // 128)
    d["wd"] = _pack_down(np.ascontiguousarray(np.asarray(inputs["L0_ffn_w_down"], f).T), cfg.FF, cfg.D)
    d["masks"] = _make_masks01()
    d["identity"] = np.eye(128, dtype=f)
    return d



SEM_NAMES = ("pe", "act", "dve", "gp", "cc", "ld", "msk",
             "qs0", "qs1", "vs0", "vs1", "os0", "os1",
             "wc0", "wc1", "uc0", "uc1", "dc0", "dc1", "vc", "vc1",
             "wl0", "wl1", "dl0", "dl1",
             "fwb0", "fwb1", "dwb0", "dwb1",
             "hl", "hl1", "vla", "vla1", "vlb", "vlb1", "ota", "otb", "out")


class Emitter:
    ENGS = ("sync", "tensor", "scalar", "vector", "gpsimd")

    def __init__(self):
        self.q = {e: [] for e in self.ENGS}
        self.ctr = {k: 0 for k in SEM_NAMES}
        self.lastwait = {e: {} for e in self.ENGS}
        self.sems = {}

    def emit(self, eng, fn):
        self.q[eng].append(fn)

    def wait(self, eng, sem, val):
        if val is None or val <= 0:
            return
        lw = self.lastwait[eng]
        if lw.get(sem, 0) >= val:
            return
        lw[sem] = val
        s = self.sems[sem]

        def f(h, s=s, v=val):
            h.wait_ge(s, v)
        self.q[eng].append(f)

    def snap(self):
        return dict(self.ctr)

    def wait_snap(self, eng, snap, names):
        for n in names:
            self.wait(eng, n, snap[n])

    def bump(self, name, by=1):
        self.ctr[name] += by
        return self.ctr[name]


def build_program(cfg):
    nc = bass.Bass("TRN2", target_bir_lowering=False, debug=False)
    E = Emitter()
    DT, T, QT, KB, HPC, FQ, TH = cfg.DT, cfg.T, cfg.QT, cfg.KB, cfg.HPC, cfg.FQ, cfg.TH
    VW = cfg.VW
    D, FF, NH, S = cfg.D, cfg.FF, cfg.NH, cfg.S
    SLAB = DT * 128 * 128
    VSLAB = DT * 128 * VW
    DSLAB = 128 * FQ * 128

    xT_h = nc.dram_tensor("xT", [D, T], F32, kind="ExternalInput")
    yT_h = nc.dram_tensor("yT", [D, T], F32, kind="ExternalOutput")
    wqk_h = {0: nc.dram_tensor("wqkA", [2 * DT, SLAB], F32, kind="ExternalInput"),
             1: nc.dram_tensor("wqkB", [2 * DT, SLAB], F32, kind="ExternalInput")}
    wv_h = {0: nc.dram_tensor("wvA", [D // VW, VSLAB], F32, kind="ExternalInput"),
            1: nc.dram_tensor("wvB", [D // VW, VSLAB], F32, kind="ExternalInput")}
    bqk_h = {0: nc.dram_tensor("bqkA", [128, 2 * DT], F32, kind="ExternalInput"),
             1: nc.dram_tensor("bqkB", [128, 2 * DT], F32, kind="ExternalInput")}
    bv_h = {0: nc.dram_tensor("bvA", [128, NH], F32, kind="ExternalInput"),
            1: nc.dram_tensor("bvB", [128, NH], F32, kind="ExternalInput")}
    wo_h = {0: nc.dram_tensor("woA", [DT, SLAB], F32, kind="ExternalInput"),
            1: nc.dram_tensor("woB", [DT, SLAB], F32, kind="ExternalInput")}
    bo_h = {0: nc.dram_tensor("boA", [128, DT], F32, kind="ExternalInput"),
            1: nc.dram_tensor("boB", [128, DT], F32, kind="ExternalInput")}
    wg_h = nc.dram_tensor("wg", [FF // 128, SLAB], F32, kind="ExternalInput")
    wu_h = nc.dram_tensor("wu", [FF // 128, SLAB], F32, kind="ExternalInput")
    wd_h = nc.dram_tensor("wd", [8 * DT, DSLAB], F32, kind="ExternalInput")
    masks_h = nc.dram_tensor("masks", [128, 4 * 512], F32, kind="ExternalInput")
    ident_h = nc.dram_tensor("identity", [128, 128], F32, kind="ExternalInput")

    NCK = D // 512
    CH = 4 * 512 * T
    VBASE = 2 * NCK * CH
    qkv_local = nc.dram_tensor("qkv_local", [3 * D, T], BF16)
    qkv_gath = nc.dram_tensor("qkv_gath", [4 * 3 * D, T], BF16)
    o_local = nc.dram_tensor("o_local", [HPC * 128, S], BF16)
    o_gath = nc.dram_tensor("o_gath", [4 * HPC * 128, S], BF16)
    wg_bf = nc.dram_tensor("wg_bf", [FF // 128, SLAB], BF16)
    wu_bf = nc.dram_tensor("wu_bf", [FF // 128, SLAB], BF16)
    wd_bf = nc.dram_tensor("wd_bf", [8 * DT, DSLAB], BF16)

    es = ExitStack()
    with es:
        sbt = lambda name, shape, dt: es.enter_context(nc.sbuf_tensor(name, shape, dt))
        x_sb = sbt("x_sb", [128, DT * T], F32)
        xn_sb = sbt("xn_sb", [128, max(DT, NH) * T], BF16)
        sst_sb = sbt("sst_sb", [128, 6 * 512], F32)
        arena = sbt("arena", [128, max(5 * KB * 128, FQ * T)], BF16)
        WB = DT * 128
        wt = sbt("wt", [128, max(4 * WB + 2 * FQ * 128, 2 * WB + DT * VW) + DT * VW], BF16)
        stage = sbt("stage", [128, 2 * 1024], BF16)
        vstage = sbt("vstage", [128, 2 * 512], BF16)
        mask01_sb = sbt("mask01_sb", [128, 4 * 512], BF16)
        ident_sb = sbt("ident_sb", [128, 128], F32)
        bqk_sb = {ly: sbt(f"bqk{ly}_sb", [128, 2 * DT], F32) for ly in (0, 1)}
        bo_sb = {ly: sbt(f"bo{ly}_sb", [128, DT], F32) for ly in (0, 1)}
        ones_c_bf = sbt("ones_c_bf", [128, 1], BF16)
        ones_r_f = sbt("ones_r_f", [1, 128], F32)
        onescf = sbt("onescf", [128, 1], F32)
        Mc_sb = sbt("Mc_sb", [128, 1], F32)
        Mp_sb = sbt("Mp_sb", [128, 8], F32)
        mu_sb = sbt("mu_sb", [1, 3], F32)
        nmu_sb = sbt("nmu_sb", [1, 1], F32)
        f_sb = sbt("f_sb", [1, 2], F32)
        fbc_sb = sbt("fbc_sb", [128, 2], F32)
        bias_sb = sbt("bias_sb", [128, 2], F32)
        packA = sbt("packA", [128, 2 * T], F32)
        packD = sbt("packD", [128, 1536], F32)
        bvh_sb = {ly: sbt(f"bvh{ly}_sb", [128, NH], F32) for ly in (0, 1)}
        mrow_sb = packA[0:1, 0:T]
        rstd_sb = packA[0:1, T:2 * T]
        ex2_sb = packA[32:33, 0:T]
        msq_sb = packA[32:33, T:2 * T]
        var_sb = ex2_sb
        sd_sb = msq_sb
        rec_sb = packD[0:1, 0:512]
        recb_sb = packD[:, 512:1024]

        ps = [es.enter_context(nc.psum_tensor(f"ps{i}", [128, 1024], F32)) for i in range(4)]

        for name in SEM_NAMES:
            E.sems[name] = es.enter_context(nc.semaphore("sem_" + name))
        sems = E.sems

        QKV_EL = KB * 128
        Q_views = [arena[:, 0:QKV_EL], arena[:, QKV_EL:2 * QKV_EL]]
        K_views = [arena[:, 2 * QKV_EL:3 * QKV_EL], arena[:, 3 * QKV_EL:4 * QKV_EL]]
        V_view = arena[:, 4 * QKV_EL:5 * QKV_EL]
        gated_view = arena[:, 0:FQ * T]
        wbuf = [wt[:, 0:WB], wt[:, WB:2 * WB]]
        WTB = max(4 * WB + 2 * FQ * 128, 2 * WB + DT * VW)
        vwbufs = [wt[:, 2 * WB:2 * WB + DT * VW], wt[:, WTB:WTB + DT * VW]]
        uwbuf = [wt[:, 2 * WB:3 * WB], wt[:, 3 * WB:4 * WB]]
        dwbuf = [wt[:, 4 * WB:4 * WB + FQ * 128], wt[:, 4 * WB + FQ * 128:4 * WB + 2 * FQ * 128]]
        qstage = [stage[:, 0:1024], stage[:, 1024:2048]]
        silustage = qstage
        lnb = [sst_sb[:, 1024:1536].bitcast(BF16), sst_sb[:, 1536:2048].bitcast(BF16)]
        lnsq = [sst_sb[:, 2048:2560].bitcast(BF16), sst_sb[:, 2560:3072].bitcast(BF16)]
        vst = [vstage[:, 0:VW], vstage[:, VW:2 * VW]]
        ost = [vstage[:, 0:512], vstage[:, 512:1024]]
        pring = stage[:, 0:2048]

        RT = {}

        def p_init():
            def mkregs(h, tag):
                g1 = h.register(f"r_rk_{tag}")
                RT[f"_g1{tag}"] = g1
                rk = g1.__enter__()
                g2 = h.register(f"r_off_{tag}")
                RT[f"_g2{tag}"] = g2
                off = g2.__enter__()
                pid = h.partition_id()
                h.reg_mov(rk, pid % 4)
                RT[f"rk_{tag}"] = rk
                RT[f"off_{tag}"] = off

            def roff(h, tag, mult, add):
                h.reg_mov(RT[f"off_{tag}"], RT[f"rk_{tag}"])
                if mult != 1:
                    h.reg_alu(RT[f"off_{tag}"], RT[f"off_{tag}"], mult, ALU.mult)
                if add:
                    h.reg_alu(RT[f"off_{tag}"], RT[f"off_{tag}"], add, ALU.add)
                return RT[f"off_{tag}"]
            RT["roff"] = roff
            for eng, tag in (("sync", "sp"), ("scalar", "act"), ("gpsimd", "pl")):
                def fr(h, tag=tag):
                    mkregs(h, tag)
                E.emit(eng, fr)

            def fx(h):
                h.dma_start(out=x_sb[:, :].rearrange("p (t n) -> p t n", t=DT),
                            in_=xT_h.ap().rearrange("(t p) n -> p t n", p=128)).then_inc(sems["ld"], 16)
            E.emit("sync", fx); E.bump("ld", 16)
            for dst, src in [(ident_sb, ident_h), (bqk_sb[0], bqk_h[0]), (bqk_sb[1], bqk_h[1]),
                             (bo_sb[0], bo_h[0]), (bo_sb[1], bo_h[1]),
                             (bvh_sb[0], bv_h[0]), (bvh_sb[1], bv_h[1])]:
                def f2(h, dst=dst, src=src):
                    h.dma_start(out=dst[:, :], in_=src.ap()).then_inc(sems["ld"], 16)
                E.emit("sync", f2); E.bump("ld", 16)

            def fm(h):
                h.dma_start(out=mask01_sb[:, :], in_=masks_h.ap()).then_inc(sems["msk"], 16)
            E.emit("gpsimd", fm); E.bump("msk", 16)

            def fo(h):
                h.memset(ones_c_bf[:, :], 1.0).then_inc(sems["dve"], 1)
                h.memset(ones_r_f[:, :], 1.0).then_inc(sems["dve"], 1)
                h.memset(onescf[:, :], 1.0).then_inc(sems["dve"], 1)
            E.emit("vector", fo); E.bump("dve", 3)

        def p_ln(prev=None):
            ent = E.snap()
            pe_evt = {}
            for dt in range(DT):
                par = dt % 2
                if dt == 0:
                    E.wait_snap("scalar", ent, ("ld", "qs0", "qs1"))
                if dt < 2:
                    E.wait("scalar", "pe", last_ln_pe[0])
                if prev is not None:
                    E.wait("scalar", "dve", prev[dt])
                else:
                    E.wait_snap("scalar", ent, ("dve",))
                if dt >= 2:
                    E.wait("scalar", "pe", pe_evt[dt - 2])
                def fc(h, dt=dt, par=par):
                    h.copy(out=lnb[par][:, :], in_=x_sb[:, dt * T:(dt + 1) * T])
                    h.square(out=lnsq[par][:, :],
                             in_=x_sb[:, dt * T:(dt + 1) * T]).then_inc(sems["act"], 2)
                E.emit("scalar", fc); E.bump("act", 2)
                E.wait("tensor", "act", E.ctr["act"])
                if dt == 0:
                    E.wait("tensor", "act", ent["act"])
                    E.wait("tensor", "dve", prev[0] if prev is not None else ent["dve"])
                first, last = dt == 0, dt == DT - 1
                def fmm(h, par=par, first=first, last=last):
                    for hf in range(TH):
                        h.matmul(ps[2][0:1, hf * 512:hf * 512 + 512], ones_c_bf[:, :],
                                 lnb[par][:, hf * 512:hf * 512 + 512], start=first, stop=last)
                    for hf in range(TH):
                        ins = h.matmul(ps[3][0:1, hf * 512:hf * 512 + 512], ones_c_bf[:, :],
                                       lnsq[par][:, hf * 512:hf * 512 + 512], start=first, stop=last)
                    ins.then_inc(sems["pe"], 1)
                E.emit("tensor", fmm); pe_evt[dt] = E.bump("pe")
            last_ln_pe[0] = E.ctr["pe"]
            E.wait("vector", "pe", E.ctr["pe"])
            inv = 1.0 / float(D)
            def f1(h):
                for hf in range(TH):
                    h.tensor_scalar_mul(out=mrow_sb[0:1, hf * 512:hf * 512 + 512],
                                        in0=ps[2][0:1, hf * 512:hf * 512 + 512], scalar1=inv)
                for hf in range(TH):
                    ins = h.tensor_scalar_mul(out=ex2_sb[0:1, hf * 512:hf * 512 + 512],
                                              in0=ps[3][0:1, hf * 512:hf * 512 + 512], scalar1=inv)
                ins.then_inc(sems["dve"], 1)
            E.emit("vector", f1); E.bump("dve")
            E.wait("scalar", "dve", E.ctr["dve"])
            def f2(h):
                h.square(out=msq_sb[:, :], in_=mrow_sb[:, :]).then_inc(sems["act"], 1)
            E.emit("scalar", f2); E.bump("act")
            E.wait("vector", "act", E.ctr["act"])
            def f3(h):
                h.scalar_tensor_tensor(out=var_sb[:, :], in0=ex2_sb[:, :], scalar=1e-5,
                                       in1=msq_sb[:, :], op0=ALU.add,
                                       op1=ALU.subtract).then_inc(sems["dve"], 1)
            E.emit("vector", f3); E.bump("dve")
            E.wait("scalar", "dve", E.ctr["dve"])
            def f4(h):
                h.activation(out=sd_sb[:, :], in_=var_sb[:, :], func=AF.Sqrt).then_inc(sems["act"], 1)
            E.emit("scalar", f4); E.bump("act")
            E.wait("vector", "act", E.ctr["act"])
            def f5(h):
                h.reciprocal(out=rstd_sb[:, :], in_=sd_sb[:, :]).then_inc(sems["dve"], 1)
            E.emit("vector", f5); E.bump("dve")
            E.wait("tensor", "dve", E.ctr["dve"])
            def f7(h):
                for hf in range(TH):
                    h.matmul(ps[0][:, hf * 512:hf * 512 + 512], ones_r_f[:, :],
                             mrow_sb[0:1, hf * 512:hf * 512 + 512], start=True, stop=True)
                for hf in range(TH):
                    ins = h.matmul(ps[1][:, hf * 512:hf * 512 + 512], ones_r_f[:, :],
                                   rstd_sb[0:1, hf * 512:hf * 512 + 512], start=True, stop=True)
                ins.then_inc(sems["pe"], 1)
            E.emit("tensor", f7); E.bump("pe")
            E.wait("vector", "pe", E.ctr["pe"])
            dve_base = E.ctr["dve"]
            t1_sb = sst_sb[:, 0:T]
            for dt in range(DT):
                def fn(h, dt=dt, w=dve_base + 2 * dt, t1_sb=t1_sb):
                    h.wait_ge(sems["dve"], w)
                    h.tensor_sub(out=t1_sb[:, :], in0=x_sb[:, dt * T:(dt + 1) * T],
                                 in1=ps[0][:, 0:T]).then_inc(sems["dve"], 1)
                    h.wait_ge(sems["dve"], w + 1)
                    h.tensor_mul(out=xn_sb[:, dt * T:(dt + 1) * T], in0=t1_sb[:, :],
                                 in1=ps[1][:, 0:T]).then_inc(sems["dve"], 1)
                E.emit("vector", fn); E.bump("dve", 2)
            return E.ctr["dve"]

        def wcast(dst, src_h, off, tiles, width, sem):
            def f(h, dst=dst, src_h=src_h, off=off, tiles=tiles, width=width, sem=sem):
                h.dma_start(out=dst[:, :].rearrange("p (t f) -> p t f", t=tiles),
                            in_=bass.AP(src_h, off, [[tiles * width, 128], [width, tiles], [1, width]])
                            ).then_inc(sems[sem], 16)
            E.emit("gpsimd", f)
            return E.bump(sem, 16)

        def p_attn(ly, xn_dve):
            ent = E.snap()
            E.wait("sync", "cc", ent["cc"])
            cc_base = ent["cc"]
            v_pe, v_act = {}, {}
            E.wait("tensor", "ld", ent["ld"])
            for fc in range(D // VW):
                vpar = fc % 2
                vwbuf = vwbufs[vpar]
                vcs = "vc" if vpar == 0 else "vc1"
                if fc >= 2:
                    E.wait("gpsimd", "pe", v_pe[(fc - 2, T // 128 - 1)])
                else:
                    E.wait_snap("gpsimd", ent, ("pe", "fwb0", "fwb1", "dwb0", "dwb1",
                                                "wl0", "wl1", "dl0", "dl1"))
                wev = wcast(vwbuf, wv_h[ly], fc * VSLAB, DT, VW, vcs)
                E.wait("tensor", vcs, wev)
                for tb in range(T // 128):
                    par = tb % 2
                    psv = ps[2 + par]
                    if tb >= 2:
                        E.wait("tensor", "act", v_act[(fc, tb - 2)])
                    elif fc > 0:
                        E.wait("tensor", "act", v_act[(fc - 1, T // 128 - 2 + tb)])
                    else:
                        E.wait_snap("tensor", ent, ("dve", "act"))
                    def fmm(h, tb=tb, fc=fc, psv=psv, vwbuf=vwbuf):
                        for dt in range(DT):
                            ins = h.matmul(psv[:, 0:VW],
                                           xn_sb[:, dt * T + tb * 128: dt * T + (tb + 1) * 128],
                                           vwbuf[:, dt * VW:(dt + 1) * VW],
                                           start=(dt == 0), stop=(dt == DT - 1))
                        ins.then_inc(sems["pe"], 1)
                    E.emit("tensor", fmm); v_pe[(fc, tb)] = E.bump("pe")
                    E.wait("scalar", "pe", v_pe[(fc, tb)])
                    E.wait("scalar", f"vs{par}", E.ctr[f"vs{par}"])
                    E.wait("scalar", "os0", E.ctr["os0"])
                    def fev(h, par=par, psv=psv):
                        h.copy(out=vst[par][:, :], in_=psv[:, 0:VW]).then_inc(sems["act"], 1)
                    E.emit("scalar", fev); v_act[(fc, tb)] = E.bump("act")
                    E.wait("sync", "act", v_act[(fc, tb)])
                    def fst(h, fc=fc, tb=tb, par=par):
                        voff = 2 * D * T + (tb * 128) * D + fc * VW
                        h.dma_start(out=bass.AP(qkv_local, voff, [[D, 128], [1, VW]]),
                                    in_=vst[par][:, :]).then_inc(sems[f"vs{par}"], 16)
                    E.emit("sync", fst); E.bump(f"vs{par}", 16)
            vs_snap = (E.ctr["vs0"], E.ctr["vs1"])
            def fccv(h):
                h.collective_compute("AllGather", ALU.bypass,
                                     replica_groups=[[0, 1, 2, 3], [4, 5, 6, 7]],
                                     ins=[qkv_local[2 * D:3 * D, :].opt()],
                                     outs=[bass.AP(qkv_gath, VBASE, [[1, 4 * D * T]])]
                                     ).then_inc(sems["cc"])
            pe_evt, act_evt = {}, {}
            fb_order = list(range(DT, 2 * DT)) + list(range(DT))
            pending_cc = []
            for it, fb in enumerate(fb_order):
                par = it % 2
                wsem = f"wc{par}"
                if it >= 2:
                    E.wait("gpsimd", "pe", pe_evt[it - 2])
                else:
                    E.wait_snap("gpsimd", ent, ("pe", "fwb0", "fwb1", "wl0", "wl1"))
                if it == 2:
                    E.wait("gpsimd", "vs0", vs_snap[0])
                    E.wait("gpsimd", "vs1", vs_snap[1])
                    for s in ("hl", "vla", "vlb"):
                        E.wait("gpsimd", s, E.ctr[s])
                    E.emit("gpsimd", fccv); E.bump("cc")
                while pending_cc and pending_cc[0][0] <= it:
                    _, fcc_fn, q0, q1 = pending_cc.pop(0)
                    E.wait("gpsimd", "qs0", q0)
                    E.wait("gpsimd", "qs1", q1)
                    E.emit("gpsimd", fcc_fn); E.bump("cc")
                wev = wcast(wbuf[par], wqk_h[ly], fb * SLAB, DT, 128, wsem)
                E.wait("tensor", wsem, wev)
                E.wait("tensor", "dve", xn_dve)
                if it >= 2:
                    E.wait("tensor", "act", act_evt[it - 2])
                def fmm(h, par=par):
                    for dt in range(DT):
                        for hf in range(TH):
                            ins = h.matmul(ps[par][:, hf * 512:hf * 512 + 512],
                                           wbuf[par][:, dt * 128:(dt + 1) * 128],
                                           xn_sb[:, dt * T + hf * 512: dt * T + hf * 512 + 512],
                                           start=(dt == 0), stop=(dt == DT - 1))
                    ins.then_inc(sems["pe"], 1)
                E.emit("tensor", fmm); pe_evt[it] = E.bump("pe")
                E.wait("scalar", "pe", pe_evt[it])
                E.wait("scalar", f"qs{par}", E.ctr[f"qs{par}"])
                E.wait_snap("scalar", ent, ("dve",))
                def fev(h, fb=fb, par=par):
                    h.add(out=qstage[par][:, 0:T], in_=ps[par][:, 0:T],
                          add=bqk_sb[ly][:, fb:fb + 1]).then_inc(sems["act"], 1)
                E.emit("scalar", fev); act_evt[it] = E.bump("act")
                E.wait("sync", "act", act_evt[it])
                def fst(h, fb=fb, par=par):
                    h.dma_start(out=bass.AP(qkv_local, fb * 128 * T, [[T, 128], [1, T]]),
                                in_=qstage[par][:, 0:T]).then_inc(sems[f"qs{par}"], 16)
                E.emit("sync", fst); E.bump(f"qs{par}", 16)
                if it % 4 == 3:
                    ck = it // 4
                    in_row = fb_order[it - 3] * 128
                    def fcck(h, ck=ck, in_row=in_row):
                        h.collective_compute("AllGather", ALU.bypass,
                                             replica_groups=[[0, 1, 2, 3], [4, 5, 6, 7]],
                                             ins=[qkv_local[in_row:in_row + 512, :].opt()],
                                             outs=[qkv_gath[ck * 2048:(ck + 1) * 2048, :].opt()]
                                             ).then_inc(sems["cc"])
                    pending_cc.append((max(it + 4, 20 + 5 * ck), fcck,
                                       E.ctr["qs0"], E.ctr["qs1"]))
            for _, fcc_fn, q0, q1 in pending_cc:
                E.wait("gpsimd", "qs0", q0)
                E.wait("gpsimd", "qs1", q1)
                E.emit("gpsimd", fcc_fn); E.bump("cc")
            pending_cc = []
            cc_all = E.ctr["cc"]
            E.wait("sync", "cc", cc_all)
            E.wait("scalar" if ly == 0 else "gpsimd", "cc", cc_all)
            head_done_pe = E.ctr["pe"]
            a4ent = E.snap()
            SBANKS = [ps[0][:, 0:512], ps[0][:, 512:1024],
                      ps[1][:, 0:512], ps[1][:, 512:1024],
                      ps[2][:, 0:512], ps[2][:, 512:1024]]
            SPAIRS = [ps[0], ps[1], ps[2]]
            first_scores = [True]
            fr3_guard = [0]
            last_bank_exp = [a4ent["act"]] * 6
            av_ring = []
            ring_ctr = [0]
            veng = "scalar" if ly == 0 else "gpsimd"
            vtag = "act" if ly == 0 else "pl"
            vsem = "vla" if ly == 0 else "vlb"
            pend_o = []
            qk_mult = CH if HPC * 128 == 512 else HPC * 128 * T

            def head_load(hp, guard_pe):
                hpar = hp % 2
                hls = "hl" if hpar == 0 else "hl1"
                vls = vsem if hpar == 0 else vsem + "1"
                E.wait("sync", "pe", guard_pe)
                E.wait(veng, "pe", guard_pe)
                def fld(h, hp=hp, hpar=hpar, hls=hls):
                    ro = RT["roff"](h, "sp", qk_mult, hp * 128 * T + NCK * CH)
                    h.dma_start(out=Q_views[hpar][:, :].rearrange("p (r t) -> p r t", r=4),
                                in_=bass.AP(qkv_gath, ro,
                                            [[T, 128], [512 * T, 4], [1, T]])).then_inc(sems[hls], 16)
                    ro = RT["roff"](h, "sp", qk_mult, hp * 128 * T)
                    h.dma_start(out=K_views[hpar][:, :].rearrange("p (r t) -> p r t", r=4),
                                in_=bass.AP(qkv_gath, ro,
                                            [[T, 128], [512 * T, 4], [1, T]])).then_inc(sems[hls], 16)
                E.emit("sync", fld)
                qk_ev = E.bump(hls, 16 * 2)
                return (hls, qk_ev)

            done_pe = {}
            loads = {0: head_load(0, ent["pe"])}
            for hp in range(HPC):
                if pend_o:
                    fcc_fn, o0, o1 = pend_o.pop(0)
                    E.wait("gpsimd", "os0", o0)
                    E.wait("gpsimd", "os1", o1)
                    E.emit("gpsimd", fcc_fn); E.bump("cc")
                if hp + 1 < HPC:
                    loads[hp + 1] = head_load(hp + 1, done_pe.get(hp - 1, ent["pe"]))
                E.wait(veng, "pe", done_pe.get(hp - 1, ent["pe"]))
                def fldv(h, hp=hp):
                    for r in range(4):
                        ro = RT["roff"](h, vtag, HPC * 128, hp * 128 + VBASE + r * D * T)
                        h.dma_start(out=V_view[:, r * T:(r + 1) * T].rearrange(
                                        "p (b f) -> p b f", b=T // 128),
                                    in_=bass.AP(qkv_gath, ro,
                                                [[D, 128], [128 * D, T // 128], [1, 128]])
                                    ).then_inc(sems[vsem], 16)
                E.emit(veng, fldv)
                v_ev = E.bump(vsem, 16 * 4)
                hls, qk_ev = loads[hp]
                E.wait("tensor", hls, qk_ev)
                E.wait("tensor", vsem, v_ev)
                hpar = hp % 2
                Q_view, K_view = Q_views[hpar], K_views[hpar]
                for qt in range(QT):
                    nkb = 4 * (qt + 1)
                    bq = (hp * QT + qt) % 2
                    exp_evt, zz_evt, mm_evt = {}, {}, {}
                    bias_act = [0]
                    av_last = 0
                    for g in range(nkb + 6):
                        e = g - 6
                        if 0 <= e < nkb:
                            rs = ring_ctr[0] % 4
                            ring_ctr[0] += 1
                            diag = e >= 4 * qt
                            E.wait("scalar", "pe", mm_evt[e])
                            E.wait("scalar", "act", bias_act[0])
                            if len(av_ring) >= 4:
                                E.wait("scalar", "pe", av_ring[-4])
                            def fex(h, e=e, rs=rs, bq=bq):
                                h.activation(out=pring[:, rs * 512:(rs + 1) * 512],
                                             in_=SBANKS[e % 6], func=AF.Exp,
                                             bias=bias_sb[:, bq:bq + 1],
                                             scale=C_SM).then_inc(sems["act"], 1)
                            E.emit("scalar", fex)
                            ev = E.bump("act")
                            exp_evt[e] = (ev, rs)
                            last_bank_exp[e % 6] = ev
                            if diag:
                                pat = e - 4 * qt
                                E.wait("vector", "act", ev)
                                E.wait("vector", "msk", 16)
                                def fzz(h, rs=rs, pat=pat):
                                    h.tensor_mul(out=pring[:, rs * 512:(rs + 1) * 512],
                                                 in0=pring[:, rs * 512:(rs + 1) * 512],
                                                 in1=mask01_sb[:, pat * 512:(pat + 1) * 512]
                                                 ).then_inc(sems["dve"], 1)
                                E.emit("vector", fzz)
                                zz_evt[e] = E.bump("dve")
                        if g < nkb:
                            E.wait("tensor", "act", last_bank_exp[g % 6])
                            if first_scores[0]:
                                E.wait("tensor", "dve", xn_dve)
                                E.wait_snap("tensor", a4ent, ("dve", "act"))
                                first_scores[0] = False
                            def fsc(h, kb=g, qt=qt, si=g % 6, KV=K_view, QV=Q_view):
                                h.matmul(SBANKS[si], KV[:, kb * 128:(kb + 1) * 128],
                                         QV[:, qt * 512:(qt + 1) * 512],
                                         start=True, stop=True).then_inc(sems["pe"], 1)
                            E.emit("tensor", fsc)
                            mm_evt[g] = E.bump("pe")
                            if g == 1:
                                E.wait("vector", "pe", mm_evt[1])
                                def fpr(h, w=E.ctr["dve"]):
                                    h.wait_ge(sems["dve"], w)
                                    h.tensor_reduce(out=Mp_sb[:, 0:1], in_=ps[0][:, 0:1024],
                                                    axis=AX.X, op=ALU.max).then_inc(sems["dve"], 1)
                                E.emit("vector", fpr); pr_evt = E.bump("dve")
                            if g == 2:
                                E.wait("tensor", "dve", pr_evt)
                                if fr3_guard[0]:
                                    E.wait("tensor", "dve", fr3_guard[0])
                                E.wait("tensor", "ld", ent["ld"])
                                def ftr(h):
                                    h.transpose(ps[3][0:1, 512:640], Mp_sb[:, 0:1],
                                                ident_sb[:, :]).then_inc(sems["pe"], 1)
                                E.emit("tensor", ftr); tr_evt = E.bump("pe")
                                E.wait("vector", "pe", tr_evt)
                                def fmu(h):
                                    h.tensor_reduce(out=mu_sb[0:1, 0:1],
                                                    in_=ps[3][0:1, 512:640], axis=AX.X,
                                                    op=ALU.max).then_inc(sems["dve"], 1)
                                E.emit("vector", fmu); E.bump("dve")
                                E.wait("tensor", "dve", E.ctr["dve"])
                                def fb4(h):
                                    h.matmul(ps[3][:, 640:641], ones_r_f[:, :],
                                             mu_sb[0:1, 0:1], start=True,
                                             stop=True).then_inc(sems["pe"], 1)
                                E.emit("tensor", fb4); b4 = E.bump("pe")
                                E.wait("scalar", "pe", b4)
                                def fbi(h, bq=bq):
                                    h.activation(out=bias_sb[:, bq:bq + 1],
                                                 in_=ps[3][:, 640:641], func=AF.Copy,
                                                 scale=-C_SM).then_inc(sems["act"], 1)
                                E.emit("scalar", fbi); bias_act[0] = E.bump("act")
                        if 0 <= e < nkb:
                            ev, rs = exp_evt[e]
                            E.wait("tensor", "act", ev)
                            if e in zz_evt:
                                E.wait("tensor", "dve", zz_evt[e])
                            if e == 0 and fr3_guard[0]:
                                E.wait("tensor", "dve", fr3_guard[0])
                            def fav(h, kb=e, rs=rs, first=(e == 0), last=(e == nkb - 1),
                                    VV=V_view):
                                h.matmul(ps[3][:, 0:512], VV[:, kb * 128:(kb + 1) * 128],
                                         pring[:, rs * 512:(rs + 1) * 512],
                                         start=first, stop=last)
                                h.matmul(ps[3][0:1, 512:1024], ones_c_bf[:, :],
                                         pring[:, rs * 512:(rs + 1) * 512],
                                         start=first, stop=last).then_inc(sems["pe"], 1)
                            E.emit("tensor", fav)
                            av_ring.append(E.bump("pe"))
                            av_last = av_ring[-1]
                    E.wait("vector", "pe", av_last)
                    def fr1(h, w=E.ctr["dve"]):
                        h.wait_ge(sems["dve"], w)
                        h.reciprocal(out=rec_sb[:, :],
                                     in_=ps[3][0:1, 512:1024]).then_inc(sems["dve"], 1)
                    E.emit("vector", fr1); E.bump("dve")
                    E.wait("tensor", "dve", E.ctr["dve"])
                    def fr2(h):
                        h.matmul(ps[2][:, 0:512], ones_r_f[:, :], rec_sb[:, :],
                                 start=True, stop=True).then_inc(sems["pe"], 1)
                    E.emit("tensor", fr2); r2 = E.bump("pe")
                    E.wait("scalar", "pe", r2)
                    E.wait("scalar", "dve", fr3_guard[0])
                    def frc(h):
                        h.copy(out=recb_sb[:, :], in_=ps[2][:, 0:512]).then_inc(sems["act"], 1)
                    E.emit("scalar", frc)
                    last_bank_exp[4] = E.bump("act")
                    ix = hp * QT + qt
                    opar2 = ix % 2
                    E.wait("vector", "act", last_bank_exp[4])
                    E.wait("vector", f"os{opar2}", E.ctr[f"os{opar2}"])
                    E.wait("vector", "vs0", E.ctr["vs0"])
                    E.wait("vector", "vs1", E.ctr["vs1"])
                    def fr3(h, opar2=opar2):
                        h.tensor_mul(out=ost[opar2][:, :], in0=ps[3][:, 0:512],
                                     in1=recb_sb[:, :]).then_inc(sems["dve"], 1)
                    E.emit("vector", fr3); E.bump("dve")
                    fr3_guard[0] = E.ctr["dve"]
                    E.wait("sync", "dve", E.ctr["dve"])
                    def fr4(h, hp=hp, qt=qt, opar2=opar2):
                        h.dma_start(out=bass.AP(o_local, hp * 128 * S + qt * 512, [[S, 128], [1, 512]]),
                                    in_=ost[opar2][:, :]).then_inc(sems[f"os{opar2}"], 16)
                    E.emit("sync", fr4); E.bump(f"os{opar2}", 16)
                done_pe[hp] = E.ctr["pe"]
                E.wait("gpsimd", "ota", ent["ota"])
                E.wait("gpsimd", "otb", ent["otb"])
                def fcc2(h, hp=hp):
                    h.collective_compute("AllGather", ALU.bypass,
                                         replica_groups=[[0, 1, 2, 3], [4, 5, 6, 7]],
                                         ins=[o_local[hp * 128:(hp + 1) * 128, :].opt()],
                                         outs=[o_gath[hp * 4 * 128:(hp + 1) * 4 * 128, :].opt()]
                                         ).then_inc(sems["cc"])
                pend_o.append((fcc2, E.ctr["os0"], E.ctr["os1"]))
            for fcc_fn, o0, o1 in pend_o:
                E.wait("gpsimd", "os0", o0)
                E.wait("gpsimd", "os1", o1)
                E.emit("gpsimd", fcc_fn); E.bump("cc")
            cc_o = E.ctr["cc"]
            osem = "ota" if ly == 0 else "otb"
            E.wait(veng, "cc", cc_o)
            E.wait(veng, "pe", E.ctr["pe"])
            def fot(h):
                for r in range(4):
                    ro = RT["roff"](h, vtag, T, r * 128 * S)
                    h.dma_start(out=xn_sb[:, (r * HPC) * T:(r * HPC + HPC) * T].rearrange(
                                    "p (hh n) -> p hh n", hh=HPC),
                                in_=bass.AP(o_gath, ro,
                                            [[S, 128], [4 * 128 * S, HPC], [1, T]])).then_inc(sems[osem], 16)
            E.emit(veng, fot)
            ot_ld = E.bump(osem, 16 * 4)
            E.wait("vector", osem, ot_ld)
            E.wait("vector", "ld", ent["ld"])
            for hh in range(NH):
                def fbv(h, hh=hh, ly=ly):
                    h.tensor_scalar_add(out=xn_sb[:, hh * T:(hh + 1) * T],
                                        in0=xn_sb[:, hh * T:(hh + 1) * T],
                                        scalar1=bvh_sb[ly][:, hh:hh + 1]).then_inc(sems["dve"], 1)
                E.emit("vector", fbv); E.bump("dve")
            ot_dve = E.ctr["dve"]
            op_pe, op_dve = {}, {}
            for db in range(DT):
                par = db % 2
                wsem = f"wc{par}"
                if db >= 2:
                    E.wait("gpsimd", "pe", op_pe[db - 2])
                else:
                    E.wait("gpsimd", "pe", pe_evt[2 * DT - 2 + db])
                wev = wcast(wbuf[par], wo_h[ly], db * SLAB, DT, 128, wsem)
                E.wait("tensor", wsem, wev)
                E.wait("tensor", osem, ot_ld)
                E.wait("tensor", "dve", ot_dve)
                if db >= 2:
                    E.wait("tensor", "dve", op_dve[db - 2])
                def fmm(h, par=par):
                    for ht in range(NH):
                        for hf in range(TH):
                            ins = h.matmul(ps[par][:, hf * 512:hf * 512 + 512],
                                           wbuf[par][:, ht * 128:(ht + 1) * 128],
                                           xn_sb[:, ht * T + hf * 512: ht * T + hf * 512 + 512],
                                           start=(ht == 0), stop=(ht == NH - 1))
                    ins.then_inc(sems["pe"], 1)
                E.emit("tensor", fmm); op_pe[db] = E.bump("pe")
                E.wait("vector", "pe", op_pe[db])
                def fres(h, db=db, par=par):
                    h.scalar_tensor_tensor(out=x_sb[:, db * T:(db + 1) * T], in0=ps[par][:, 0:T],
                                           scalar=bo_sb[ly][:, db:db + 1],
                                           in1=x_sb[:, db * T:(db + 1) * T],
                                           op0=ALU.add, op1=ALU.add).then_inc(sems["dve"], 1)
                E.emit("vector", fres); op_dve[db] = E.bump("dve")
            return op_dve

        def p_ffn(second, xn_dve):
            ent = E.snap()
            g_pe, g_act, g_dve = {}, {}, {}
            d_pe_g, d_dve_g = {}, {}
            down_guard = 0
            for hq in range(8):
                for fb in range(FQ):
                    gb = hq * FQ + fb
                    par = fb % 2
                    weng = "gpsimd" if not second else "sync"
                    if gb >= 2:
                        E.wait(weng, "pe", g_pe[(gb - 2, "u")])
                    else:
                        E.wait_snap(weng, ent, ("pe", "fwb0", "fwb1"))
                    if not second:
                        E.wait("gpsimd", f"fwb{par}", E.ctr[f"fwb{par}"])
                        cg = wcast(wbuf[par], wg_h, gb * SLAB, DT, 128, f"wc{par}")
                        cu = wcast(uwbuf[par], wu_h, gb * SLAB, DT, 128, f"uc{par}")
                        E.wait("sync", f"wc{par}", cg)
                        E.wait("sync", f"uc{par}", cu)
                        def fwb(h, gb=gb, par=par):
                            h.dma_start(out=bass.AP(wg_bf, gb * SLAB, [[SLAB // 128, 128], [1, SLAB // 128]]),
                                        in_=wbuf[par][:, :]).then_inc(sems[f"fwb{par}"], 16)
                            h.dma_start(out=bass.AP(wu_bf, gb * SLAB, [[SLAB // 128, 128], [1, SLAB // 128]]),
                                        in_=uwbuf[par][:, :]).then_inc(sems[f"fwb{par}"], 16)
                        E.emit("sync", fwb); E.bump(f"fwb{par}", 32)
                        E.wait("tensor", f"wc{par}", cg)
                        E.wait("tensor", f"uc{par}", cu)
                    else:
                        def fw(h, gb=gb, par=par):
                            h.dma_start(out=wbuf[par][:, :].rearrange("p (t f) -> p t f", t=DT),
                                        in_=bass.AP(wg_bf, gb * SLAB,
                                                    [[SLAB // 128, 128], [128, DT], [1, 128]])
                                        ).then_inc(sems[f"wl{par}"], 16)
                            h.dma_start(out=uwbuf[par][:, :].rearrange("p (t f) -> p t f", t=DT),
                                        in_=bass.AP(wu_bf, gb * SLAB,
                                                    [[SLAB // 128, 128], [128, DT], [1, 128]])
                                        ).then_inc(sems[f"wl{par}"], 16)
                        E.emit("sync", fw)
                        wlev = E.bump(f"wl{par}", 32)
                        E.wait("tensor", f"wl{par}", wlev)
                    E.wait("tensor", "dve", xn_dve)
                    E.wait("tensor", "dve", down_guard)
                    if gb >= 2:
                        E.wait("tensor", "act", g_act[gb - 2])
                        E.wait("tensor", "dve", g_dve[gb - 2])
                    def fg(h, par=par):
                        for dt in range(DT):
                            for hf in range(TH):
                                ins = h.matmul(ps[par][:, hf * 512:hf * 512 + 512],
                                               wbuf[par][:, dt * 128:(dt + 1) * 128],
                                               xn_sb[:, dt * T + hf * 512: dt * T + hf * 512 + 512],
                                               start=(dt == 0), stop=(dt == DT - 1))
                        ins.then_inc(sems["pe"], 1)
                    E.emit("tensor", fg); g_pe[(gb, "g")] = E.bump("pe")
                    def fu(h, par=par):
                        for dt in range(DT):
                            for hf in range(TH):
                                ins = h.matmul(ps[2 + par][:, hf * 512:hf * 512 + 512],
                                               uwbuf[par][:, dt * 128:(dt + 1) * 128],
                                               xn_sb[:, dt * T + hf * 512: dt * T + hf * 512 + 512],
                                               start=(dt == 0), stop=(dt == DT - 1))
                        ins.then_inc(sems["pe"], 1)
                    E.emit("tensor", fu); g_pe[(gb, "u")] = E.bump("pe")
                    E.wait("scalar", "pe", g_pe[(gb, "g")])
                    if gb >= 2:
                        E.wait("scalar", "dve", g_dve[gb - 2])
                    else:
                        E.wait_snap("scalar", ent, ("dve", "qs0", "qs1"))
                    def fsi(h, par=par):
                        h.activation(out=silustage[par][:, 0:T], in_=ps[par][:, 0:T],
                                     func=AF.Silu).then_inc(sems["act"], 1)
                    E.emit("scalar", fsi); g_act[gb] = E.bump("act")
                    E.wait("vector", "pe", g_pe[(gb, "u")])
                    E.wait("vector", "act", g_act[gb])
                    def fgm(h, fb=fb, par=par):
                        h.tensor_mul(out=gated_view[:, fb * T:(fb + 1) * T], in0=ps[2 + par][:, 0:T],
                                     in1=silustage[par][:, 0:T]).then_inc(sems["dve"], 1)
                    E.emit("vector", fgm); g_dve[gb] = E.bump("dve")
                gate_dve = E.ctr["dve"]
                for db in range(DT):
                    par = db % 2
                    sl = hq * DT + db
                    weng = "gpsimd" if not second else "sync"
                    if sl >= 2:
                        E.wait(weng, "pe", d_pe_g[sl - 2])
                    else:
                        E.wait_snap(weng, ent, ("pe", "dwb0", "dwb1"))
                    if not second:
                        E.wait("gpsimd", f"dwb{par}", E.ctr[f"dwb{par}"])
                        cd = wcast(dwbuf[par], wd_h, sl * DSLAB, FQ, 128, f"dc{par}")
                        E.wait("sync", f"dc{par}", cd)
                        def fwb(h, sl=sl, par=par):
                            h.dma_start(out=bass.AP(wd_bf, sl * DSLAB,
                                                    [[DSLAB // 128, 128], [1, DSLAB // 128]]),
                                        in_=dwbuf[par][:, :]).then_inc(sems[f"dwb{par}"], 16)
                        E.emit("sync", fwb); E.bump(f"dwb{par}", 16)
                        E.wait("tensor", f"dc{par}", cd)
                    else:
                        def fw(h, sl=sl, par=par):
                            h.dma_start(out=dwbuf[par][:, :].rearrange("p (t f) -> p t f", t=FQ),
                                        in_=bass.AP(wd_bf, sl * DSLAB,
                                                    [[DSLAB // 128, 128], [128, FQ], [1, 128]])
                                        ).then_inc(sems[f"dl{par}"], 16)
                        E.emit("sync", fw)
                        dlev = E.bump(f"dl{par}", 16)
                        E.wait("tensor", f"dl{par}", dlev)
                    E.wait("tensor", "dve", gate_dve)
                    if sl >= 2:
                        E.wait("tensor", "dve", d_dve_g.get(sl - 2, 0))
                    def fdm(h, par=par):
                        for hb in range(FQ):
                            for hf in range(TH):
                                ins = h.matmul(ps[par][:, hf * 512:hf * 512 + 512],
                                               dwbuf[par][:, hb * 128:(hb + 1) * 128],
                                               gated_view[:, hb * T + hf * 512: hb * T + hf * 512 + 512],
                                               start=(hb == 0), stop=(hb == FQ - 1))
                        ins.then_inc(sems["pe"], 1)
                    E.emit("tensor", fdm); d_pe_g[sl] = E.bump("pe")
                    E.wait("vector", "pe", d_pe_g[sl])
                    def fres(h, db=db, par=par):
                        h.scalar_tensor_tensor(out=x_sb[:, db * T:(db + 1) * T], in0=ps[par][:, 0:T],
                                               scalar=0.0, in1=x_sb[:, db * T:(db + 1) * T],
                                               op0=ALU.add, op1=ALU.add).then_inc(sems["dve"], 1)
                    E.emit("vector", fres); d_dve_g[sl] = E.bump("dve")
                down_guard = E.ctr["dve"]
            return {db: d_dve_g[7 * DT + db] for db in range(DT)}

        fold_guard = [0, 0]
        last_ln_pe = [0]

        p_init()
        xn1 = p_ln()
        prevA = p_attn(0, xn1)
        xn2 = p_ln(prevA)
        prevF = p_ffn(False, xn2)
        xn3 = p_ln(prevF)
        prevB = p_attn(1, xn3)
        xn4 = p_ln(prevB)
        p_ffn(True, xn4)

        E.wait("sync", "dve", E.ctr["dve"])
        def ffin(h):
            h.dma_start(out=yT_h.ap().rearrange("(t p) n -> p t n", p=128),
                        in_=x_sb[:, :].rearrange("p (t n) -> p t n", t=DT)).then_inc(sems["out"], 16)
        E.emit("sync", ffin)
        E.bump("out", 16)
        E.wait("sync", "out", 16)

        with nc.Block() as block:
            @block.sync
            def _(h):
                for f in E.q["sync"]:
                    f(h)

            @block.tensor
            def _(h):
                for f in E.q["tensor"]:
                    f(h)

            @block.scalar
            def _(h):
                for f in E.q["scalar"]:
                    f(h)

            @block.vector
            def _(h):
                for f in E.q["vector"]:
                    f(h)

            @block.gpsimd
            def _(h):
                for f in E.q["gpsimd"]:
                    f(h)

    return nc



def make_in_maps(cfg, inputs):
    shared = host_inputs(cfg, inputs)
    x = np.asarray(inputs["L0_attn_a_x"], dtype=np.float32)
    in_maps = []
    for c in range(8):
        g, r = c // 4, c % 4
        xs = np.ascontiguousarray(x[g, r * cfg.T:(r + 1) * cfg.T, :].T)
        m = {"xT": xs}
        m.update(shared)
        in_maps.append(m)
    return in_maps


def assemble_out(cfg, results):
    out = np.empty((cfg.B, cfg.S, cfg.D), dtype=np.float32)
    for c in range(8):
        g, r = c // 4, c % 4
        out[g, r * cfg.T:(r + 1) * cfg.T, :] = results[c]["yT"].T
    return out


def kernel(**inputs):
    cfg = CFG
    nc = build_program(cfg)
    in_maps = make_in_maps(cfg, inputs)
    res = run_bass_kernel_spmd(nc, in_maps, core_ids=list(range(8)))
    return assemble_out(cfg, res.results)


# version 16
# speedup vs baseline: 1.0154x; 1.0154x over previous
import numpy as np
from contextlib import ExitStack

import concourse.bass as bass
import concourse.mybir as mybir
from concourse.bass_utils import run_bass_kernel_spmd
from concourse.aot_env import allow_ici_all_to_all

F32 = mybir.dt.float32
BF16 = mybir.dt.bfloat16
AF = mybir.ActivationFunctionType
ALU = mybir.AluOpType
AX = mybir.AxisListType

NEG = -1e30
C_SM = float(1.0 / np.sqrt(128.0))


class Cfg:
    def __init__(self, D=2048, FF=8192, NH=16, S=4096, B=2):
        self.D, self.FF, self.NH, self.S, self.B = D, FF, NH, S, B
        self.DT = D // 128
        self.T = S // 4
        self.HPC = NH // 4
        self.QT = S // 512
        self.KB = S // 128
        self.FQ = FF // 8 // 128
        self.TH = self.T // 512
        self.VW = min(256, (self.NH // 4) * 128)


CFG = Cfg()



def _pack_lhsT(w_lhsT, nblk):
    K, M = w_lhsT.shape
    dt = K // 128
    a = w_lhsT.reshape(dt, 128, nblk, 128).transpose(2, 1, 0, 3)
    return np.ascontiguousarray(a.reshape(nblk, dt * 128 * 128))


def _pack_rhs(w, nchunk, width):
    K, M = w.shape
    dt = K // 128
    a = w.reshape(dt, 128, nchunk, width).transpose(2, 1, 0, 3)
    return np.ascontiguousarray(a.reshape(nchunk, dt * 128 * width))


def _pack_down(wd_T, ff, d):
    fq = ff // 8 // 128
    dbn = d // 128
    a = wd_T.reshape(8, fq, 128, dbn, 128).transpose(0, 3, 2, 1, 4)
    return np.ascontiguousarray(a.reshape(8 * dbn, 128 * fq * 128))


def _make_masks01():
    m = np.zeros((128, 4, 512), dtype=np.float32)
    p = np.arange(128)[:, None]
    qf = np.arange(512)[None, :]
    for s in range(4):
        m[:, s, :] = np.where(s * 128 + p <= qf, 1.0, 0.0)
    return np.ascontiguousarray(m.reshape(128, 4 * 512))


def host_inputs(cfg, inputs):
    f = np.float32
    d = {}
    pairs = (("A", np.ascontiguousarray(np.asarray(inputs["L0_attn_a_W_qkv"], f).T),
              np.asarray(inputs["L0_attn_a_b_qkv"], f)),
             ("B", np.ascontiguousarray(np.asarray(inputs["L0_attn_b_w_qkv"], f)),
              np.asarray(inputs["L0_attn_b_b_qkv"], f)))
    for tag, w, b in pairs:
        d[f"wqk{tag}"] = _pack_lhsT(w[:, : 2 * cfg.D], 2 * cfg.DT)
        d[f"wv{tag}"] = _pack_rhs(w[:, 2 * cfg.D:], cfg.D // cfg.VW, cfg.VW)
        d[f"bqk{tag}"] = np.ascontiguousarray(b[: 2 * cfg.D].reshape(2 * cfg.DT, 128).T)
        d[f"bv{tag}"] = np.ascontiguousarray(b[2 * cfg.D:].reshape(cfg.NH, 128).T)
    d["woA"] = _pack_lhsT(np.ascontiguousarray(np.asarray(inputs["L0_attn_a_W_out"], f).T), cfg.DT)
    d["woB"] = _pack_lhsT(np.ascontiguousarray(np.asarray(inputs["L0_attn_b_w_proj"], f)), cfg.DT)
    d["boA"] = np.ascontiguousarray(np.asarray(inputs["L0_attn_a_b_out"], f).reshape(cfg.DT, 128).T)
    d["boB"] = np.ascontiguousarray(np.asarray(inputs["L0_attn_b_b_proj"], f).reshape(cfg.DT, 128).T)
    d["wg"] = _pack_lhsT(np.ascontiguousarray(np.asarray(inputs["L0_ffn_w_gate"], f).T), cfg.FF // 128)
    d["wu"] = _pack_lhsT(np.ascontiguousarray(np.asarray(inputs["L0_ffn_w_up"], f).T), cfg.FF // 128)
    d["wd"] = _pack_down(np.ascontiguousarray(np.asarray(inputs["L0_ffn_w_down"], f).T), cfg.FF, cfg.D)
    d["masks"] = _make_masks01()
    d["identity"] = np.eye(128, dtype=f)
    return d



SEM_NAMES = ("pe", "act", "dve", "gp", "cc", "ld", "msk",
             "qs0", "qs1", "vs0", "vs1", "os0", "os1",
             "wc0", "wc1", "uc0", "uc1", "dc0", "dc1", "vc", "vc1",
             "wl0", "wl1", "dl0", "dl1",
             "fwb0", "fwb1", "dwb0", "dwb1",
             "hl", "hl1", "vla", "vla1", "vlb", "vlb1", "ota", "otb", "out")


class Emitter:
    ENGS = ("sync", "tensor", "scalar", "vector", "gpsimd")

    def __init__(self):
        self.q = {e: [] for e in self.ENGS}
        self.ctr = {k: 0 for k in SEM_NAMES}
        self.lastwait = {e: {} for e in self.ENGS}
        self.sems = {}

    def emit(self, eng, fn):
        self.q[eng].append(fn)

    def wait(self, eng, sem, val):
        if val is None or val <= 0:
            return
        lw = self.lastwait[eng]
        if lw.get(sem, 0) >= val:
            return
        lw[sem] = val
        s = self.sems[sem]

        def f(h, s=s, v=val):
            h.wait_ge(s, v)
        self.q[eng].append(f)

    def snap(self):
        return dict(self.ctr)

    def wait_snap(self, eng, snap, names):
        for n in names:
            self.wait(eng, n, snap[n])

    def bump(self, name, by=1):
        self.ctr[name] += by
        return self.ctr[name]


def build_program(cfg):
    nc = bass.Bass("TRN2", target_bir_lowering=False, debug=False)
    E = Emitter()
    DT, T, QT, KB, HPC, FQ, TH = cfg.DT, cfg.T, cfg.QT, cfg.KB, cfg.HPC, cfg.FQ, cfg.TH
    VW = cfg.VW
    D, FF, NH, S = cfg.D, cfg.FF, cfg.NH, cfg.S
    SLAB = DT * 128 * 128
    VSLAB = DT * 128 * VW
    DSLAB = 128 * FQ * 128

    xT_h = nc.dram_tensor("xT", [D, T], F32, kind="ExternalInput")
    yT_h = nc.dram_tensor("yT", [D, T], F32, kind="ExternalOutput")
    wqk_h = {0: nc.dram_tensor("wqkA", [2 * DT, SLAB], F32, kind="ExternalInput"),
             1: nc.dram_tensor("wqkB", [2 * DT, SLAB], F32, kind="ExternalInput")}
    wv_h = {0: nc.dram_tensor("wvA", [D // VW, VSLAB], F32, kind="ExternalInput"),
            1: nc.dram_tensor("wvB", [D // VW, VSLAB], F32, kind="ExternalInput")}
    bqk_h = {0: nc.dram_tensor("bqkA", [128, 2 * DT], F32, kind="ExternalInput"),
             1: nc.dram_tensor("bqkB", [128, 2 * DT], F32, kind="ExternalInput")}
    bv_h = {0: nc.dram_tensor("bvA", [128, NH], F32, kind="ExternalInput"),
            1: nc.dram_tensor("bvB", [128, NH], F32, kind="ExternalInput")}
    wo_h = {0: nc.dram_tensor("woA", [DT, SLAB], F32, kind="ExternalInput"),
            1: nc.dram_tensor("woB", [DT, SLAB], F32, kind="ExternalInput")}
    bo_h = {0: nc.dram_tensor("boA", [128, DT], F32, kind="ExternalInput"),
            1: nc.dram_tensor("boB", [128, DT], F32, kind="ExternalInput")}
    wg_h = nc.dram_tensor("wg", [FF // 128, SLAB], F32, kind="ExternalInput")
    wu_h = nc.dram_tensor("wu", [FF // 128, SLAB], F32, kind="ExternalInput")
    wd_h = nc.dram_tensor("wd", [8 * DT, DSLAB], F32, kind="ExternalInput")
    masks_h = nc.dram_tensor("masks", [128, 4 * 512], F32, kind="ExternalInput")
    ident_h = nc.dram_tensor("identity", [128, 128], F32, kind="ExternalInput")

    NCK = D // 512
    CH = 4 * 512 * T
    VBASE = 2 * NCK * CH
    qkv_local = nc.dram_tensor("qkv_local", [3 * D, T], BF16)
    qkv_gath = nc.dram_tensor("qkv_gath", [4 * 3 * D, T], BF16)
    o_local = nc.dram_tensor("o_local", [HPC * 128, S], BF16)
    o_gath = nc.dram_tensor("o_gath", [4 * HPC * 128, S], BF16)
    wg_bf = nc.dram_tensor("wg_bf", [FF // 128, SLAB], BF16)
    wu_bf = nc.dram_tensor("wu_bf", [FF // 128, SLAB], BF16)
    wd_bf = nc.dram_tensor("wd_bf", [8 * DT, DSLAB], BF16)

    es = ExitStack()
    with es:
        sbt = lambda name, shape, dt: es.enter_context(nc.sbuf_tensor(name, shape, dt))
        x_sb = sbt("x_sb", [128, DT * T], F32)
        xn_sb = sbt("xn_sb", [128, max(DT, NH) * T], BF16)
        sst_sb = sbt("sst_sb", [128, 6 * 512], F32)
        arena = sbt("arena", [128, max(5 * KB * 128, FQ * T)], BF16)
        WB = DT * 128
        wt = sbt("wt", [128, max(4 * WB + 2 * FQ * 128, 2 * WB + DT * VW) + DT * VW], BF16)
        stage = sbt("stage", [128, 2 * 1024], BF16)
        vstage = sbt("vstage", [128, 2 * 512], BF16)
        mask01_sb = sbt("mask01_sb", [128, 4 * 512], BF16)
        ident_sb = sbt("ident_sb", [128, 128], F32)
        bqk_sb = {ly: sbt(f"bqk{ly}_sb", [128, 2 * DT], F32) for ly in (0, 1)}
        bo_sb = {ly: sbt(f"bo{ly}_sb", [128, DT], F32) for ly in (0, 1)}
        ones_c_bf = sbt("ones_c_bf", [128, 1], BF16)
        ones_r_f = sbt("ones_r_f", [1, 128], F32)
        onescf = sbt("onescf", [128, 1], F32)
        Mc_sb = sbt("Mc_sb", [128, 1], F32)
        Mp_sb = sbt("Mp_sb", [128, 8], F32)
        mu_sb = sbt("mu_sb", [1, 3], F32)
        nmu_sb = sbt("nmu_sb", [1, 1], F32)
        f_sb = sbt("f_sb", [1, 2], F32)
        fbc_sb = sbt("fbc_sb", [128, 2], F32)
        bias_sb = sbt("bias_sb", [128, 2], F32)
        packA = sbt("packA", [128, 2 * T], F32)
        packD = sbt("packD", [128, 1536], F32)
        bvh_sb = {ly: sbt(f"bvh{ly}_sb", [128, NH], F32) for ly in (0, 1)}
        mrow_sb = packA[0:1, 0:T]
        rstd_sb = packA[0:1, T:2 * T]
        ex2_sb = packA[32:33, 0:T]
        msq_sb = packA[32:33, T:2 * T]
        var_sb = ex2_sb
        sd_sb = msq_sb
        rec_sb = packD[0:1, 0:512]
        recb_sb = packD[:, 512:1024]

        ps = [es.enter_context(nc.psum_tensor(f"ps{i}", [128, 1024], F32)) for i in range(4)]

        for name in SEM_NAMES:
            E.sems[name] = es.enter_context(nc.semaphore("sem_" + name))
        sems = E.sems

        QKV_EL = KB * 128
        Q_views = [arena[:, 0:QKV_EL], arena[:, QKV_EL:2 * QKV_EL]]
        K_views = [arena[:, 2 * QKV_EL:3 * QKV_EL], arena[:, 3 * QKV_EL:4 * QKV_EL]]
        V_view = arena[:, 4 * QKV_EL:5 * QKV_EL]
        gated_view = arena[:, 0:FQ * T]
        wbuf = [wt[:, 0:WB], wt[:, WB:2 * WB]]
        WTB = max(4 * WB + 2 * FQ * 128, 2 * WB + DT * VW)
        vwbufs = [wt[:, 2 * WB:2 * WB + DT * VW], wt[:, WTB:WTB + DT * VW]]
        uwbuf = [wt[:, 2 * WB:3 * WB], wt[:, 3 * WB:4 * WB]]
        dwbuf = [wt[:, 4 * WB:4 * WB + FQ * 128], wt[:, 4 * WB + FQ * 128:4 * WB + 2 * FQ * 128]]
        qstage = [stage[:, 0:1024], stage[:, 1024:2048]]
        silustage = qstage
        lnb = [sst_sb[:, 1024:1536].bitcast(BF16), sst_sb[:, 1536:2048].bitcast(BF16)]
        lnsq = [sst_sb[:, 2048:2560].bitcast(BF16), sst_sb[:, 2560:3072].bitcast(BF16)]
        vst = [vstage[:, 0:VW], vstage[:, VW:2 * VW]]
        ost = [vstage[:, 0:512], vstage[:, 512:1024]]
        pring = stage[:, 0:2048]

        RT = {}

        def p_init():
            def mkregs(h, tag):
                g1 = h.register(f"r_rk_{tag}")
                RT[f"_g1{tag}"] = g1
                rk = g1.__enter__()
                g2 = h.register(f"r_off_{tag}")
                RT[f"_g2{tag}"] = g2
                off = g2.__enter__()
                pid = h.partition_id()
                h.reg_mov(rk, pid % 4)
                RT[f"rk_{tag}"] = rk
                RT[f"off_{tag}"] = off

            def roff(h, tag, mult, add):
                h.reg_mov(RT[f"off_{tag}"], RT[f"rk_{tag}"])
                if mult != 1:
                    h.reg_alu(RT[f"off_{tag}"], RT[f"off_{tag}"], mult, ALU.mult)
                if add:
                    h.reg_alu(RT[f"off_{tag}"], RT[f"off_{tag}"], add, ALU.add)
                return RT[f"off_{tag}"]
            RT["roff"] = roff
            for eng, tag in (("sync", "sp"), ("scalar", "act"), ("gpsimd", "pl")):
                def fr(h, tag=tag):
                    mkregs(h, tag)
                E.emit(eng, fr)

            def fx(h):
                h.dma_start(out=x_sb[:, :].rearrange("p (t n) -> p t n", t=DT),
                            in_=xT_h.ap().rearrange("(t p) n -> p t n", p=128)).then_inc(sems["ld"], 16)
            E.emit("sync", fx); E.bump("ld", 16)
            for dst, src in [(ident_sb, ident_h), (bqk_sb[0], bqk_h[0]), (bqk_sb[1], bqk_h[1]),
                             (bo_sb[0], bo_h[0]), (bo_sb[1], bo_h[1]),
                             (bvh_sb[0], bv_h[0]), (bvh_sb[1], bv_h[1])]:
                def f2(h, dst=dst, src=src):
                    h.dma_start(out=dst[:, :], in_=src.ap()).then_inc(sems["ld"], 16)
                E.emit("sync", f2); E.bump("ld", 16)

            def fm(h):
                h.dma_start(out=mask01_sb[:, :], in_=masks_h.ap()).then_inc(sems["msk"], 16)
            E.emit("gpsimd", fm); E.bump("msk", 16)

            def fo(h):
                h.memset(ones_c_bf[:, :], 1.0).then_inc(sems["dve"], 1)
                h.memset(ones_r_f[:, :], 1.0).then_inc(sems["dve"], 1)
                h.memset(onescf[:, :], 1.0).then_inc(sems["dve"], 1)
            E.emit("vector", fo); E.bump("dve", 3)

        def p_ln(prev=None):
            ent = E.snap()
            pe_evt = {}
            for dt in range(DT):
                par = dt % 2
                if dt == 0:
                    E.wait_snap("scalar", ent, ("ld", "qs0", "qs1"))
                if dt < 2:
                    E.wait("scalar", "pe", last_ln_pe[0])
                if prev is not None:
                    E.wait("scalar", "dve", prev[dt])
                else:
                    E.wait_snap("scalar", ent, ("dve",))
                if dt >= 2:
                    E.wait("scalar", "pe", pe_evt[dt - 2])
                def fc(h, dt=dt, par=par):
                    h.copy(out=lnb[par][:, :], in_=x_sb[:, dt * T:(dt + 1) * T])
                    h.square(out=lnsq[par][:, :],
                             in_=x_sb[:, dt * T:(dt + 1) * T]).then_inc(sems["act"], 2)
                E.emit("scalar", fc); E.bump("act", 2)
                E.wait("tensor", "act", E.ctr["act"])
                if dt == 0:
                    E.wait("tensor", "act", ent["act"])
                    E.wait("tensor", "dve", prev[0] if prev is not None else ent["dve"])
                first, last = dt == 0, dt == DT - 1
                def fmm(h, par=par, first=first, last=last):
                    for hf in range(TH):
                        h.matmul(ps[2][0:1, hf * 512:hf * 512 + 512], ones_c_bf[:, :],
                                 lnb[par][:, hf * 512:hf * 512 + 512], start=first, stop=last)
                    for hf in range(TH):
                        ins = h.matmul(ps[3][0:1, hf * 512:hf * 512 + 512], ones_c_bf[:, :],
                                       lnsq[par][:, hf * 512:hf * 512 + 512], start=first, stop=last)
                    ins.then_inc(sems["pe"], 1)
                E.emit("tensor", fmm); pe_evt[dt] = E.bump("pe")
            last_ln_pe[0] = E.ctr["pe"]
            E.wait("vector", "pe", E.ctr["pe"])
            inv = 1.0 / float(D)
            def f1(h):
                for hf in range(TH):
                    h.tensor_scalar_mul(out=mrow_sb[0:1, hf * 512:hf * 512 + 512],
                                        in0=ps[2][0:1, hf * 512:hf * 512 + 512], scalar1=inv)
                for hf in range(TH):
                    ins = h.tensor_scalar_mul(out=ex2_sb[0:1, hf * 512:hf * 512 + 512],
                                              in0=ps[3][0:1, hf * 512:hf * 512 + 512], scalar1=inv)
                ins.then_inc(sems["dve"], 1)
            E.emit("vector", f1); E.bump("dve")
            E.wait("scalar", "dve", E.ctr["dve"])
            def f2(h):
                h.square(out=msq_sb[:, :], in_=mrow_sb[:, :]).then_inc(sems["act"], 1)
            E.emit("scalar", f2); E.bump("act")
            E.wait("vector", "act", E.ctr["act"])
            def f3(h):
                h.scalar_tensor_tensor(out=var_sb[:, :], in0=ex2_sb[:, :], scalar=1e-5,
                                       in1=msq_sb[:, :], op0=ALU.add,
                                       op1=ALU.subtract).then_inc(sems["dve"], 1)
            E.emit("vector", f3); E.bump("dve")
            E.wait("scalar", "dve", E.ctr["dve"])
            def f4(h):
                h.activation(out=sd_sb[:, :], in_=var_sb[:, :], func=AF.Sqrt).then_inc(sems["act"], 1)
            E.emit("scalar", f4); E.bump("act")
            E.wait("vector", "act", E.ctr["act"])
            def f5(h):
                h.reciprocal(out=rstd_sb[:, :], in_=sd_sb[:, :]).then_inc(sems["dve"], 1)
            E.emit("vector", f5); E.bump("dve")
            E.wait("tensor", "dve", E.ctr["dve"])
            def f7(h):
                for hf in range(TH):
                    h.matmul(ps[0][:, hf * 512:hf * 512 + 512], ones_r_f[:, :],
                             mrow_sb[0:1, hf * 512:hf * 512 + 512], start=True, stop=True)
                for hf in range(TH):
                    ins = h.matmul(ps[1][:, hf * 512:hf * 512 + 512], ones_r_f[:, :],
                                   rstd_sb[0:1, hf * 512:hf * 512 + 512], start=True, stop=True)
                ins.then_inc(sems["pe"], 1)
            E.emit("tensor", f7); E.bump("pe")
            E.wait("vector", "pe", E.ctr["pe"])
            dve_base = E.ctr["dve"]
            t1_sb = sst_sb[:, 0:T]
            for dt in range(DT):
                def fn(h, dt=dt, w=dve_base + 2 * dt, t1_sb=t1_sb):
                    h.wait_ge(sems["dve"], w)
                    h.tensor_sub(out=t1_sb[:, :], in0=x_sb[:, dt * T:(dt + 1) * T],
                                 in1=ps[0][:, 0:T]).then_inc(sems["dve"], 1)
                    h.wait_ge(sems["dve"], w + 1)
                    h.tensor_mul(out=xn_sb[:, dt * T:(dt + 1) * T], in0=t1_sb[:, :],
                                 in1=ps[1][:, 0:T]).then_inc(sems["dve"], 1)
                E.emit("vector", fn); E.bump("dve", 2)
            return E.ctr["dve"]

        def wcast(dst, src_h, off, tiles, width, sem):
            def f(h, dst=dst, src_h=src_h, off=off, tiles=tiles, width=width, sem=sem):
                h.dma_start(out=dst[:, :].rearrange("p (t f) -> p t f", t=tiles),
                            in_=bass.AP(src_h, off, [[tiles * width, 128], [width, tiles], [1, width]])
                            ).then_inc(sems[sem], 16)
            E.emit("gpsimd", f)
            return E.bump(sem, 16)

        def p_attn(ly, xn_dve):
            ent = E.snap()
            E.wait("sync", "cc", ent["cc"])
            cc_base = ent["cc"]
            v_pe, v_act = {}, {}
            E.wait("tensor", "ld", ent["ld"])
            for fc in range(D // VW):
                vpar = fc % 2
                vwbuf = vwbufs[vpar]
                vcs = "vc" if vpar == 0 else "vc1"
                if fc >= 2:
                    E.wait("gpsimd", "pe", v_pe[(fc - 2, T // 128 - 1)])
                else:
                    E.wait_snap("gpsimd", ent, ("pe", "fwb0", "fwb1", "dwb0", "dwb1",
                                                "wl0", "wl1", "dl0", "dl1"))
                wev = wcast(vwbuf, wv_h[ly], fc * VSLAB, DT, VW, vcs)
                E.wait("tensor", vcs, wev)
                for tb in range(T // 128):
                    par = tb % 2
                    psv = ps[2 + par]
                    if tb >= 2:
                        E.wait("tensor", "act", v_act[(fc, tb - 2)])
                    elif fc > 0:
                        E.wait("tensor", "act", v_act[(fc - 1, T // 128 - 2 + tb)])
                    else:
                        E.wait_snap("tensor", ent, ("dve", "act"))
                    def fmm(h, tb=tb, fc=fc, psv=psv, vwbuf=vwbuf):
                        for dt in range(DT):
                            ins = h.matmul(psv[:, 0:VW],
                                           xn_sb[:, dt * T + tb * 128: dt * T + (tb + 1) * 128],
                                           vwbuf[:, dt * VW:(dt + 1) * VW],
                                           start=(dt == 0), stop=(dt == DT - 1))
                        ins.then_inc(sems["pe"], 1)
                    E.emit("tensor", fmm); v_pe[(fc, tb)] = E.bump("pe")
                    E.wait("scalar", "pe", v_pe[(fc, tb)])
                    E.wait("scalar", f"vs{par}", E.ctr[f"vs{par}"])
                    E.wait("scalar", "os0", E.ctr["os0"])
                    def fev(h, par=par, psv=psv):
                        h.copy(out=vst[par][:, :], in_=psv[:, 0:VW]).then_inc(sems["act"], 1)
                    E.emit("scalar", fev); v_act[(fc, tb)] = E.bump("act")
                    E.wait("sync", "act", v_act[(fc, tb)])
                    def fst(h, fc=fc, tb=tb, par=par):
                        voff = 2 * D * T + (tb * 128) * D + fc * VW
                        h.dma_start(out=bass.AP(qkv_local, voff, [[D, 128], [1, VW]]),
                                    in_=vst[par][:, :]).then_inc(sems[f"vs{par}"], 16)
                    E.emit("sync", fst); E.bump(f"vs{par}", 16)
            vs_snap = (E.ctr["vs0"], E.ctr["vs1"])
            def fccv(h):
                h.collective_compute("AllGather", ALU.bypass,
                                     replica_groups=[[0, 1, 2, 3], [4, 5, 6, 7]],
                                     ins=[qkv_local[2 * D:3 * D, :].opt()],
                                     outs=[bass.AP(qkv_gath, VBASE, [[1, 4 * D * T]])]
                                     ).then_inc(sems["cc"])
            pe_evt, act_evt = {}, {}
            fb_order = list(range(DT, 2 * DT)) + list(range(DT))
            pending_cc = []
            for it, fb in enumerate(fb_order):
                par = it % 2
                wsem = f"wc{par}"
                if it >= 2:
                    E.wait("gpsimd", "pe", pe_evt[it - 2])
                else:
                    E.wait_snap("gpsimd", ent, ("pe", "fwb0", "fwb1", "wl0", "wl1"))
                if it == 2:
                    E.wait("gpsimd", "vs0", vs_snap[0])
                    E.wait("gpsimd", "vs1", vs_snap[1])
                    for s in ("hl", "vla", "vlb"):
                        E.wait("gpsimd", s, E.ctr[s])
                    E.emit("gpsimd", fccv); E.bump("cc")
                while pending_cc and pending_cc[0][0] <= it:
                    _, fcc_fn, q0, q1 = pending_cc.pop(0)
                    E.wait("gpsimd", "qs0", q0)
                    E.wait("gpsimd", "qs1", q1)
                    E.emit("gpsimd", fcc_fn); E.bump("cc")
                wev = wcast(wbuf[par], wqk_h[ly], fb * SLAB, DT, 128, wsem)
                E.wait("tensor", wsem, wev)
                E.wait("tensor", "dve", xn_dve)
                if it >= 2:
                    E.wait("tensor", "act", act_evt[it - 2])
                def fmm(h, par=par):
                    for dt in range(DT):
                        for hf in range(TH):
                            ins = h.matmul(ps[par][:, hf * 512:hf * 512 + 512],
                                           wbuf[par][:, dt * 128:(dt + 1) * 128],
                                           xn_sb[:, dt * T + hf * 512: dt * T + hf * 512 + 512],
                                           start=(dt == 0), stop=(dt == DT - 1))
                    ins.then_inc(sems["pe"], 1)
                E.emit("tensor", fmm); pe_evt[it] = E.bump("pe")
                E.wait("scalar", "pe", pe_evt[it])
                E.wait("scalar", f"qs{par}", E.ctr[f"qs{par}"])
                E.wait_snap("scalar", ent, ("dve",))
                def fev(h, fb=fb, par=par):
                    h.add(out=qstage[par][:, 0:T], in_=ps[par][:, 0:T],
                          add=bqk_sb[ly][:, fb:fb + 1]).then_inc(sems["act"], 1)
                E.emit("scalar", fev); act_evt[it] = E.bump("act")
                E.wait("sync", "act", act_evt[it])
                def fst(h, fb=fb, par=par):
                    h.dma_start(out=bass.AP(qkv_local, fb * 128 * T, [[T, 128], [1, T]]),
                                in_=qstage[par][:, 0:T]).then_inc(sems[f"qs{par}"], 16)
                E.emit("sync", fst); E.bump(f"qs{par}", 16)
                if it == DT - 1:
                    def fcck(h):
                        h.collective_compute("AllGather", ALU.bypass,
                                             replica_groups=[[0, 1, 2, 3], [4, 5, 6, 7]],
                                             ins=[qkv_local[D:2 * D, :].opt()],
                                             outs=[qkv_gath[0:4 * D, :].opt()]
                                             ).then_inc(sems["cc"])
                    pending_cc.append((it + 4, fcck, E.ctr["qs0"], E.ctr["qs1"]))
                if it == 2 * DT - 1:
                    def fccq(h):
                        h.collective_compute("AllGather", ALU.bypass,
                                             replica_groups=[[0, 1, 2, 3], [4, 5, 6, 7]],
                                             ins=[qkv_local[0:D, :].opt()],
                                             outs=[qkv_gath[4 * D:8 * D, :].opt()]
                                             ).then_inc(sems["cc"])
                    pending_cc.append((it + 4, fccq, E.ctr["qs0"], E.ctr["qs1"]))
            for _, fcc_fn, q0, q1 in pending_cc:
                E.wait("gpsimd", "qs0", q0)
                E.wait("gpsimd", "qs1", q1)
                E.emit("gpsimd", fcc_fn); E.bump("cc")
            pending_cc = []
            cc_all = E.ctr["cc"]
            E.wait("sync", "cc", cc_all)
            E.wait("scalar" if ly == 0 else "gpsimd", "cc", cc_all)
            head_done_pe = E.ctr["pe"]
            a4ent = E.snap()
            SBANKS = [ps[0][:, 0:512], ps[0][:, 512:1024],
                      ps[1][:, 0:512], ps[1][:, 512:1024],
                      ps[2][:, 0:512], ps[2][:, 512:1024]]
            SPAIRS = [ps[0], ps[1], ps[2]]
            first_scores = [True]
            fr3_guard = [0]
            last_bank_exp = [a4ent["act"]] * 6
            av_ring = []
            ring_ctr = [0]
            veng = "scalar" if ly == 0 else "gpsimd"
            vtag = "act" if ly == 0 else "pl"
            vsem = "vla" if ly == 0 else "vlb"
            pend_o = []
            qk_mult = CH if HPC * 128 == 512 else HPC * 128 * T

            def head_load(hp, guard_pe):
                hpar = hp % 2
                hls = "hl" if hpar == 0 else "hl1"
                vls = vsem if hpar == 0 else vsem + "1"
                E.wait("sync", "pe", guard_pe)
                E.wait(veng, "pe", guard_pe)
                def fld(h, hp=hp, hpar=hpar, hls=hls):
                    ro = RT["roff"](h, "sp", HPC * 128 * T, hp * 128 * T + 4 * D * T)
                    h.dma_start(out=Q_views[hpar][:, :].rearrange("p (r t) -> p r t", r=4),
                                in_=bass.AP(qkv_gath, ro,
                                            [[T, 128], [D * T, 4], [1, T]])).then_inc(sems[hls], 16)
                    ro = RT["roff"](h, "sp", HPC * 128 * T, hp * 128 * T)
                    h.dma_start(out=K_views[hpar][:, :].rearrange("p (r t) -> p r t", r=4),
                                in_=bass.AP(qkv_gath, ro,
                                            [[T, 128], [D * T, 4], [1, T]])).then_inc(sems[hls], 16)
                E.emit("sync", fld)
                qk_ev = E.bump(hls, 16 * 2)
                return (hls, qk_ev)

            done_pe = {}
            loads = {0: head_load(0, ent["pe"])}
            for hp in range(HPC):
                if pend_o:
                    fcc_fn, o0, o1 = pend_o.pop(0)
                    E.wait("gpsimd", "os0", o0)
                    E.wait("gpsimd", "os1", o1)
                    E.emit("gpsimd", fcc_fn); E.bump("cc")
                if hp + 1 < HPC:
                    loads[hp + 1] = head_load(hp + 1, done_pe.get(hp - 1, ent["pe"]))
                E.wait(veng, "pe", done_pe.get(hp - 1, ent["pe"]))
                def fldv(h, hp=hp):
                    for r in range(4):
                        ro = RT["roff"](h, vtag, HPC * 128, hp * 128 + VBASE + r * D * T)
                        h.dma_start(out=V_view[:, r * T:(r + 1) * T].rearrange(
                                        "p (b f) -> p b f", b=T // 128),
                                    in_=bass.AP(qkv_gath, ro,
                                                [[D, 128], [128 * D, T // 128], [1, 128]])
                                    ).then_inc(sems[vsem], 16)
                E.emit(veng, fldv)
                v_ev = E.bump(vsem, 16 * 4)
                hls, qk_ev = loads[hp]
                E.wait("tensor", hls, qk_ev)
                E.wait("tensor", vsem, v_ev)
                hpar = hp % 2
                Q_view, K_view = Q_views[hpar], K_views[hpar]
                for qt in range(QT):
                    nkb = 4 * (qt + 1)
                    bq = (hp * QT + qt) % 2
                    exp_evt, zz_evt, mm_evt = {}, {}, {}
                    bias_act = [0]
                    av_last = 0
                    for g in range(nkb + 6):
                        e = g - 6
                        if 0 <= e < nkb:
                            rs = ring_ctr[0] % 4
                            ring_ctr[0] += 1
                            diag = e >= 4 * qt
                            E.wait("scalar", "pe", mm_evt[e])
                            E.wait("scalar", "act", bias_act[0])
                            if len(av_ring) >= 4:
                                E.wait("scalar", "pe", av_ring[-4])
                            def fex(h, e=e, rs=rs, bq=bq):
                                h.activation(out=pring[:, rs * 512:(rs + 1) * 512],
                                             in_=SBANKS[e % 6], func=AF.Exp,
                                             bias=bias_sb[:, bq:bq + 1],
                                             scale=C_SM).then_inc(sems["act"], 1)
                            E.emit("scalar", fex)
                            ev = E.bump("act")
                            exp_evt[e] = (ev, rs)
                            last_bank_exp[e % 6] = ev
                            if diag:
                                pat = e - 4 * qt
                                E.wait("vector", "act", ev)
                                E.wait("vector", "msk", 16)
                                def fzz(h, rs=rs, pat=pat):
                                    h.tensor_mul(out=pring[:, rs * 512:(rs + 1) * 512],
                                                 in0=pring[:, rs * 512:(rs + 1) * 512],
                                                 in1=mask01_sb[:, pat * 512:(pat + 1) * 512]
                                                 ).then_inc(sems["dve"], 1)
                                E.emit("vector", fzz)
                                zz_evt[e] = E.bump("dve")
                        if g < nkb:
                            E.wait("tensor", "act", last_bank_exp[g % 6])
                            if first_scores[0]:
                                E.wait("tensor", "dve", xn_dve)
                                E.wait_snap("tensor", a4ent, ("dve", "act"))
                                first_scores[0] = False
                            def fsc(h, kb=g, qt=qt, si=g % 6, KV=K_view, QV=Q_view):
                                h.matmul(SBANKS[si], KV[:, kb * 128:(kb + 1) * 128],
                                         QV[:, qt * 512:(qt + 1) * 512],
                                         start=True, stop=True).then_inc(sems["pe"], 1)
                            E.emit("tensor", fsc)
                            mm_evt[g] = E.bump("pe")
                            if g == 1:
                                E.wait("vector", "pe", mm_evt[1])
                                def fpr(h, w=E.ctr["dve"]):
                                    h.wait_ge(sems["dve"], w)
                                    h.tensor_reduce(out=Mp_sb[:, 0:1], in_=ps[0][:, 0:1024],
                                                    axis=AX.X, op=ALU.max).then_inc(sems["dve"], 1)
                                E.emit("vector", fpr); pr_evt = E.bump("dve")
                            if g == 2:
                                E.wait("tensor", "dve", pr_evt)
                                if fr3_guard[0]:
                                    E.wait("tensor", "dve", fr3_guard[0])
                                E.wait("tensor", "ld", ent["ld"])
                                def ftr(h):
                                    h.transpose(ps[3][0:1, 512:640], Mp_sb[:, 0:1],
                                                ident_sb[:, :]).then_inc(sems["pe"], 1)
                                E.emit("tensor", ftr); tr_evt = E.bump("pe")
                                E.wait("vector", "pe", tr_evt)
                                def fmu(h):
                                    h.tensor_reduce(out=mu_sb[0:1, 0:1],
                                                    in_=ps[3][0:1, 512:640], axis=AX.X,
                                                    op=ALU.max).then_inc(sems["dve"], 1)
                                E.emit("vector", fmu); E.bump("dve")
                                E.wait("tensor", "dve", E.ctr["dve"])
                                def fb4(h):
                                    h.matmul(ps[3][:, 640:641], ones_r_f[:, :],
                                             mu_sb[0:1, 0:1], start=True,
                                             stop=True).then_inc(sems["pe"], 1)
                                E.emit("tensor", fb4); b4 = E.bump("pe")
                                E.wait("scalar", "pe", b4)
                                def fbi(h, bq=bq):
                                    h.activation(out=bias_sb[:, bq:bq + 1],
                                                 in_=ps[3][:, 640:641], func=AF.Copy,
                                                 scale=-C_SM).then_inc(sems["act"], 1)
                                E.emit("scalar", fbi); bias_act[0] = E.bump("act")
                        if 0 <= e < nkb:
                            ev, rs = exp_evt[e]
                            E.wait("tensor", "act", ev)
                            if e in zz_evt:
                                E.wait("tensor", "dve", zz_evt[e])
                            if e == 0 and fr3_guard[0]:
                                E.wait("tensor", "dve", fr3_guard[0])
                            def fav(h, kb=e, rs=rs, first=(e == 0), last=(e == nkb - 1),
                                    VV=V_view):
                                h.matmul(ps[3][:, 0:512], VV[:, kb * 128:(kb + 1) * 128],
                                         pring[:, rs * 512:(rs + 1) * 512],
                                         start=first, stop=last)
                                h.matmul(ps[3][0:1, 512:1024], ones_c_bf[:, :],
                                         pring[:, rs * 512:(rs + 1) * 512],
                                         start=first, stop=last).then_inc(sems["pe"], 1)
                            E.emit("tensor", fav)
                            av_ring.append(E.bump("pe"))
                            av_last = av_ring[-1]
                    E.wait("vector", "pe", av_last)
                    def fr1(h, w=E.ctr["dve"]):
                        h.wait_ge(sems["dve"], w)
                        h.reciprocal(out=rec_sb[:, :],
                                     in_=ps[3][0:1, 512:1024]).then_inc(sems["dve"], 1)
                    E.emit("vector", fr1); E.bump("dve")
                    E.wait("tensor", "dve", E.ctr["dve"])
                    def fr2(h):
                        h.matmul(ps[2][:, 0:512], ones_r_f[:, :], rec_sb[:, :],
                                 start=True, stop=True).then_inc(sems["pe"], 1)
                    E.emit("tensor", fr2); r2 = E.bump("pe")
                    E.wait("scalar", "pe", r2)
                    E.wait("scalar", "dve", fr3_guard[0])
                    def frc(h):
                        h.copy(out=recb_sb[:, :], in_=ps[2][:, 0:512]).then_inc(sems["act"], 1)
                    E.emit("scalar", frc)
                    last_bank_exp[4] = E.bump("act")
                    ix = hp * QT + qt
                    opar2 = ix % 2
                    E.wait("vector", "act", last_bank_exp[4])
                    E.wait("vector", f"os{opar2}", E.ctr[f"os{opar2}"])
                    E.wait("vector", "vs0", E.ctr["vs0"])
                    E.wait("vector", "vs1", E.ctr["vs1"])
                    def fr3(h, opar2=opar2):
                        h.tensor_mul(out=ost[opar2][:, :], in0=ps[3][:, 0:512],
                                     in1=recb_sb[:, :]).then_inc(sems["dve"], 1)
                    E.emit("vector", fr3); E.bump("dve")
                    fr3_guard[0] = E.ctr["dve"]
                    E.wait("sync", "dve", E.ctr["dve"])
                    def fr4(h, hp=hp, qt=qt, opar2=opar2):
                        h.dma_start(out=bass.AP(o_local, hp * 128 * S + qt * 512, [[S, 128], [1, 512]]),
                                    in_=ost[opar2][:, :]).then_inc(sems[f"os{opar2}"], 16)
                    E.emit("sync", fr4); E.bump(f"os{opar2}", 16)
                done_pe[hp] = E.ctr["pe"]
                E.wait("gpsimd", "ota", ent["ota"])
                E.wait("gpsimd", "otb", ent["otb"])
                def fcc2(h, hp=hp):
                    h.collective_compute("AllGather", ALU.bypass,
                                         replica_groups=[[0, 1, 2, 3], [4, 5, 6, 7]],
                                         ins=[o_local[hp * 128:(hp + 1) * 128, :].opt()],
                                         outs=[o_gath[hp * 4 * 128:(hp + 1) * 4 * 128, :].opt()]
                                         ).then_inc(sems["cc"])
                pend_o.append((fcc2, E.ctr["os0"], E.ctr["os1"]))
            for fcc_fn, o0, o1 in pend_o:
                E.wait("gpsimd", "os0", o0)
                E.wait("gpsimd", "os1", o1)
                E.emit("gpsimd", fcc_fn); E.bump("cc")
            cc_o = E.ctr["cc"]
            osem = "ota" if ly == 0 else "otb"
            E.wait(veng, "cc", cc_o)
            E.wait(veng, "pe", E.ctr["pe"])
            def fot(h):
                for r in range(4):
                    ro = RT["roff"](h, vtag, T, r * 128 * S)
                    h.dma_start(out=xn_sb[:, (r * HPC) * T:(r * HPC + HPC) * T].rearrange(
                                    "p (hh n) -> p hh n", hh=HPC),
                                in_=bass.AP(o_gath, ro,
                                            [[S, 128], [4 * 128 * S, HPC], [1, T]])).then_inc(sems[osem], 16)
            E.emit(veng, fot)
            ot_ld = E.bump(osem, 16 * 4)
            E.wait("vector", osem, ot_ld)
            E.wait("vector", "ld", ent["ld"])
            for hh in range(NH):
                def fbv(h, hh=hh, ly=ly):
                    h.tensor_scalar_add(out=xn_sb[:, hh * T:(hh + 1) * T],
                                        in0=xn_sb[:, hh * T:(hh + 1) * T],
                                        scalar1=bvh_sb[ly][:, hh:hh + 1]).then_inc(sems["dve"], 1)
                E.emit("vector", fbv); E.bump("dve")
            ot_dve = E.ctr["dve"]
            op_pe, op_dve = {}, {}
            for db in range(DT):
                par = db % 2
                wsem = f"wc{par}"
                if db >= 2:
                    E.wait("gpsimd", "pe", op_pe[db - 2])
                else:
                    E.wait("gpsimd", "pe", pe_evt[2 * DT - 2 + db])
                wev = wcast(wbuf[par], wo_h[ly], db * SLAB, DT, 128, wsem)
                E.wait("tensor", wsem, wev)
                E.wait("tensor", osem, ot_ld)
                E.wait("tensor", "dve", ot_dve)
                if db >= 2:
                    E.wait("tensor", "dve", op_dve[db - 2])
                def fmm(h, par=par):
                    for ht in range(NH):
                        for hf in range(TH):
                            ins = h.matmul(ps[par][:, hf * 512:hf * 512 + 512],
                                           wbuf[par][:, ht * 128:(ht + 1) * 128],
                                           xn_sb[:, ht * T + hf * 512: ht * T + hf * 512 + 512],
                                           start=(ht == 0), stop=(ht == NH - 1))
                    ins.then_inc(sems["pe"], 1)
                E.emit("tensor", fmm); op_pe[db] = E.bump("pe")
                E.wait("vector", "pe", op_pe[db])
                def fres(h, db=db, par=par):
                    h.scalar_tensor_tensor(out=x_sb[:, db * T:(db + 1) * T], in0=ps[par][:, 0:T],
                                           scalar=bo_sb[ly][:, db:db + 1],
                                           in1=x_sb[:, db * T:(db + 1) * T],
                                           op0=ALU.add, op1=ALU.add).then_inc(sems["dve"], 1)
                E.emit("vector", fres); op_dve[db] = E.bump("dve")
            return op_dve

        def p_ffn(second, xn_dve):
            ent = E.snap()
            g_pe, g_act, g_dve = {}, {}, {}
            d_pe_g, d_dve_g = {}, {}
            down_guard = 0
            for hq in range(8):
                for fb in range(FQ):
                    gb = hq * FQ + fb
                    par = fb % 2
                    weng = "gpsimd" if not second else "sync"
                    if gb >= 2:
                        E.wait(weng, "pe", g_pe[(gb - 2, "u")])
                    else:
                        E.wait_snap(weng, ent, ("pe", "fwb0", "fwb1"))
                    if not second:
                        E.wait("gpsimd", f"fwb{par}", E.ctr[f"fwb{par}"])
                        cg = wcast(wbuf[par], wg_h, gb * SLAB, DT, 128, f"wc{par}")
                        cu = wcast(uwbuf[par], wu_h, gb * SLAB, DT, 128, f"uc{par}")
                        E.wait("sync", f"wc{par}", cg)
                        E.wait("sync", f"uc{par}", cu)
                        def fwb(h, gb=gb, par=par):
                            h.dma_start(out=bass.AP(wg_bf, gb * SLAB, [[SLAB // 128, 128], [1, SLAB // 128]]),
                                        in_=wbuf[par][:, :]).then_inc(sems[f"fwb{par}"], 16)
                            h.dma_start(out=bass.AP(wu_bf, gb * SLAB, [[SLAB // 128, 128], [1, SLAB // 128]]),
                                        in_=uwbuf[par][:, :]).then_inc(sems[f"fwb{par}"], 16)
                        E.emit("sync", fwb); E.bump(f"fwb{par}", 32)
                        E.wait("tensor", f"wc{par}", cg)
                        E.wait("tensor", f"uc{par}", cu)
                    else:
                        def fw(h, gb=gb, par=par):
                            h.dma_start(out=wbuf[par][:, :].rearrange("p (t f) -> p t f", t=DT),
                                        in_=bass.AP(wg_bf, gb * SLAB,
                                                    [[SLAB // 128, 128], [128, DT], [1, 128]])
                                        ).then_inc(sems[f"wl{par}"], 16)
                            h.dma_start(out=uwbuf[par][:, :].rearrange("p (t f) -> p t f", t=DT),
                                        in_=bass.AP(wu_bf, gb * SLAB,
                                                    [[SLAB // 128, 128], [128, DT], [1, 128]])
                                        ).then_inc(sems[f"wl{par}"], 16)
                        E.emit("sync", fw)
                        wlev = E.bump(f"wl{par}", 32)
                        E.wait("tensor", f"wl{par}", wlev)
                    E.wait("tensor", "dve", xn_dve)
                    E.wait("tensor", "dve", down_guard)
                    if gb >= 2:
                        E.wait("tensor", "act", g_act[gb - 2])
                        E.wait("tensor", "dve", g_dve[gb - 2])
                    def fg(h, par=par):
                        for dt in range(DT):
                            for hf in range(TH):
                                ins = h.matmul(ps[par][:, hf * 512:hf * 512 + 512],
                                               wbuf[par][:, dt * 128:(dt + 1) * 128],
                                               xn_sb[:, dt * T + hf * 512: dt * T + hf * 512 + 512],
                                               start=(dt == 0), stop=(dt == DT - 1))
                        ins.then_inc(sems["pe"], 1)
                    E.emit("tensor", fg); g_pe[(gb, "g")] = E.bump("pe")
                    def fu(h, par=par):
                        for dt in range(DT):
                            for hf in range(TH):
                                ins = h.matmul(ps[2 + par][:, hf * 512:hf * 512 + 512],
                                               uwbuf[par][:, dt * 128:(dt + 1) * 128],
                                               xn_sb[:, dt * T + hf * 512: dt * T + hf * 512 + 512],
                                               start=(dt == 0), stop=(dt == DT - 1))
                        ins.then_inc(sems["pe"], 1)
                    E.emit("tensor", fu); g_pe[(gb, "u")] = E.bump("pe")
                    E.wait("scalar", "pe", g_pe[(gb, "g")])
                    if gb >= 2:
                        E.wait("scalar", "dve", g_dve[gb - 2])
                    else:
                        E.wait_snap("scalar", ent, ("dve", "qs0", "qs1"))
                    def fsi(h, par=par):
                        h.activation(out=silustage[par][:, 0:T], in_=ps[par][:, 0:T],
                                     func=AF.Silu).then_inc(sems["act"], 1)
                    E.emit("scalar", fsi); g_act[gb] = E.bump("act")
                    E.wait("vector", "pe", g_pe[(gb, "u")])
                    E.wait("vector", "act", g_act[gb])
                    def fgm(h, fb=fb, par=par):
                        h.tensor_mul(out=gated_view[:, fb * T:(fb + 1) * T], in0=ps[2 + par][:, 0:T],
                                     in1=silustage[par][:, 0:T]).then_inc(sems["dve"], 1)
                    E.emit("vector", fgm); g_dve[gb] = E.bump("dve")
                gate_dve = E.ctr["dve"]
                for db in range(DT):
                    par = db % 2
                    sl = hq * DT + db
                    weng = "gpsimd" if not second else "sync"
                    if sl >= 2:
                        E.wait(weng, "pe", d_pe_g[sl - 2])
                    else:
                        E.wait_snap(weng, ent, ("pe", "dwb0", "dwb1"))
                    if not second:
                        E.wait("gpsimd", f"dwb{par}", E.ctr[f"dwb{par}"])
                        cd = wcast(dwbuf[par], wd_h, sl * DSLAB, FQ, 128, f"dc{par}")
                        E.wait("sync", f"dc{par}", cd)
                        def fwb(h, sl=sl, par=par):
                            h.dma_start(out=bass.AP(wd_bf, sl * DSLAB,
                                                    [[DSLAB // 128, 128], [1, DSLAB // 128]]),
                                        in_=dwbuf[par][:, :]).then_inc(sems[f"dwb{par}"], 16)
                        E.emit("sync", fwb); E.bump(f"dwb{par}", 16)
                        E.wait("tensor", f"dc{par}", cd)
                    else:
                        def fw(h, sl=sl, par=par):
                            h.dma_start(out=dwbuf[par][:, :].rearrange("p (t f) -> p t f", t=FQ),
                                        in_=bass.AP(wd_bf, sl * DSLAB,
                                                    [[DSLAB // 128, 128], [128, FQ], [1, 128]])
                                        ).then_inc(sems[f"dl{par}"], 16)
                        E.emit("sync", fw)
                        dlev = E.bump(f"dl{par}", 16)
                        E.wait("tensor", f"dl{par}", dlev)
                    E.wait("tensor", "dve", gate_dve)
                    if sl >= 2:
                        E.wait("tensor", "dve", d_dve_g.get(sl - 2, 0))
                    def fdm(h, par=par):
                        for hb in range(FQ):
                            for hf in range(TH):
                                ins = h.matmul(ps[par][:, hf * 512:hf * 512 + 512],
                                               dwbuf[par][:, hb * 128:(hb + 1) * 128],
                                               gated_view[:, hb * T + hf * 512: hb * T + hf * 512 + 512],
                                               start=(hb == 0), stop=(hb == FQ - 1))
                        ins.then_inc(sems["pe"], 1)
                    E.emit("tensor", fdm); d_pe_g[sl] = E.bump("pe")
                    E.wait("vector", "pe", d_pe_g[sl])
                    def fres(h, db=db, par=par):
                        h.scalar_tensor_tensor(out=x_sb[:, db * T:(db + 1) * T], in0=ps[par][:, 0:T],
                                               scalar=0.0, in1=x_sb[:, db * T:(db + 1) * T],
                                               op0=ALU.add, op1=ALU.add).then_inc(sems["dve"], 1)
                    E.emit("vector", fres); d_dve_g[sl] = E.bump("dve")
                down_guard = E.ctr["dve"]
            return {db: d_dve_g[7 * DT + db] for db in range(DT)}

        fold_guard = [0, 0]
        last_ln_pe = [0]

        p_init()
        xn1 = p_ln()
        prevA = p_attn(0, xn1)
        xn2 = p_ln(prevA)
        prevF = p_ffn(False, xn2)
        xn3 = p_ln(prevF)
        prevB = p_attn(1, xn3)
        xn4 = p_ln(prevB)
        p_ffn(True, xn4)

        E.wait("sync", "dve", E.ctr["dve"])
        def ffin(h):
            h.dma_start(out=yT_h.ap().rearrange("(t p) n -> p t n", p=128),
                        in_=x_sb[:, :].rearrange("p (t n) -> p t n", t=DT)).then_inc(sems["out"], 16)
        E.emit("sync", ffin)
        E.bump("out", 16)
        E.wait("sync", "out", 16)

        with nc.Block() as block:
            @block.sync
            def _(h):
                for f in E.q["sync"]:
                    f(h)

            @block.tensor
            def _(h):
                for f in E.q["tensor"]:
                    f(h)

            @block.scalar
            def _(h):
                for f in E.q["scalar"]:
                    f(h)

            @block.vector
            def _(h):
                for f in E.q["vector"]:
                    f(h)

            @block.gpsimd
            def _(h):
                for f in E.q["gpsimd"]:
                    f(h)

    return nc



def make_in_maps(cfg, inputs):
    shared = host_inputs(cfg, inputs)
    x = np.asarray(inputs["L0_attn_a_x"], dtype=np.float32)
    in_maps = []
    for c in range(8):
        g, r = c // 4, c % 4
        xs = np.ascontiguousarray(x[g, r * cfg.T:(r + 1) * cfg.T, :].T)
        m = {"xT": xs}
        m.update(shared)
        in_maps.append(m)
    return in_maps


def assemble_out(cfg, results):
    out = np.empty((cfg.B, cfg.S, cfg.D), dtype=np.float32)
    for c in range(8):
        g, r = c // 4, c % 4
        out[g, r * cfg.T:(r + 1) * cfg.T, :] = results[c]["yT"].T
    return out


def kernel(**inputs):
    cfg = CFG
    nc = build_program(cfg)
    in_maps = make_in_maps(cfg, inputs)
    res = run_bass_kernel_spmd(nc, in_maps, core_ids=list(range(8)))
    return assemble_out(cfg, res.results)


# version 17
# speedup vs baseline: 1.0209x; 1.0053x over previous
import numpy as np
from contextlib import ExitStack

import concourse.bass as bass
import concourse.mybir as mybir
from concourse.bass_utils import run_bass_kernel_spmd
from concourse.aot_env import allow_ici_all_to_all

F32 = mybir.dt.float32
BF16 = mybir.dt.bfloat16
AF = mybir.ActivationFunctionType
ALU = mybir.AluOpType
AX = mybir.AxisListType

NEG = -1e30
C_SM = float(1.0 / np.sqrt(128.0))


class Cfg:
    def __init__(self, D=2048, FF=8192, NH=16, S=4096, B=2):
        self.D, self.FF, self.NH, self.S, self.B = D, FF, NH, S, B
        self.DT = D // 128
        self.T = S // 4
        self.HPC = NH // 4
        self.QT = S // 512
        self.KB = S // 128
        self.FQ = FF // 8 // 128
        self.TH = self.T // 512
        self.VW = min(256, (self.NH // 4) * 128)


CFG = Cfg()



def _pack_lhsT(w_lhsT, nblk):
    K, M = w_lhsT.shape
    dt = K // 128
    a = w_lhsT.reshape(dt, 128, nblk, 128).transpose(2, 1, 0, 3)
    return np.ascontiguousarray(a.reshape(nblk, dt * 128 * 128))


def _pack_rhs(w, nchunk, width):
    K, M = w.shape
    dt = K // 128
    a = w.reshape(dt, 128, nchunk, width).transpose(2, 1, 0, 3)
    return np.ascontiguousarray(a.reshape(nchunk, dt * 128 * width))


def _pack_down(wd_T, ff, d):
    fq = ff // 8 // 128
    dbn = d // 128
    a = wd_T.reshape(8, fq, 128, dbn, 128).transpose(0, 3, 2, 1, 4)
    return np.ascontiguousarray(a.reshape(8 * dbn, 128 * fq * 128))


def _make_masks01():
    m = np.zeros((128, 4, 512), dtype=np.float32)
    p = np.arange(128)[:, None]
    qf = np.arange(512)[None, :]
    for s in range(4):
        m[:, s, :] = np.where(s * 128 + p <= qf, 1.0, 0.0)
    return np.ascontiguousarray(m.reshape(128, 4 * 512))


def host_inputs(cfg, inputs):
    f = np.float32
    d = {}
    pairs = (("A", np.ascontiguousarray(np.asarray(inputs["L0_attn_a_W_qkv"], f).T),
              np.asarray(inputs["L0_attn_a_b_qkv"], f)),
             ("B", np.ascontiguousarray(np.asarray(inputs["L0_attn_b_w_qkv"], f)),
              np.asarray(inputs["L0_attn_b_b_qkv"], f)))
    for tag, w, b in pairs:
        d[f"wqk{tag}"] = _pack_lhsT(w[:, : 2 * cfg.D], 2 * cfg.DT)
        d[f"wv{tag}"] = _pack_rhs(w[:, 2 * cfg.D:], cfg.D // cfg.VW, cfg.VW)
        d[f"bqk{tag}"] = np.ascontiguousarray(b[: 2 * cfg.D].reshape(2 * cfg.DT, 128).T)
        d[f"bv{tag}"] = np.ascontiguousarray(b[2 * cfg.D:].reshape(cfg.NH, 128).T)
    d["woA"] = _pack_lhsT(np.ascontiguousarray(np.asarray(inputs["L0_attn_a_W_out"], f).T), cfg.DT)
    d["woB"] = _pack_lhsT(np.ascontiguousarray(np.asarray(inputs["L0_attn_b_w_proj"], f)), cfg.DT)
    d["boA"] = np.ascontiguousarray(np.asarray(inputs["L0_attn_a_b_out"], f).reshape(cfg.DT, 128).T)
    d["boB"] = np.ascontiguousarray(np.asarray(inputs["L0_attn_b_b_proj"], f).reshape(cfg.DT, 128).T)
    d["wg"] = _pack_lhsT(np.ascontiguousarray(np.asarray(inputs["L0_ffn_w_gate"], f).T), cfg.FF // 128)
    d["wu"] = _pack_lhsT(np.ascontiguousarray(np.asarray(inputs["L0_ffn_w_up"], f).T), cfg.FF // 128)
    d["wd"] = _pack_down(np.ascontiguousarray(np.asarray(inputs["L0_ffn_w_down"], f).T), cfg.FF, cfg.D)
    d["masks"] = _make_masks01()
    d["identity"] = np.eye(128, dtype=f)
    return d



SEM_NAMES = ("pe", "act", "dve", "gp", "cc", "ld", "msk",
             "qs0", "qs1", "vs0", "vs1", "os0", "os1",
             "wc0", "wc1", "uc0", "uc1", "dc0", "dc1", "vc", "vc1",
             "wl0", "wl1", "dl0", "dl1",
             "fwb0", "fwb1", "dwb0", "dwb1",
             "hl", "hl1", "vla", "vla1", "vlb", "vlb1", "ota", "otb", "out")


class Emitter:
    ENGS = ("sync", "tensor", "scalar", "vector", "gpsimd")

    def __init__(self):
        self.q = {e: [] for e in self.ENGS}
        self.ctr = {k: 0 for k in SEM_NAMES}
        self.lastwait = {e: {} for e in self.ENGS}
        self.sems = {}

    def emit(self, eng, fn):
        self.q[eng].append(fn)

    def wait(self, eng, sem, val):
        if val is None or val <= 0:
            return
        lw = self.lastwait[eng]
        if lw.get(sem, 0) >= val:
            return
        lw[sem] = val
        s = self.sems[sem]

        def f(h, s=s, v=val):
            h.wait_ge(s, v)
        self.q[eng].append(f)

    def snap(self):
        return dict(self.ctr)

    def wait_snap(self, eng, snap, names):
        for n in names:
            self.wait(eng, n, snap[n])

    def bump(self, name, by=1):
        self.ctr[name] += by
        return self.ctr[name]


def build_program(cfg):
    nc = bass.Bass("TRN2", target_bir_lowering=False, debug=False)
    E = Emitter()
    DT, T, QT, KB, HPC, FQ, TH = cfg.DT, cfg.T, cfg.QT, cfg.KB, cfg.HPC, cfg.FQ, cfg.TH
    VW = cfg.VW
    D, FF, NH, S = cfg.D, cfg.FF, cfg.NH, cfg.S
    SLAB = DT * 128 * 128
    VSLAB = DT * 128 * VW
    DSLAB = 128 * FQ * 128

    xT_h = nc.dram_tensor("xT", [D, T], F32, kind="ExternalInput")
    yT_h = nc.dram_tensor("yT", [D, T], F32, kind="ExternalOutput")
    wqk_h = {0: nc.dram_tensor("wqkA", [2 * DT, SLAB], F32, kind="ExternalInput"),
             1: nc.dram_tensor("wqkB", [2 * DT, SLAB], F32, kind="ExternalInput")}
    wv_h = {0: nc.dram_tensor("wvA", [D // VW, VSLAB], F32, kind="ExternalInput"),
            1: nc.dram_tensor("wvB", [D // VW, VSLAB], F32, kind="ExternalInput")}
    bqk_h = {0: nc.dram_tensor("bqkA", [128, 2 * DT], F32, kind="ExternalInput"),
             1: nc.dram_tensor("bqkB", [128, 2 * DT], F32, kind="ExternalInput")}
    bv_h = {0: nc.dram_tensor("bvA", [128, NH], F32, kind="ExternalInput"),
            1: nc.dram_tensor("bvB", [128, NH], F32, kind="ExternalInput")}
    wo_h = {0: nc.dram_tensor("woA", [DT, SLAB], F32, kind="ExternalInput"),
            1: nc.dram_tensor("woB", [DT, SLAB], F32, kind="ExternalInput")}
    bo_h = {0: nc.dram_tensor("boA", [128, DT], F32, kind="ExternalInput"),
            1: nc.dram_tensor("boB", [128, DT], F32, kind="ExternalInput")}
    wg_h = nc.dram_tensor("wg", [FF // 128, SLAB], F32, kind="ExternalInput")
    wu_h = nc.dram_tensor("wu", [FF // 128, SLAB], F32, kind="ExternalInput")
    wd_h = nc.dram_tensor("wd", [8 * DT, DSLAB], F32, kind="ExternalInput")
    masks_h = nc.dram_tensor("masks", [128, 4 * 512], F32, kind="ExternalInput")
    ident_h = nc.dram_tensor("identity", [128, 128], F32, kind="ExternalInput")

    NCK = D // 512
    CH = 4 * 512 * T
    VBASE = 2 * NCK * CH
    qkv_local = nc.dram_tensor("qkv_local", [3 * D, T], BF16)
    qkv_gath = nc.dram_tensor("qkv_gath", [4 * 3 * D, T], BF16)
    o_local = nc.dram_tensor("o_local", [HPC * 128, S], BF16)
    o_gath = nc.dram_tensor("o_gath", [4 * HPC * 128, S], BF16)
    wg_bf = nc.dram_tensor("wg_bf", [FF // 128, SLAB], BF16)
    wu_bf = nc.dram_tensor("wu_bf", [FF // 128, SLAB], BF16)
    wd_bf = nc.dram_tensor("wd_bf", [8 * DT, DSLAB], BF16)

    es = ExitStack()
    with es:
        sbt = lambda name, shape, dt: es.enter_context(nc.sbuf_tensor(name, shape, dt))
        x_sb = sbt("x_sb", [128, DT * T], F32)
        xn_sb = sbt("xn_sb", [128, max(DT, NH) * T], BF16)
        sst_sb = sbt("sst_sb", [128, 6 * 512], F32)
        arena = sbt("arena", [128, max(5 * KB * 128, FQ * T)], BF16)
        WB = DT * 128
        wt = sbt("wt", [128, max(4 * WB + 2 * FQ * 128, 2 * WB + DT * VW) + DT * VW], BF16)
        stage = sbt("stage", [128, 2 * 1024], BF16)
        vstage = sbt("vstage", [128, 2 * 512], BF16)
        mask01_sb = sbt("mask01_sb", [128, 4 * 512], BF16)
        ident_sb = sbt("ident_sb", [128, 128], F32)
        bqk_sb = {ly: sbt(f"bqk{ly}_sb", [128, 2 * DT], F32) for ly in (0, 1)}
        bo_sb = {ly: sbt(f"bo{ly}_sb", [128, DT], F32) for ly in (0, 1)}
        ones_c_bf = sbt("ones_c_bf", [128, 1], BF16)
        ones_r_f = sbt("ones_r_f", [1, 128], F32)
        onescf = sbt("onescf", [128, 1], F32)
        Mc_sb = sbt("Mc_sb", [128, 1], F32)
        Mp_sb = sbt("Mp_sb", [128, 8], F32)
        mu_sb = sbt("mu_sb", [1, 3], F32)
        nmu_sb = sbt("nmu_sb", [1, 1], F32)
        f_sb = sbt("f_sb", [1, 2], F32)
        fbc_sb = sbt("fbc_sb", [128, 2], F32)
        bias_sb = sbt("bias_sb", [128, 2], F32)
        packA = sbt("packA", [128, 2 * T], F32)
        packD = sbt("packD", [128, 1536], F32)
        bvh_sb = {ly: sbt(f"bvh{ly}_sb", [128, NH], F32) for ly in (0, 1)}
        mrow_sb = packA[0:1, 0:T]
        rstd_sb = packA[0:1, T:2 * T]
        ex2_sb = packA[32:33, 0:T]
        msq_sb = packA[32:33, T:2 * T]
        var_sb = ex2_sb
        sd_sb = msq_sb
        rec_sb = packD[0:1, 0:512]
        recb_sb = packD[:, 512:1024]

        ps = [es.enter_context(nc.psum_tensor(f"ps{i}", [128, 1024], F32)) for i in range(4)]

        for name in SEM_NAMES:
            E.sems[name] = es.enter_context(nc.semaphore("sem_" + name))
        sems = E.sems

        QKV_EL = KB * 128
        Q_views = [arena[:, 0:QKV_EL], arena[:, QKV_EL:2 * QKV_EL]]
        K_views = [arena[:, 2 * QKV_EL:3 * QKV_EL], arena[:, 3 * QKV_EL:4 * QKV_EL]]
        V_view = arena[:, 4 * QKV_EL:5 * QKV_EL]
        gated_view = arena[:, 0:FQ * T]
        wbuf = [wt[:, 0:WB], wt[:, WB:2 * WB]]
        WTB = max(4 * WB + 2 * FQ * 128, 2 * WB + DT * VW)
        vwbufs = [wt[:, 2 * WB:2 * WB + DT * VW], wt[:, WTB:WTB + DT * VW]]
        uwbuf = [wt[:, 2 * WB:3 * WB], wt[:, 3 * WB:4 * WB]]
        dwbuf = [wt[:, 4 * WB:4 * WB + FQ * 128], wt[:, 4 * WB + FQ * 128:4 * WB + 2 * FQ * 128]]
        qstage = [stage[:, 0:1024], stage[:, 1024:2048]]
        silustage = qstage
        lnb = [sst_sb[:, 1024:1536].bitcast(BF16), sst_sb[:, 1536:2048].bitcast(BF16)]
        lnsq = [sst_sb[:, 2048:2560].bitcast(BF16), sst_sb[:, 2560:3072].bitcast(BF16)]
        vst = [vstage[:, 0:VW], vstage[:, VW:2 * VW]]
        ost = [vstage[:, 0:512], vstage[:, 512:1024]]
        pring = stage[:, 0:2048]

        RT = {}

        def p_init():
            def mkregs(h, tag):
                g1 = h.register(f"r_rk_{tag}")
                RT[f"_g1{tag}"] = g1
                rk = g1.__enter__()
                g2 = h.register(f"r_off_{tag}")
                RT[f"_g2{tag}"] = g2
                off = g2.__enter__()
                pid = h.partition_id()
                h.reg_mov(rk, pid % 4)
                RT[f"rk_{tag}"] = rk
                RT[f"off_{tag}"] = off

            def roff(h, tag, mult, add):
                h.reg_mov(RT[f"off_{tag}"], RT[f"rk_{tag}"])
                if mult != 1:
                    h.reg_alu(RT[f"off_{tag}"], RT[f"off_{tag}"], mult, ALU.mult)
                if add:
                    h.reg_alu(RT[f"off_{tag}"], RT[f"off_{tag}"], add, ALU.add)
                return RT[f"off_{tag}"]
            RT["roff"] = roff
            for eng, tag in (("sync", "sp"), ("scalar", "act"), ("gpsimd", "pl")):
                def fr(h, tag=tag):
                    mkregs(h, tag)
                E.emit(eng, fr)

            def fx(h):
                h.dma_start(out=x_sb[:, :].rearrange("p (t n) -> p t n", t=DT),
                            in_=xT_h.ap().rearrange("(t p) n -> p t n", p=128)).then_inc(sems["ld"], 16)
            E.emit("sync", fx); E.bump("ld", 16)
            for dst, src in [(ident_sb, ident_h), (bqk_sb[0], bqk_h[0]), (bqk_sb[1], bqk_h[1]),
                             (bo_sb[0], bo_h[0]), (bo_sb[1], bo_h[1]),
                             (bvh_sb[0], bv_h[0]), (bvh_sb[1], bv_h[1])]:
                def f2(h, dst=dst, src=src):
                    h.dma_start(out=dst[:, :], in_=src.ap()).then_inc(sems["ld"], 16)
                E.emit("sync", f2); E.bump("ld", 16)

            def fm(h):
                h.dma_start(out=mask01_sb[:, :], in_=masks_h.ap()).then_inc(sems["msk"], 16)
            E.emit("gpsimd", fm); E.bump("msk", 16)

            def fo(h):
                h.memset(ones_c_bf[:, :], 1.0).then_inc(sems["dve"], 1)
                h.memset(ones_r_f[:, :], 1.0).then_inc(sems["dve"], 1)
                h.memset(onescf[:, :], 1.0).then_inc(sems["dve"], 1)
            E.emit("vector", fo); E.bump("dve", 3)

        def p_ln(prev=None):
            ent = E.snap()
            pe_evt = {}
            for dt in range(DT):
                par = dt % 2
                if dt == 0:
                    E.wait_snap("scalar", ent, ("ld", "qs0", "qs1"))
                if dt < 2:
                    E.wait("scalar", "pe", last_ln_pe[0])
                if prev is not None:
                    E.wait("scalar", "dve", prev[dt])
                else:
                    E.wait_snap("scalar", ent, ("dve",))
                if dt >= 2:
                    E.wait("scalar", "pe", pe_evt[dt - 2])
                def fc(h, dt=dt, par=par):
                    h.copy(out=lnb[par][:, :], in_=x_sb[:, dt * T:(dt + 1) * T])
                    h.square(out=lnsq[par][:, :],
                             in_=x_sb[:, dt * T:(dt + 1) * T]).then_inc(sems["act"], 2)
                E.emit("scalar", fc); E.bump("act", 2)
                E.wait("tensor", "act", E.ctr["act"])
                if dt == 0:
                    E.wait("tensor", "act", ent["act"])
                    E.wait("tensor", "dve", prev[0] if prev is not None else ent["dve"])
                first, last = dt == 0, dt == DT - 1
                def fmm(h, par=par, first=first, last=last):
                    for hf in range(TH):
                        h.matmul(ps[2][0:1, hf * 512:hf * 512 + 512], ones_c_bf[:, :],
                                 lnb[par][:, hf * 512:hf * 512 + 512], start=first, stop=last)
                    for hf in range(TH):
                        ins = h.matmul(ps[3][0:1, hf * 512:hf * 512 + 512], ones_c_bf[:, :],
                                       lnsq[par][:, hf * 512:hf * 512 + 512], start=first, stop=last)
                    ins.then_inc(sems["pe"], 1)
                E.emit("tensor", fmm); pe_evt[dt] = E.bump("pe")
            last_ln_pe[0] = E.ctr["pe"]
            E.wait("vector", "pe", E.ctr["pe"])
            inv = 1.0 / float(D)
            def f1(h):
                for hf in range(TH):
                    h.tensor_scalar_mul(out=mrow_sb[0:1, hf * 512:hf * 512 + 512],
                                        in0=ps[2][0:1, hf * 512:hf * 512 + 512], scalar1=inv)
                for hf in range(TH):
                    ins = h.tensor_scalar_mul(out=ex2_sb[0:1, hf * 512:hf * 512 + 512],
                                              in0=ps[3][0:1, hf * 512:hf * 512 + 512], scalar1=inv)
                ins.then_inc(sems["dve"], 1)
            E.emit("vector", f1); E.bump("dve")
            E.wait("scalar", "dve", E.ctr["dve"])
            def f2(h):
                h.square(out=msq_sb[:, :], in_=mrow_sb[:, :]).then_inc(sems["act"], 1)
            E.emit("scalar", f2); E.bump("act")
            E.wait("vector", "act", E.ctr["act"])
            def f3(h):
                h.scalar_tensor_tensor(out=var_sb[:, :], in0=ex2_sb[:, :], scalar=1e-5,
                                       in1=msq_sb[:, :], op0=ALU.add,
                                       op1=ALU.subtract).then_inc(sems["dve"], 1)
            E.emit("vector", f3); E.bump("dve")
            E.wait("scalar", "dve", E.ctr["dve"])
            def f4(h):
                h.activation(out=sd_sb[:, :], in_=var_sb[:, :], func=AF.Sqrt).then_inc(sems["act"], 1)
            E.emit("scalar", f4); E.bump("act")
            E.wait("vector", "act", E.ctr["act"])
            def f5(h):
                h.reciprocal(out=rstd_sb[:, :], in_=sd_sb[:, :]).then_inc(sems["dve"], 1)
            E.emit("vector", f5); E.bump("dve")
            E.wait("tensor", "dve", E.ctr["dve"])
            def f7(h):
                for hf in range(TH):
                    h.matmul(ps[0][:, hf * 512:hf * 512 + 512], ones_r_f[:, :],
                             mrow_sb[0:1, hf * 512:hf * 512 + 512], start=True, stop=True)
                for hf in range(TH):
                    ins = h.matmul(ps[1][:, hf * 512:hf * 512 + 512], ones_r_f[:, :],
                                   rstd_sb[0:1, hf * 512:hf * 512 + 512], start=True, stop=True)
                ins.then_inc(sems["pe"], 1)
            E.emit("tensor", f7); E.bump("pe")
            E.wait("vector", "pe", E.ctr["pe"])
            dve_base = E.ctr["dve"]
            t1_sb = sst_sb[:, 0:T]
            for dt in range(DT):
                def fn(h, dt=dt, w=dve_base + 2 * dt, t1_sb=t1_sb):
                    h.wait_ge(sems["dve"], w)
                    h.tensor_sub(out=t1_sb[:, :], in0=x_sb[:, dt * T:(dt + 1) * T],
                                 in1=ps[0][:, 0:T]).then_inc(sems["dve"], 1)
                    h.wait_ge(sems["dve"], w + 1)
                    h.tensor_mul(out=xn_sb[:, dt * T:(dt + 1) * T], in0=t1_sb[:, :],
                                 in1=ps[1][:, 0:T]).then_inc(sems["dve"], 1)
                E.emit("vector", fn); E.bump("dve", 2)
            return E.ctr["dve"]

        def wcast(dst, src_h, off, tiles, width, sem):
            def f(h, dst=dst, src_h=src_h, off=off, tiles=tiles, width=width, sem=sem):
                h.dma_start(out=dst[:, :].rearrange("p (t f) -> p t f", t=tiles),
                            in_=bass.AP(src_h, off, [[tiles * width, 128], [width, tiles], [1, width]])
                            ).then_inc(sems[sem], 16)
            E.emit("gpsimd", f)
            return E.bump(sem, 16)

        def p_attn(ly, xn_dve):
            ent = E.snap()
            E.wait("sync", "cc", ent["cc"])
            cc_base = ent["cc"]
            v_pe, v_act = {}, {}
            E.wait("tensor", "ld", ent["ld"])
            for fc in range(D // VW):
                vpar = fc % 2
                vwbuf = vwbufs[vpar]
                vcs = "vc" if vpar == 0 else "vc1"
                if fc >= 2:
                    E.wait("gpsimd", "pe", v_pe[(fc - 2, T // 128 - 1)])
                else:
                    E.wait_snap("gpsimd", ent, ("pe", "fwb0", "fwb1", "dwb0", "dwb1",
                                                "wl0", "wl1", "dl0", "dl1"))
                wev = wcast(vwbuf, wv_h[ly], fc * VSLAB, DT, VW, vcs)
                E.wait("tensor", vcs, wev)
                for tb in range(T // 128):
                    par = tb % 2
                    psv = ps[2 + par]
                    if tb >= 2:
                        E.wait("tensor", "act", v_act[(fc, tb - 2)])
                    elif fc > 0:
                        E.wait("tensor", "act", v_act[(fc - 1, T // 128 - 2 + tb)])
                    else:
                        E.wait_snap("tensor", ent, ("dve", "act"))
                    def fmm(h, tb=tb, fc=fc, psv=psv, vwbuf=vwbuf):
                        for dt in range(DT):
                            ins = h.matmul(psv[:, 0:VW],
                                           xn_sb[:, dt * T + tb * 128: dt * T + (tb + 1) * 128],
                                           vwbuf[:, dt * VW:(dt + 1) * VW],
                                           start=(dt == 0), stop=(dt == DT - 1))
                        ins.then_inc(sems["pe"], 1)
                    E.emit("tensor", fmm); v_pe[(fc, tb)] = E.bump("pe")
                    E.wait("scalar", "pe", v_pe[(fc, tb)])
                    E.wait("scalar", f"vs{par}", E.ctr[f"vs{par}"])
                    E.wait("scalar", "os0", E.ctr["os0"])
                    def fev(h, par=par, psv=psv):
                        h.copy(out=vst[par][:, :], in_=psv[:, 0:VW]).then_inc(sems["act"], 1)
                    E.emit("scalar", fev); v_act[(fc, tb)] = E.bump("act")
                    E.wait("sync", "act", v_act[(fc, tb)])
                    def fst(h, fc=fc, tb=tb, par=par):
                        voff = 2 * D * T + (tb * 128) * D + fc * VW
                        h.dma_start(out=bass.AP(qkv_local, voff, [[D, 128], [1, VW]]),
                                    in_=vst[par][:, :]).then_inc(sems[f"vs{par}"], 16)
                    E.emit("sync", fst); E.bump(f"vs{par}", 16)
            vs_snap = (E.ctr["vs0"], E.ctr["vs1"])
            def fccv(h):
                h.collective_compute("AllGather", ALU.bypass,
                                     replica_groups=[[0, 1, 2, 3], [4, 5, 6, 7]],
                                     ins=[qkv_local[2 * D:3 * D, :].opt()],
                                     outs=[bass.AP(qkv_gath, VBASE, [[1, 4 * D * T]])]
                                     ).then_inc(sems["cc"])
            pe_evt, act_evt = {}, {}
            fb_order = list(range(DT, 2 * DT)) + list(range(DT))
            pending_cc = []
            for it, fb in enumerate(fb_order):
                par = it % 2
                wsem = f"wc{par}"
                if it >= 2:
                    E.wait("gpsimd", "pe", pe_evt[it - 2])
                else:
                    E.wait_snap("gpsimd", ent, ("pe", "fwb0", "fwb1", "wl0", "wl1"))
                if it == 2:
                    E.wait("gpsimd", "vs0", vs_snap[0])
                    E.wait("gpsimd", "vs1", vs_snap[1])
                    for s in ("hl", "vla", "vlb"):
                        E.wait("gpsimd", s, E.ctr[s])
                    E.emit("gpsimd", fccv); E.bump("cc")
                while pending_cc and pending_cc[0][0] <= it:
                    _, fcc_fn, q0, q1 = pending_cc.pop(0)
                    E.wait("gpsimd", "qs0", q0)
                    E.wait("gpsimd", "qs1", q1)
                    E.emit("gpsimd", fcc_fn); E.bump("cc")
                wev = wcast(wbuf[par], wqk_h[ly], fb * SLAB, DT, 128, wsem)
                E.wait("tensor", wsem, wev)
                E.wait("tensor", "dve", xn_dve)
                if it >= 2:
                    E.wait("tensor", "act", act_evt[it - 2])
                def fmm(h, par=par):
                    for dt in range(DT):
                        for hf in range(TH):
                            ins = h.matmul(ps[par][:, hf * 512:hf * 512 + 512],
                                           wbuf[par][:, dt * 128:(dt + 1) * 128],
                                           xn_sb[:, dt * T + hf * 512: dt * T + hf * 512 + 512],
                                           start=(dt == 0), stop=(dt == DT - 1))
                    ins.then_inc(sems["pe"], 1)
                E.emit("tensor", fmm); pe_evt[it] = E.bump("pe")
                E.wait("scalar", "pe", pe_evt[it])
                E.wait("scalar", f"qs{par}", E.ctr[f"qs{par}"])
                E.wait_snap("scalar", ent, ("dve",))
                def fev(h, fb=fb, par=par):
                    h.add(out=qstage[par][:, 0:T], in_=ps[par][:, 0:T],
                          add=bqk_sb[ly][:, fb:fb + 1]).then_inc(sems["act"], 1)
                E.emit("scalar", fev); act_evt[it] = E.bump("act")
                E.wait("sync", "act", act_evt[it])
                def fst(h, fb=fb, par=par):
                    h.dma_start(out=bass.AP(qkv_local, fb * 128 * T, [[T, 128], [1, T]]),
                                in_=qstage[par][:, 0:T]).then_inc(sems[f"qs{par}"], 16)
                E.emit("sync", fst); E.bump(f"qs{par}", 16)
                if it == DT - 1:
                    def fcck(h):
                        h.collective_compute("AllGather", ALU.bypass,
                                             replica_groups=[[0, 1, 2, 3], [4, 5, 6, 7]],
                                             ins=[qkv_local[D:2 * D, :].opt()],
                                             outs=[qkv_gath[0:4 * D, :].opt()]
                                             ).then_inc(sems["cc"])
                    pending_cc.append((it + 4, fcck, E.ctr["qs0"], E.ctr["qs1"]))
                if it == 2 * DT - 1:
                    def fccq(h):
                        h.collective_compute("AllGather", ALU.bypass,
                                             replica_groups=[[0, 1, 2, 3], [4, 5, 6, 7]],
                                             ins=[qkv_local[0:D, :].opt()],
                                             outs=[qkv_gath[4 * D:8 * D, :].opt()]
                                             ).then_inc(sems["cc"])
                    pending_cc.append((it + 4, fccq, E.ctr["qs0"], E.ctr["qs1"]))
            for _, fcc_fn, q0, q1 in pending_cc:
                E.wait("gpsimd", "qs0", q0)
                E.wait("gpsimd", "qs1", q1)
                E.emit("gpsimd", fcc_fn); E.bump("cc")
            pending_cc = []
            cc_all = E.ctr["cc"]
            E.wait("sync", "cc", cc_all)
            E.wait("scalar" if ly == 0 else "gpsimd", "cc", cc_all)
            head_done_pe = E.ctr["pe"]
            a4ent = E.snap()
            SBANKS = [ps[0][:, 0:512], ps[0][:, 512:1024],
                      ps[1][:, 0:512], ps[1][:, 512:1024],
                      ps[2][:, 0:512], ps[2][:, 512:1024]]
            SPAIRS = [ps[0], ps[1], ps[2]]
            first_scores = [True]
            fr3_guard = [0]
            last_bank_exp = [a4ent["act"]] * 6
            av_ring = []
            ring_ctr = [0]
            veng = "scalar" if ly == 0 else "gpsimd"
            vtag = "act" if ly == 0 else "pl"
            vsem = "vla" if ly == 0 else "vlb"
            pend_o = []
            qk_mult = CH if HPC * 128 == 512 else HPC * 128 * T

            def head_load(hp, guard_pe):
                hpar = hp % 2
                hls = "hl" if hpar == 0 else "hl1"
                vls = vsem if hpar == 0 else vsem + "1"
                E.wait("sync", "pe", guard_pe)
                E.wait(veng, "pe", guard_pe)
                def fld(h, hp=hp, hpar=hpar, hls=hls):
                    ro = RT["roff"](h, "sp", HPC * 128 * T, hp * 128 * T + 4 * D * T)
                    h.dma_start(out=Q_views[hpar][:, :].rearrange("p (r t) -> p r t", r=4),
                                in_=bass.AP(qkv_gath, ro,
                                            [[T, 128], [D * T, 4], [1, T]])).then_inc(sems[hls], 16)
                    ro = RT["roff"](h, "sp", HPC * 128 * T, hp * 128 * T)
                    h.dma_start(out=K_views[hpar][:, :].rearrange("p (r t) -> p r t", r=4),
                                in_=bass.AP(qkv_gath, ro,
                                            [[T, 128], [D * T, 4], [1, T]])).then_inc(sems[hls], 16)
                E.emit("sync", fld)
                qk_ev = E.bump(hls, 16 * 2)
                return (hls, qk_ev)

            done_pe = {}
            loads = {0: head_load(0, ent["pe"])}
            for hp in range(HPC):
                if pend_o:
                    fcc_fn, o0, o1 = pend_o.pop(0)
                    E.wait("gpsimd", "os0", o0)
                    E.wait("gpsimd", "os1", o1)
                    E.emit("gpsimd", fcc_fn); E.bump("cc")
                if hp + 1 < HPC:
                    loads[hp + 1] = head_load(hp + 1, done_pe.get(hp - 1, ent["pe"]))
                E.wait(veng, "pe", done_pe.get(hp - 1, ent["pe"]))
                def fldv(h, hp=hp):
                    for r in range(4):
                        ro = RT["roff"](h, vtag, HPC * 128, hp * 128 + VBASE + r * D * T)
                        h.dma_start(out=V_view[:, r * T:(r + 1) * T].rearrange(
                                        "p (b f) -> p b f", b=T // 128),
                                    in_=bass.AP(qkv_gath, ro,
                                                [[D, 128], [128 * D, T // 128], [1, 128]])
                                    ).then_inc(sems[vsem], 16)
                E.emit(veng, fldv)
                v_ev = E.bump(vsem, 16 * 4)
                hls, qk_ev = loads[hp]
                E.wait("tensor", hls, qk_ev)
                hpar = hp % 2
                Q_view, K_view = Q_views[hpar], K_views[hpar]
                for qt in range(QT):
                    nkb = 4 * (qt + 1)
                    bq = (hp * QT + qt) % 2
                    exp_evt, zz_evt, mm_evt = {}, {}, {}
                    bias_act = [0]
                    av_last = 0
                    for g in range(nkb + 6):
                        e = g - 6
                        if 0 <= e < nkb:
                            rs = ring_ctr[0] % 4
                            ring_ctr[0] += 1
                            diag = e >= 4 * qt
                            E.wait("scalar", "pe", mm_evt[e])
                            E.wait("scalar", "act", bias_act[0])
                            if len(av_ring) >= 4:
                                E.wait("scalar", "pe", av_ring[-4])
                            def fex(h, e=e, rs=rs, bq=bq):
                                h.activation(out=pring[:, rs * 512:(rs + 1) * 512],
                                             in_=SBANKS[e % 6], func=AF.Exp,
                                             bias=bias_sb[:, bq:bq + 1],
                                             scale=C_SM).then_inc(sems["act"], 1)
                            E.emit("scalar", fex)
                            ev = E.bump("act")
                            exp_evt[e] = (ev, rs)
                            last_bank_exp[e % 6] = ev
                            if diag:
                                pat = e - 4 * qt
                                E.wait("vector", "act", ev)
                                E.wait("vector", "msk", 16)
                                def fzz(h, rs=rs, pat=pat):
                                    h.tensor_mul(out=pring[:, rs * 512:(rs + 1) * 512],
                                                 in0=pring[:, rs * 512:(rs + 1) * 512],
                                                 in1=mask01_sb[:, pat * 512:(pat + 1) * 512]
                                                 ).then_inc(sems["dve"], 1)
                                E.emit("vector", fzz)
                                zz_evt[e] = E.bump("dve")
                        if g < nkb:
                            E.wait("tensor", "act", last_bank_exp[g % 6])
                            if first_scores[0]:
                                E.wait("tensor", "dve", xn_dve)
                                E.wait_snap("tensor", a4ent, ("dve", "act"))
                                first_scores[0] = False
                            def fsc(h, kb=g, qt=qt, si=g % 6, KV=K_view, QV=Q_view):
                                h.matmul(SBANKS[si], KV[:, kb * 128:(kb + 1) * 128],
                                         QV[:, qt * 512:(qt + 1) * 512],
                                         start=True, stop=True).then_inc(sems["pe"], 1)
                            E.emit("tensor", fsc)
                            mm_evt[g] = E.bump("pe")
                            if g == 1:
                                E.wait("vector", "pe", mm_evt[1])
                                def fpr(h, w=E.ctr["dve"]):
                                    h.wait_ge(sems["dve"], w)
                                    h.tensor_reduce(out=Mp_sb[:, 0:1], in_=ps[0][:, 0:1024],
                                                    axis=AX.X, op=ALU.max).then_inc(sems["dve"], 1)
                                E.emit("vector", fpr); pr_evt = E.bump("dve")
                            if g == 2:
                                E.wait("tensor", "dve", pr_evt)
                                if fr3_guard[0]:
                                    E.wait("tensor", "dve", fr3_guard[0])
                                E.wait("tensor", "ld", ent["ld"])
                                def ftr(h):
                                    h.transpose(ps[3][0:1, 512:640], Mp_sb[:, 0:1],
                                                ident_sb[:, :]).then_inc(sems["pe"], 1)
                                E.emit("tensor", ftr); tr_evt = E.bump("pe")
                                E.wait("vector", "pe", tr_evt)
                                def fmu(h):
                                    h.tensor_reduce(out=mu_sb[0:1, 0:1],
                                                    in_=ps[3][0:1, 512:640], axis=AX.X,
                                                    op=ALU.max).then_inc(sems["dve"], 1)
                                E.emit("vector", fmu); E.bump("dve")
                                E.wait("tensor", "dve", E.ctr["dve"])
                                def fb4(h):
                                    h.matmul(ps[3][:, 640:641], ones_r_f[:, :],
                                             mu_sb[0:1, 0:1], start=True,
                                             stop=True).then_inc(sems["pe"], 1)
                                E.emit("tensor", fb4); b4 = E.bump("pe")
                                E.wait("scalar", "pe", b4)
                                def fbi(h, bq=bq):
                                    h.activation(out=bias_sb[:, bq:bq + 1],
                                                 in_=ps[3][:, 640:641], func=AF.Copy,
                                                 scale=-C_SM).then_inc(sems["act"], 1)
                                E.emit("scalar", fbi); bias_act[0] = E.bump("act")
                        if 0 <= e < nkb:
                            ev, rs = exp_evt[e]
                            E.wait("tensor", "act", ev)
                            if e in zz_evt:
                                E.wait("tensor", "dve", zz_evt[e])
                            if e == 0 and fr3_guard[0]:
                                E.wait("tensor", "dve", fr3_guard[0])
                            if e == 0 and qt == 0:
                                E.wait("tensor", vsem, v_ev)
                            def fav(h, kb=e, rs=rs, first=(e == 0), last=(e == nkb - 1),
                                    VV=V_view):
                                h.matmul(ps[3][:, 0:512], VV[:, kb * 128:(kb + 1) * 128],
                                         pring[:, rs * 512:(rs + 1) * 512],
                                         start=first, stop=last)
                                h.matmul(ps[3][0:1, 512:1024], ones_c_bf[:, :],
                                         pring[:, rs * 512:(rs + 1) * 512],
                                         start=first, stop=last).then_inc(sems["pe"], 1)
                            E.emit("tensor", fav)
                            av_ring.append(E.bump("pe"))
                            av_last = av_ring[-1]
                    E.wait("vector", "pe", av_last)
                    def fr1(h, w=E.ctr["dve"]):
                        h.wait_ge(sems["dve"], w)
                        h.reciprocal(out=rec_sb[:, :],
                                     in_=ps[3][0:1, 512:1024]).then_inc(sems["dve"], 1)
                    E.emit("vector", fr1); E.bump("dve")
                    E.wait("tensor", "dve", E.ctr["dve"])
                    def fr2(h):
                        h.matmul(ps[2][:, 0:512], ones_r_f[:, :], rec_sb[:, :],
                                 start=True, stop=True).then_inc(sems["pe"], 1)
                    E.emit("tensor", fr2); r2 = E.bump("pe")
                    E.wait("scalar", "pe", r2)
                    E.wait("scalar", "dve", fr3_guard[0])
                    def frc(h):
                        h.copy(out=recb_sb[:, :], in_=ps[2][:, 0:512]).then_inc(sems["act"], 1)
                    E.emit("scalar", frc)
                    last_bank_exp[4] = E.bump("act")
                    ix = hp * QT + qt
                    opar2 = ix % 2
                    E.wait("vector", "act", last_bank_exp[4])
                    E.wait("vector", f"os{opar2}", E.ctr[f"os{opar2}"])
                    E.wait("vector", "vs0", E.ctr["vs0"])
                    E.wait("vector", "vs1", E.ctr["vs1"])
                    def fr3(h, opar2=opar2):
                        h.tensor_mul(out=ost[opar2][:, :], in0=ps[3][:, 0:512],
                                     in1=recb_sb[:, :]).then_inc(sems["dve"], 1)
                    E.emit("vector", fr3); E.bump("dve")
                    fr3_guard[0] = E.ctr["dve"]
                    E.wait("sync", "dve", E.ctr["dve"])
                    def fr4(h, hp=hp, qt=qt, opar2=opar2):
                        h.dma_start(out=bass.AP(o_local, hp * 128 * S + qt * 512, [[S, 128], [1, 512]]),
                                    in_=ost[opar2][:, :]).then_inc(sems[f"os{opar2}"], 16)
                    E.emit("sync", fr4); E.bump(f"os{opar2}", 16)
                done_pe[hp] = E.ctr["pe"]
                E.wait("gpsimd", "ota", ent["ota"])
                E.wait("gpsimd", "otb", ent["otb"])
                def fcc2(h, hp=hp):
                    h.collective_compute("AllGather", ALU.bypass,
                                         replica_groups=[[0, 1, 2, 3], [4, 5, 6, 7]],
                                         ins=[o_local[hp * 128:(hp + 1) * 128, :].opt()],
                                         outs=[o_gath[hp * 4 * 128:(hp + 1) * 4 * 128, :].opt()]
                                         ).then_inc(sems["cc"])
                pend_o.append((fcc2, E.ctr["os0"], E.ctr["os1"]))
            for fcc_fn, o0, o1 in pend_o:
                E.wait("gpsimd", "os0", o0)
                E.wait("gpsimd", "os1", o1)
                E.emit("gpsimd", fcc_fn); E.bump("cc")
            cc_o = E.ctr["cc"]
            osem = "ota" if ly == 0 else "otb"
            E.wait(veng, "cc", cc_o)
            E.wait(veng, "pe", E.ctr["pe"])
            def fot(h):
                for r in range(4):
                    ro = RT["roff"](h, vtag, T, r * 128 * S)
                    h.dma_start(out=xn_sb[:, (r * HPC) * T:(r * HPC + HPC) * T].rearrange(
                                    "p (hh n) -> p hh n", hh=HPC),
                                in_=bass.AP(o_gath, ro,
                                            [[S, 128], [4 * 128 * S, HPC], [1, T]])).then_inc(sems[osem], 16)
            E.emit(veng, fot)
            ot_ld = E.bump(osem, 16 * 4)
            E.wait("vector", osem, ot_ld)
            E.wait("vector", "ld", ent["ld"])
            for hh in range(NH):
                def fbv(h, hh=hh, ly=ly):
                    h.tensor_scalar_add(out=xn_sb[:, hh * T:(hh + 1) * T],
                                        in0=xn_sb[:, hh * T:(hh + 1) * T],
                                        scalar1=bvh_sb[ly][:, hh:hh + 1]).then_inc(sems["dve"], 1)
                E.emit("vector", fbv); E.bump("dve")
            ot_dve = E.ctr["dve"]
            op_pe, op_dve = {}, {}
            for db in range(DT):
                par = db % 2
                wsem = f"wc{par}"
                if db >= 2:
                    E.wait("gpsimd", "pe", op_pe[db - 2])
                else:
                    E.wait("gpsimd", "pe", pe_evt[2 * DT - 2 + db])
                wev = wcast(wbuf[par], wo_h[ly], db * SLAB, DT, 128, wsem)
                E.wait("tensor", wsem, wev)
                E.wait("tensor", osem, ot_ld)
                E.wait("tensor", "dve", ot_dve)
                if db >= 2:
                    E.wait("tensor", "dve", op_dve[db - 2])
                def fmm(h, par=par):
                    for ht in range(NH):
                        for hf in range(TH):
                            ins = h.matmul(ps[par][:, hf * 512:hf * 512 + 512],
                                           wbuf[par][:, ht * 128:(ht + 1) * 128],
                                           xn_sb[:, ht * T + hf * 512: ht * T + hf * 512 + 512],
                                           start=(ht == 0), stop=(ht == NH - 1))
                    ins.then_inc(sems["pe"], 1)
                E.emit("tensor", fmm); op_pe[db] = E.bump("pe")
                E.wait("vector", "pe", op_pe[db])
                def fres(h, db=db, par=par):
                    h.scalar_tensor_tensor(out=x_sb[:, db * T:(db + 1) * T], in0=ps[par][:, 0:T],
                                           scalar=bo_sb[ly][:, db:db + 1],
                                           in1=x_sb[:, db * T:(db + 1) * T],
                                           op0=ALU.add, op1=ALU.add).then_inc(sems["dve"], 1)
                E.emit("vector", fres); op_dve[db] = E.bump("dve")
            return op_dve

        def p_ffn(second, xn_dve):
            ent = E.snap()
            g_pe, g_act, g_dve = {}, {}, {}
            d_pe_g, d_dve_g = {}, {}
            down_guard = 0
            for hq in range(8):
                for fb in range(FQ):
                    gb = hq * FQ + fb
                    par = fb % 2
                    weng = "gpsimd" if not second else "sync"
                    if gb >= 2:
                        E.wait(weng, "pe", g_pe[(gb - 2, "u")])
                    else:
                        E.wait_snap(weng, ent, ("pe", "fwb0", "fwb1"))
                    if not second:
                        E.wait("gpsimd", f"fwb{par}", E.ctr[f"fwb{par}"])
                        cg = wcast(wbuf[par], wg_h, gb * SLAB, DT, 128, f"wc{par}")
                        cu = wcast(uwbuf[par], wu_h, gb * SLAB, DT, 128, f"uc{par}")
                        E.wait("sync", f"wc{par}", cg)
                        E.wait("sync", f"uc{par}", cu)
                        def fwb(h, gb=gb, par=par):
                            h.dma_start(out=bass.AP(wg_bf, gb * SLAB, [[SLAB // 128, 128], [1, SLAB // 128]]),
                                        in_=wbuf[par][:, :]).then_inc(sems[f"fwb{par}"], 16)
                            h.dma_start(out=bass.AP(wu_bf, gb * SLAB, [[SLAB // 128, 128], [1, SLAB // 128]]),
                                        in_=uwbuf[par][:, :]).then_inc(sems[f"fwb{par}"], 16)
                        E.emit("sync", fwb); E.bump(f"fwb{par}", 32)
                        E.wait("tensor", f"wc{par}", cg)
                        E.wait("tensor", f"uc{par}", cu)
                    else:
                        def fw(h, gb=gb, par=par):
                            h.dma_start(out=wbuf[par][:, :].rearrange("p (t f) -> p t f", t=DT),
                                        in_=bass.AP(wg_bf, gb * SLAB,
                                                    [[SLAB // 128, 128], [128, DT], [1, 128]])
                                        ).then_inc(sems[f"wl{par}"], 16)
                            h.dma_start(out=uwbuf[par][:, :].rearrange("p (t f) -> p t f", t=DT),
                                        in_=bass.AP(wu_bf, gb * SLAB,
                                                    [[SLAB // 128, 128], [128, DT], [1, 128]])
                                        ).then_inc(sems[f"wl{par}"], 16)
                        E.emit("sync", fw)
                        wlev = E.bump(f"wl{par}", 32)
                        E.wait("tensor", f"wl{par}", wlev)
                    E.wait("tensor", "dve", xn_dve)
                    E.wait("tensor", "dve", down_guard)
                    if gb >= 2:
                        E.wait("tensor", "act", g_act[gb - 2])
                        E.wait("tensor", "dve", g_dve[gb - 2])
                    def fg(h, par=par):
                        for dt in range(DT):
                            for hf in range(TH):
                                ins = h.matmul(ps[par][:, hf * 512:hf * 512 + 512],
                                               wbuf[par][:, dt * 128:(dt + 1) * 128],
                                               xn_sb[:, dt * T + hf * 512: dt * T + hf * 512 + 512],
                                               start=(dt == 0), stop=(dt == DT - 1))
                        ins.then_inc(sems["pe"], 1)
                    E.emit("tensor", fg); g_pe[(gb, "g")] = E.bump("pe")
                    def fu(h, par=par):
                        for dt in range(DT):
                            for hf in range(TH):
                                ins = h.matmul(ps[2 + par][:, hf * 512:hf * 512 + 512],
                                               uwbuf[par][:, dt * 128:(dt + 1) * 128],
                                               xn_sb[:, dt * T + hf * 512: dt * T + hf * 512 + 512],
                                               start=(dt == 0), stop=(dt == DT - 1))
                        ins.then_inc(sems["pe"], 1)
                    E.emit("tensor", fu); g_pe[(gb, "u")] = E.bump("pe")
                    E.wait("scalar", "pe", g_pe[(gb, "g")])
                    if gb >= 2:
                        E.wait("scalar", "dve", g_dve[gb - 2])
                    else:
                        E.wait_snap("scalar", ent, ("dve", "qs0", "qs1"))
                    def fsi(h, par=par):
                        h.activation(out=silustage[par][:, 0:T], in_=ps[par][:, 0:T],
                                     func=AF.Silu).then_inc(sems["act"], 1)
                    E.emit("scalar", fsi); g_act[gb] = E.bump("act")
                    E.wait("vector", "pe", g_pe[(gb, "u")])
                    E.wait("vector", "act", g_act[gb])
                    def fgm(h, fb=fb, par=par):
                        h.tensor_mul(out=gated_view[:, fb * T:(fb + 1) * T], in0=ps[2 + par][:, 0:T],
                                     in1=silustage[par][:, 0:T]).then_inc(sems["dve"], 1)
                    E.emit("vector", fgm); g_dve[gb] = E.bump("dve")
                gate_dve = E.ctr["dve"]
                for db in range(DT):
                    par = db % 2
                    sl = hq * DT + db
                    weng = "gpsimd" if not second else "sync"
                    if sl >= 2:
                        E.wait(weng, "pe", d_pe_g[sl - 2])
                    else:
                        E.wait_snap(weng, ent, ("pe", "dwb0", "dwb1"))
                    if not second:
                        E.wait("gpsimd", f"dwb{par}", E.ctr[f"dwb{par}"])
                        cd = wcast(dwbuf[par], wd_h, sl * DSLAB, FQ, 128, f"dc{par}")
                        E.wait("sync", f"dc{par}", cd)
                        def fwb(h, sl=sl, par=par):
                            h.dma_start(out=bass.AP(wd_bf, sl * DSLAB,
                                                    [[DSLAB // 128, 128], [1, DSLAB // 128]]),
                                        in_=dwbuf[par][:, :]).then_inc(sems[f"dwb{par}"], 16)
                        E.emit("sync", fwb); E.bump(f"dwb{par}", 16)
                        E.wait("tensor", f"dc{par}", cd)
                    else:
                        def fw(h, sl=sl, par=par):
                            h.dma_start(out=dwbuf[par][:, :].rearrange("p (t f) -> p t f", t=FQ),
                                        in_=bass.AP(wd_bf, sl * DSLAB,
                                                    [[DSLAB // 128, 128], [128, FQ], [1, 128]])
                                        ).then_inc(sems[f"dl{par}"], 16)
                        E.emit("sync", fw)
                        dlev = E.bump(f"dl{par}", 16)
                        E.wait("tensor", f"dl{par}", dlev)
                    E.wait("tensor", "dve", gate_dve)
                    if sl >= 2:
                        E.wait("tensor", "dve", d_dve_g.get(sl - 2, 0))
                    def fdm(h, par=par):
                        for hb in range(FQ):
                            for hf in range(TH):
                                ins = h.matmul(ps[par][:, hf * 512:hf * 512 + 512],
                                               dwbuf[par][:, hb * 128:(hb + 1) * 128],
                                               gated_view[:, hb * T + hf * 512: hb * T + hf * 512 + 512],
                                               start=(hb == 0), stop=(hb == FQ - 1))
                        ins.then_inc(sems["pe"], 1)
                    E.emit("tensor", fdm); d_pe_g[sl] = E.bump("pe")
                    E.wait("vector", "pe", d_pe_g[sl])
                    def fres(h, db=db, par=par):
                        h.scalar_tensor_tensor(out=x_sb[:, db * T:(db + 1) * T], in0=ps[par][:, 0:T],
                                               scalar=0.0, in1=x_sb[:, db * T:(db + 1) * T],
                                               op0=ALU.add, op1=ALU.add).then_inc(sems["dve"], 1)
                    E.emit("vector", fres); d_dve_g[sl] = E.bump("dve")
                down_guard = E.ctr["dve"]
            return {db: d_dve_g[7 * DT + db] for db in range(DT)}

        fold_guard = [0, 0]
        last_ln_pe = [0]

        p_init()
        xn1 = p_ln()
        prevA = p_attn(0, xn1)
        xn2 = p_ln(prevA)
        prevF = p_ffn(False, xn2)
        xn3 = p_ln(prevF)
        prevB = p_attn(1, xn3)
        xn4 = p_ln(prevB)
        p_ffn(True, xn4)

        E.wait("sync", "dve", E.ctr["dve"])
        def ffin(h):
            h.dma_start(out=yT_h.ap().rearrange("(t p) n -> p t n", p=128),
                        in_=x_sb[:, :].rearrange("p (t n) -> p t n", t=DT)).then_inc(sems["out"], 16)
        E.emit("sync", ffin)
        E.bump("out", 16)
        E.wait("sync", "out", 16)

        with nc.Block() as block:
            @block.sync
            def _(h):
                for f in E.q["sync"]:
                    f(h)

            @block.tensor
            def _(h):
                for f in E.q["tensor"]:
                    f(h)

            @block.scalar
            def _(h):
                for f in E.q["scalar"]:
                    f(h)

            @block.vector
            def _(h):
                for f in E.q["vector"]:
                    f(h)

            @block.gpsimd
            def _(h):
                for f in E.q["gpsimd"]:
                    f(h)

    return nc



def make_in_maps(cfg, inputs):
    shared = host_inputs(cfg, inputs)
    x = np.asarray(inputs["L0_attn_a_x"], dtype=np.float32)
    in_maps = []
    for c in range(8):
        g, r = c // 4, c % 4
        xs = np.ascontiguousarray(x[g, r * cfg.T:(r + 1) * cfg.T, :].T)
        m = {"xT": xs}
        m.update(shared)
        in_maps.append(m)
    return in_maps


def assemble_out(cfg, results):
    out = np.empty((cfg.B, cfg.S, cfg.D), dtype=np.float32)
    for c in range(8):
        g, r = c // 4, c % 4
        out[g, r * cfg.T:(r + 1) * cfg.T, :] = results[c]["yT"].T
    return out


def kernel(**inputs):
    cfg = CFG
    nc = build_program(cfg)
    in_maps = make_in_maps(cfg, inputs)
    res = run_bass_kernel_spmd(nc, in_maps, core_ids=list(range(8)))
    return assemble_out(cfg, res.results)


# version 18 (final)
# speedup vs baseline: 1.0241x; 1.0032x over previous
import numpy as np
from contextlib import ExitStack

import concourse.bass as bass
import concourse.mybir as mybir
from concourse.bass_utils import run_bass_kernel_spmd
from concourse.aot_env import allow_ici_all_to_all

F32 = mybir.dt.float32
BF16 = mybir.dt.bfloat16
AF = mybir.ActivationFunctionType
ALU = mybir.AluOpType
AX = mybir.AxisListType

NEG = -1e30
C_SM = float(1.0 / np.sqrt(128.0))


class Cfg:
    def __init__(self, D=2048, FF=8192, NH=16, S=4096, B=2):
        self.D, self.FF, self.NH, self.S, self.B = D, FF, NH, S, B
        self.DT = D // 128
        self.T = S // 4
        self.HPC = NH // 4
        self.QT = S // 512
        self.KB = S // 128
        self.FQ = FF // 8 // 128
        self.TH = self.T // 512
        self.VW = min(256, (self.NH // 4) * 128)


CFG = Cfg()



def _pack_lhsT(w_lhsT, nblk):
    K, M = w_lhsT.shape
    dt = K // 128
    a = w_lhsT.reshape(dt, 128, nblk, 128).transpose(2, 1, 0, 3)
    return np.ascontiguousarray(a.reshape(nblk, dt * 128 * 128))


def _pack_rhs(w, nchunk, width):
    K, M = w.shape
    dt = K // 128
    a = w.reshape(dt, 128, nchunk, width).transpose(2, 1, 0, 3)
    return np.ascontiguousarray(a.reshape(nchunk, dt * 128 * width))


def _pack_down(wd_T, ff, d):
    fq = ff // 8 // 128
    dbn = d // 128
    a = wd_T.reshape(8, fq, 128, dbn, 128).transpose(0, 3, 2, 1, 4)
    return np.ascontiguousarray(a.reshape(8 * dbn, 128 * fq * 128))


def _make_masks01():
    m = np.zeros((128, 4, 512), dtype=np.float32)
    p = np.arange(128)[:, None]
    qf = np.arange(512)[None, :]
    for s in range(4):
        m[:, s, :] = np.where(s * 128 + p <= qf, 1.0, 0.0)
    return np.ascontiguousarray(m.reshape(128, 4 * 512))


def host_inputs(cfg, inputs):
    f = np.float32
    d = {}
    pairs = (("A", np.ascontiguousarray(np.asarray(inputs["L0_attn_a_W_qkv"], f).T),
              np.asarray(inputs["L0_attn_a_b_qkv"], f)),
             ("B", np.ascontiguousarray(np.asarray(inputs["L0_attn_b_w_qkv"], f)),
              np.asarray(inputs["L0_attn_b_b_qkv"], f)))
    for tag, w, b in pairs:
        d[f"wqk{tag}"] = _pack_lhsT(w[:, : 2 * cfg.D], 2 * cfg.DT)
        d[f"wv{tag}"] = _pack_rhs(w[:, 2 * cfg.D:], cfg.D // cfg.VW, cfg.VW)
        d[f"bqk{tag}"] = np.ascontiguousarray(b[: 2 * cfg.D].reshape(2 * cfg.DT, 128).T)
        d[f"bv{tag}"] = np.ascontiguousarray(b[2 * cfg.D:].reshape(cfg.NH, 128).T)
    d["woA"] = _pack_lhsT(np.ascontiguousarray(np.asarray(inputs["L0_attn_a_W_out"], f).T), cfg.DT)
    d["woB"] = _pack_lhsT(np.ascontiguousarray(np.asarray(inputs["L0_attn_b_w_proj"], f)), cfg.DT)
    d["boA"] = np.ascontiguousarray(np.asarray(inputs["L0_attn_a_b_out"], f).reshape(cfg.DT, 128).T)
    d["boB"] = np.ascontiguousarray(np.asarray(inputs["L0_attn_b_b_proj"], f).reshape(cfg.DT, 128).T)
    d["wg"] = _pack_lhsT(np.ascontiguousarray(np.asarray(inputs["L0_ffn_w_gate"], f).T), cfg.FF // 128)
    d["wu"] = _pack_lhsT(np.ascontiguousarray(np.asarray(inputs["L0_ffn_w_up"], f).T), cfg.FF // 128)
    d["wd"] = _pack_down(np.ascontiguousarray(np.asarray(inputs["L0_ffn_w_down"], f).T), cfg.FF, cfg.D)
    d["masks"] = _make_masks01()
    d["identity"] = np.eye(128, dtype=f)
    return d



SEM_NAMES = ("pe", "act", "dve", "gp", "cc", "ld", "msk",
             "qs0", "qs1", "vs0", "vs1", "os0", "os1",
             "wc0", "wc1", "uc0", "uc1", "dc0", "dc1", "vc", "vc1",
             "wl0", "wl1", "dl0", "dl1",
             "fwb0", "fwb1", "dwb0", "dwb1",
             "hl", "hl1", "vla", "vla1", "vlb", "vlb1", "ota", "otb", "out")


class Emitter:
    ENGS = ("sync", "tensor", "scalar", "vector", "gpsimd")

    def __init__(self):
        self.q = {e: [] for e in self.ENGS}
        self.ctr = {k: 0 for k in SEM_NAMES}
        self.lastwait = {e: {} for e in self.ENGS}
        self.sems = {}

    def emit(self, eng, fn):
        self.q[eng].append(fn)

    def wait(self, eng, sem, val):
        if val is None or val <= 0:
            return
        lw = self.lastwait[eng]
        if lw.get(sem, 0) >= val:
            return
        lw[sem] = val
        s = self.sems[sem]

        def f(h, s=s, v=val):
            h.wait_ge(s, v)
        self.q[eng].append(f)

    def snap(self):
        return dict(self.ctr)

    def wait_snap(self, eng, snap, names):
        for n in names:
            self.wait(eng, n, snap[n])

    def bump(self, name, by=1):
        self.ctr[name] += by
        return self.ctr[name]


def build_program(cfg):
    nc = bass.Bass("TRN2", target_bir_lowering=False, debug=False)
    E = Emitter()
    DT, T, QT, KB, HPC, FQ, TH = cfg.DT, cfg.T, cfg.QT, cfg.KB, cfg.HPC, cfg.FQ, cfg.TH
    VW = cfg.VW
    D, FF, NH, S = cfg.D, cfg.FF, cfg.NH, cfg.S
    SLAB = DT * 128 * 128
    VSLAB = DT * 128 * VW
    DSLAB = 128 * FQ * 128

    xT_h = nc.dram_tensor("xT", [D, T], F32, kind="ExternalInput")
    yT_h = nc.dram_tensor("yT", [D, T], F32, kind="ExternalOutput")
    wqk_h = {0: nc.dram_tensor("wqkA", [2 * DT, SLAB], F32, kind="ExternalInput"),
             1: nc.dram_tensor("wqkB", [2 * DT, SLAB], F32, kind="ExternalInput")}
    wv_h = {0: nc.dram_tensor("wvA", [D // VW, VSLAB], F32, kind="ExternalInput"),
            1: nc.dram_tensor("wvB", [D // VW, VSLAB], F32, kind="ExternalInput")}
    bqk_h = {0: nc.dram_tensor("bqkA", [128, 2 * DT], F32, kind="ExternalInput"),
             1: nc.dram_tensor("bqkB", [128, 2 * DT], F32, kind="ExternalInput")}
    bv_h = {0: nc.dram_tensor("bvA", [128, NH], F32, kind="ExternalInput"),
            1: nc.dram_tensor("bvB", [128, NH], F32, kind="ExternalInput")}
    wo_h = {0: nc.dram_tensor("woA", [DT, SLAB], F32, kind="ExternalInput"),
            1: nc.dram_tensor("woB", [DT, SLAB], F32, kind="ExternalInput")}
    bo_h = {0: nc.dram_tensor("boA", [128, DT], F32, kind="ExternalInput"),
            1: nc.dram_tensor("boB", [128, DT], F32, kind="ExternalInput")}
    wg_h = nc.dram_tensor("wg", [FF // 128, SLAB], F32, kind="ExternalInput")
    wu_h = nc.dram_tensor("wu", [FF // 128, SLAB], F32, kind="ExternalInput")
    wd_h = nc.dram_tensor("wd", [8 * DT, DSLAB], F32, kind="ExternalInput")
    masks_h = nc.dram_tensor("masks", [128, 4 * 512], F32, kind="ExternalInput")
    ident_h = nc.dram_tensor("identity", [128, 128], F32, kind="ExternalInput")

    NCK = D // 512
    CH = 4 * 512 * T
    VBASE = 2 * NCK * CH
    qkv_local = nc.dram_tensor("qkv_local", [3 * D, T], BF16)
    qkv_gath = nc.dram_tensor("qkv_gath", [4 * 3 * D, T], BF16)
    o_local = nc.dram_tensor("o_local", [HPC * 128, S], BF16)
    o_gath = nc.dram_tensor("o_gath", [4 * HPC * 128, S], BF16)
    wg_bf = nc.dram_tensor("wg_bf", [FF // 128, SLAB], BF16)
    wu_bf = nc.dram_tensor("wu_bf", [FF // 128, SLAB], BF16)
    wd_bf = nc.dram_tensor("wd_bf", [8 * DT, DSLAB], BF16)

    es = ExitStack()
    with es:
        sbt = lambda name, shape, dt: es.enter_context(nc.sbuf_tensor(name, shape, dt))
        x_sb = sbt("x_sb", [128, DT * T], F32)
        xn_sb = sbt("xn_sb", [128, max(DT, NH) * T], BF16)
        sst_sb = sbt("sst_sb", [128, 6 * 512], F32)
        arena = sbt("arena", [128, max(5 * KB * 128, FQ * T)], BF16)
        WB = DT * 128
        wt = sbt("wt", [128, max(4 * WB + 2 * FQ * 128, 2 * WB + DT * VW) + DT * VW], BF16)
        stage = sbt("stage", [128, 2 * 1024], BF16)
        vstage = sbt("vstage", [128, 2 * 512], BF16)
        mask01_sb = sbt("mask01_sb", [128, 4 * 512], BF16)
        ident_sb = sbt("ident_sb", [128, 128], F32)
        bqk_sb = {ly: sbt(f"bqk{ly}_sb", [128, 2 * DT], F32) for ly in (0, 1)}
        bo_sb = {ly: sbt(f"bo{ly}_sb", [128, DT], F32) for ly in (0, 1)}
        ones_c_bf = sbt("ones_c_bf", [128, 1], BF16)
        ones_r_f = sbt("ones_r_f", [1, 128], F32)
        onescf = sbt("onescf", [128, 1], F32)
        Mc_sb = sbt("Mc_sb", [128, 1], F32)
        Mp_sb = sbt("Mp_sb", [128, 8], F32)
        mu_sb = sbt("mu_sb", [1, 3], F32)
        nmu_sb = sbt("nmu_sb", [1, 1], F32)
        f_sb = sbt("f_sb", [1, 2], F32)
        fbc_sb = sbt("fbc_sb", [128, 2], F32)
        bias_sb = sbt("bias_sb", [128, 2], F32)
        packA = sbt("packA", [128, 2 * T], F32)
        packD = sbt("packD", [128, 1536], F32)
        bvh_sb = {ly: sbt(f"bvh{ly}_sb", [128, NH], F32) for ly in (0, 1)}
        mrow_sb = packA[0:1, 0:T]
        rstd_sb = packA[0:1, T:2 * T]
        ex2_sb = packA[32:33, 0:T]
        msq_sb = packA[32:33, T:2 * T]
        var_sb = ex2_sb
        sd_sb = msq_sb
        rec_sb = packD[0:1, 0:512]
        recb_sb = packD[:, 512:1024]

        ps = [es.enter_context(nc.psum_tensor(f"ps{i}", [128, 1024], F32)) for i in range(4)]

        for name in SEM_NAMES:
            E.sems[name] = es.enter_context(nc.semaphore("sem_" + name))
        sems = E.sems

        QKV_EL = KB * 128
        Q_views = [arena[:, 0:QKV_EL], arena[:, QKV_EL:2 * QKV_EL]]
        K_views = [arena[:, 2 * QKV_EL:3 * QKV_EL], arena[:, 3 * QKV_EL:4 * QKV_EL]]
        V_view = arena[:, 4 * QKV_EL:5 * QKV_EL]
        gated_view = arena[:, 0:FQ * T]
        wbuf = [wt[:, 0:WB], wt[:, WB:2 * WB]]
        WTB = max(4 * WB + 2 * FQ * 128, 2 * WB + DT * VW)
        vwbufs = [wt[:, 2 * WB:2 * WB + DT * VW], wt[:, WTB:WTB + DT * VW]]
        uwbuf = [wt[:, 2 * WB:3 * WB], wt[:, 3 * WB:4 * WB]]
        dwbuf = [wt[:, 4 * WB:4 * WB + FQ * 128], wt[:, 4 * WB + FQ * 128:4 * WB + 2 * FQ * 128]]
        qstage = [stage[:, 0:1024], stage[:, 1024:2048]]
        silustage = qstage
        lnb = [sst_sb[:, 1024:1536].bitcast(BF16), sst_sb[:, 1536:2048].bitcast(BF16)]
        lnsq = [sst_sb[:, 2048:2560].bitcast(BF16), sst_sb[:, 2560:3072].bitcast(BF16)]
        vst = [vstage[:, 0:VW], vstage[:, VW:2 * VW]]
        ost = [vstage[:, 0:512], vstage[:, 512:1024]]
        pring = stage[:, 0:2048]

        RT = {}

        def p_init():
            def mkregs(h, tag):
                g1 = h.register(f"r_rk_{tag}")
                RT[f"_g1{tag}"] = g1
                rk = g1.__enter__()
                g2 = h.register(f"r_off_{tag}")
                RT[f"_g2{tag}"] = g2
                off = g2.__enter__()
                pid = h.partition_id()
                h.reg_mov(rk, pid % 4)
                RT[f"rk_{tag}"] = rk
                RT[f"off_{tag}"] = off

            def roff(h, tag, mult, add):
                h.reg_mov(RT[f"off_{tag}"], RT[f"rk_{tag}"])
                if mult != 1:
                    h.reg_alu(RT[f"off_{tag}"], RT[f"off_{tag}"], mult, ALU.mult)
                if add:
                    h.reg_alu(RT[f"off_{tag}"], RT[f"off_{tag}"], add, ALU.add)
                return RT[f"off_{tag}"]
            RT["roff"] = roff
            for eng, tag in (("sync", "sp"), ("scalar", "act"), ("gpsimd", "pl")):
                def fr(h, tag=tag):
                    mkregs(h, tag)
                E.emit(eng, fr)

            def fx(h):
                h.dma_start(out=x_sb[:, :].rearrange("p (t n) -> p t n", t=DT),
                            in_=xT_h.ap().rearrange("(t p) n -> p t n", p=128)).then_inc(sems["ld"], 16)
            E.emit("sync", fx); E.bump("ld", 16)
            for dst, src in [(ident_sb, ident_h), (bqk_sb[0], bqk_h[0]), (bqk_sb[1], bqk_h[1]),
                             (bo_sb[0], bo_h[0]), (bo_sb[1], bo_h[1]),
                             (bvh_sb[0], bv_h[0]), (bvh_sb[1], bv_h[1])]:
                def f2(h, dst=dst, src=src):
                    h.dma_start(out=dst[:, :], in_=src.ap()).then_inc(sems["ld"], 16)
                E.emit("sync", f2); E.bump("ld", 16)

            def fm(h):
                h.dma_start(out=mask01_sb[:, :], in_=masks_h.ap()).then_inc(sems["msk"], 16)
            E.emit("gpsimd", fm); E.bump("msk", 16)

            def fo(h):
                h.memset(ones_c_bf[:, :], 1.0).then_inc(sems["dve"], 1)
                h.memset(ones_r_f[:, :], 1.0).then_inc(sems["dve"], 1)
                h.memset(onescf[:, :], 1.0).then_inc(sems["dve"], 1)
            E.emit("vector", fo); E.bump("dve", 3)

        def p_ln(prev=None):
            ent = E.snap()
            pe_evt = {}
            for dt in range(DT):
                par = dt % 2
                if dt == 0:
                    E.wait_snap("scalar", ent, ("ld", "qs0", "qs1"))
                if dt < 2:
                    E.wait("scalar", "pe", last_ln_pe[0])
                if prev is not None:
                    E.wait("scalar", "dve", prev[dt])
                else:
                    E.wait_snap("scalar", ent, ("dve",))
                if dt >= 2:
                    E.wait("scalar", "pe", pe_evt[dt - 2])
                def fc(h, dt=dt, par=par):
                    h.copy(out=lnb[par][:, :], in_=x_sb[:, dt * T:(dt + 1) * T])
                    h.square(out=lnsq[par][:, :],
                             in_=x_sb[:, dt * T:(dt + 1) * T]).then_inc(sems["act"], 2)
                E.emit("scalar", fc); E.bump("act", 2)
                E.wait("tensor", "act", E.ctr["act"])
                if dt == 0:
                    E.wait("tensor", "act", ent["act"])
                    E.wait("tensor", "dve", prev[0] if prev is not None else ent["dve"])
                first, last = dt == 0, dt == DT - 1
                def fmm(h, par=par, first=first, last=last):
                    for hf in range(TH):
                        h.matmul(ps[2][0:1, hf * 512:hf * 512 + 512], ones_c_bf[:, :],
                                 lnb[par][:, hf * 512:hf * 512 + 512], start=first, stop=last)
                    for hf in range(TH):
                        ins = h.matmul(ps[3][0:1, hf * 512:hf * 512 + 512], ones_c_bf[:, :],
                                       lnsq[par][:, hf * 512:hf * 512 + 512], start=first, stop=last)
                    ins.then_inc(sems["pe"], 1)
                E.emit("tensor", fmm); pe_evt[dt] = E.bump("pe")
            last_ln_pe[0] = E.ctr["pe"]
            E.wait("vector", "pe", E.ctr["pe"])
            inv = 1.0 / float(D)
            def f1(h):
                for hf in range(TH):
                    h.tensor_scalar_mul(out=mrow_sb[0:1, hf * 512:hf * 512 + 512],
                                        in0=ps[2][0:1, hf * 512:hf * 512 + 512], scalar1=inv)
                for hf in range(TH):
                    ins = h.tensor_scalar_mul(out=ex2_sb[0:1, hf * 512:hf * 512 + 512],
                                              in0=ps[3][0:1, hf * 512:hf * 512 + 512], scalar1=inv)
                ins.then_inc(sems["dve"], 1)
            E.emit("vector", f1); E.bump("dve")
            E.wait("scalar", "dve", E.ctr["dve"])
            def f2(h):
                h.square(out=msq_sb[:, :], in_=mrow_sb[:, :]).then_inc(sems["act"], 1)
            E.emit("scalar", f2); E.bump("act")
            E.wait("vector", "act", E.ctr["act"])
            def f3(h):
                h.scalar_tensor_tensor(out=var_sb[:, :], in0=ex2_sb[:, :], scalar=1e-5,
                                       in1=msq_sb[:, :], op0=ALU.add,
                                       op1=ALU.subtract).then_inc(sems["dve"], 1)
            E.emit("vector", f3); E.bump("dve")
            E.wait("scalar", "dve", E.ctr["dve"])
            def f4(h):
                h.activation(out=sd_sb[:, :], in_=var_sb[:, :], func=AF.Sqrt).then_inc(sems["act"], 1)
            E.emit("scalar", f4); E.bump("act")
            E.wait("vector", "act", E.ctr["act"])
            def f5(h):
                h.reciprocal(out=rstd_sb[:, :], in_=sd_sb[:, :]).then_inc(sems["dve"], 1)
            E.emit("vector", f5); E.bump("dve")
            E.wait("tensor", "dve", E.ctr["dve"])
            def f7(h):
                for hf in range(TH):
                    h.matmul(ps[0][:, hf * 512:hf * 512 + 512], ones_r_f[:, :],
                             mrow_sb[0:1, hf * 512:hf * 512 + 512], start=True, stop=True)
                for hf in range(TH):
                    ins = h.matmul(ps[1][:, hf * 512:hf * 512 + 512], ones_r_f[:, :],
                                   rstd_sb[0:1, hf * 512:hf * 512 + 512], start=True, stop=True)
                ins.then_inc(sems["pe"], 1)
            E.emit("tensor", f7); E.bump("pe")
            E.wait("vector", "pe", E.ctr["pe"])
            dve_base = E.ctr["dve"]
            t1_sb = sst_sb[:, 0:T]
            norm_evts = []
            for dt in range(DT):
                def fn(h, dt=dt, w=dve_base + 2 * dt, t1_sb=t1_sb):
                    h.wait_ge(sems["dve"], w)
                    h.tensor_sub(out=t1_sb[:, :], in0=x_sb[:, dt * T:(dt + 1) * T],
                                 in1=ps[0][:, 0:T]).then_inc(sems["dve"], 1)
                    h.wait_ge(sems["dve"], w + 1)
                    h.tensor_mul(out=xn_sb[:, dt * T:(dt + 1) * T], in0=t1_sb[:, :],
                                 in1=ps[1][:, 0:T]).then_inc(sems["dve"], 1)
                E.emit("vector", fn); norm_evts.append(E.bump("dve", 2))
            return norm_evts

        def wcast(dst, src_h, off, tiles, width, sem):
            def f(h, dst=dst, src_h=src_h, off=off, tiles=tiles, width=width, sem=sem):
                h.dma_start(out=dst[:, :].rearrange("p (t f) -> p t f", t=tiles),
                            in_=bass.AP(src_h, off, [[tiles * width, 128], [width, tiles], [1, width]])
                            ).then_inc(sems[sem], 16)
            E.emit("gpsimd", f)
            return E.bump(sem, 16)

        def p_attn(ly, xn_evts):
            xn_dve = xn_evts[-1]
            ent = E.snap()
            E.wait("sync", "cc", ent["cc"])
            cc_base = ent["cc"]
            v_pe, v_act = {}, {}
            E.wait("tensor", "ld", ent["ld"])
            for fc in range(D // VW):
                vpar = fc % 2
                vwbuf = vwbufs[vpar]
                vcs = "vc" if vpar == 0 else "vc1"
                if fc >= 2:
                    E.wait("gpsimd", "pe", v_pe[(fc - 2, T // 128 - 1)])
                else:
                    E.wait_snap("gpsimd", ent, ("pe", "fwb0", "fwb1", "dwb0", "dwb1",
                                                "wl0", "wl1", "dl0", "dl1"))
                wev = wcast(vwbuf, wv_h[ly], fc * VSLAB, DT, VW, vcs)
                E.wait("tensor", vcs, wev)
                for tb in range(T // 128):
                    par = tb % 2
                    psv = ps[2 + par]
                    if tb >= 2:
                        E.wait("tensor", "act", v_act[(fc, tb - 2)])
                    elif fc > 0:
                        E.wait("tensor", "act", v_act[(fc - 1, T // 128 - 2 + tb)])
                    else:
                        E.wait_snap("tensor", ent, ("act",))
                        if tb == 1:
                            E.wait_snap("tensor", ent, ("dve",))
                    if fc == 0 and tb == 0:
                        cs = DT // 4
                        for c4 in range(4):
                            E.wait("tensor", "dve", xn_evts[c4 * cs + cs - 1])
                            def fmmc(h, c4=c4, cs=cs, psv=psv, vwbuf=vwbuf, tb=tb):
                                for dt in range(c4 * cs, c4 * cs + cs):
                                    ins = h.matmul(psv[:, 0:VW],
                                                   xn_sb[:, dt * T + tb * 128: dt * T + (tb + 1) * 128],
                                                   vwbuf[:, dt * VW:(dt + 1) * VW],
                                                   start=(dt == 0), stop=(dt == DT - 1))
                                if c4 == 3:
                                    ins.then_inc(sems["pe"], 1)
                            E.emit("tensor", fmmc)
                    else:
                        def fmm(h, tb=tb, fc=fc, psv=psv, vwbuf=vwbuf):
                            for dt in range(DT):
                                ins = h.matmul(psv[:, 0:VW],
                                               xn_sb[:, dt * T + tb * 128: dt * T + (tb + 1) * 128],
                                               vwbuf[:, dt * VW:(dt + 1) * VW],
                                               start=(dt == 0), stop=(dt == DT - 1))
                            ins.then_inc(sems["pe"], 1)
                        E.emit("tensor", fmm)
                    v_pe[(fc, tb)] = E.bump("pe")
                    E.wait("scalar", "pe", v_pe[(fc, tb)])
                    E.wait("scalar", f"vs{par}", E.ctr[f"vs{par}"])
                    E.wait("scalar", "os0", E.ctr["os0"])
                    def fev(h, par=par, psv=psv):
                        h.copy(out=vst[par][:, :], in_=psv[:, 0:VW]).then_inc(sems["act"], 1)
                    E.emit("scalar", fev); v_act[(fc, tb)] = E.bump("act")
                    E.wait("sync", "act", v_act[(fc, tb)])
                    def fst(h, fc=fc, tb=tb, par=par):
                        voff = 2 * D * T + (tb * 128) * D + fc * VW
                        h.dma_start(out=bass.AP(qkv_local, voff, [[D, 128], [1, VW]]),
                                    in_=vst[par][:, :]).then_inc(sems[f"vs{par}"], 16)
                    E.emit("sync", fst); E.bump(f"vs{par}", 16)
            vs_snap = (E.ctr["vs0"], E.ctr["vs1"])
            def fccv(h):
                h.collective_compute("AllGather", ALU.bypass,
                                     replica_groups=[[0, 1, 2, 3], [4, 5, 6, 7]],
                                     ins=[qkv_local[2 * D:3 * D, :].opt()],
                                     outs=[bass.AP(qkv_gath, VBASE, [[1, 4 * D * T]])]
                                     ).then_inc(sems["cc"])
            pe_evt, act_evt = {}, {}
            fb_order = list(range(DT, 2 * DT)) + list(range(DT))
            pending_cc = []
            for it, fb in enumerate(fb_order):
                par = it % 2
                wsem = f"wc{par}"
                if it >= 2:
                    E.wait("gpsimd", "pe", pe_evt[it - 2])
                else:
                    E.wait_snap("gpsimd", ent, ("pe", "fwb0", "fwb1", "wl0", "wl1"))
                if it == 2:
                    E.wait("gpsimd", "vs0", vs_snap[0])
                    E.wait("gpsimd", "vs1", vs_snap[1])
                    for s in ("hl", "vla", "vlb"):
                        E.wait("gpsimd", s, E.ctr[s])
                    E.emit("gpsimd", fccv); E.bump("cc")
                while pending_cc and pending_cc[0][0] <= it:
                    _, fcc_fn, q0, q1 = pending_cc.pop(0)
                    E.wait("gpsimd", "qs0", q0)
                    E.wait("gpsimd", "qs1", q1)
                    E.emit("gpsimd", fcc_fn); E.bump("cc")
                wev = wcast(wbuf[par], wqk_h[ly], fb * SLAB, DT, 128, wsem)
                E.wait("tensor", wsem, wev)
                E.wait("tensor", "dve", xn_dve)
                if it >= 2:
                    E.wait("tensor", "act", act_evt[it - 2])
                def fmm(h, par=par):
                    for dt in range(DT):
                        for hf in range(TH):
                            ins = h.matmul(ps[par][:, hf * 512:hf * 512 + 512],
                                           wbuf[par][:, dt * 128:(dt + 1) * 128],
                                           xn_sb[:, dt * T + hf * 512: dt * T + hf * 512 + 512],
                                           start=(dt == 0), stop=(dt == DT - 1))
                    ins.then_inc(sems["pe"], 1)
                E.emit("tensor", fmm); pe_evt[it] = E.bump("pe")
                E.wait("scalar", "pe", pe_evt[it])
                E.wait("scalar", f"qs{par}", E.ctr[f"qs{par}"])
                E.wait_snap("scalar", ent, ("dve",))
                def fev(h, fb=fb, par=par):
                    h.add(out=qstage[par][:, 0:T], in_=ps[par][:, 0:T],
                          add=bqk_sb[ly][:, fb:fb + 1]).then_inc(sems["act"], 1)
                E.emit("scalar", fev); act_evt[it] = E.bump("act")
                E.wait("sync", "act", act_evt[it])
                def fst(h, fb=fb, par=par):
                    h.dma_start(out=bass.AP(qkv_local, fb * 128 * T, [[T, 128], [1, T]]),
                                in_=qstage[par][:, 0:T]).then_inc(sems[f"qs{par}"], 16)
                E.emit("sync", fst); E.bump(f"qs{par}", 16)
                if it == DT - 1:
                    def fcck(h):
                        h.collective_compute("AllGather", ALU.bypass,
                                             replica_groups=[[0, 1, 2, 3], [4, 5, 6, 7]],
                                             ins=[qkv_local[D:2 * D, :].opt()],
                                             outs=[qkv_gath[0:4 * D, :].opt()]
                                             ).then_inc(sems["cc"])
                    pending_cc.append((it + 4, fcck, E.ctr["qs0"], E.ctr["qs1"]))
                if it == 2 * DT - 1:
                    def fccq(h):
                        h.collective_compute("AllGather", ALU.bypass,
                                             replica_groups=[[0, 1, 2, 3], [4, 5, 6, 7]],
                                             ins=[qkv_local[0:D, :].opt()],
                                             outs=[qkv_gath[4 * D:8 * D, :].opt()]
                                             ).then_inc(sems["cc"])
                    pending_cc.append((it + 4, fccq, E.ctr["qs0"], E.ctr["qs1"]))
            for _, fcc_fn, q0, q1 in pending_cc:
                E.wait("gpsimd", "qs0", q0)
                E.wait("gpsimd", "qs1", q1)
                E.emit("gpsimd", fcc_fn); E.bump("cc")
            pending_cc = []
            cc_all = E.ctr["cc"]
            E.wait("sync", "cc", cc_all)
            E.wait("scalar" if ly == 0 else "gpsimd", "cc", cc_all)
            head_done_pe = E.ctr["pe"]
            a4ent = E.snap()
            SBANKS = [ps[0][:, 0:512], ps[0][:, 512:1024],
                      ps[1][:, 0:512], ps[1][:, 512:1024],
                      ps[2][:, 0:512], ps[2][:, 512:1024]]
            SPAIRS = [ps[0], ps[1], ps[2]]
            first_scores = [True]
            fr3_guard = [0]
            last_bank_exp = [a4ent["act"]] * 6
            av_ring = []
            ring_ctr = [0]
            veng = "scalar" if ly == 0 else "gpsimd"
            vtag = "act" if ly == 0 else "pl"
            vsem = "vla" if ly == 0 else "vlb"
            pend_o = []
            qk_mult = CH if HPC * 128 == 512 else HPC * 128 * T

            def head_load(hp, guard_pe):
                hpar = hp % 2
                hls = "hl" if hpar == 0 else "hl1"
                vls = vsem if hpar == 0 else vsem + "1"
                E.wait("sync", "pe", guard_pe)
                E.wait(veng, "pe", guard_pe)
                def fld(h, hp=hp, hpar=hpar, hls=hls):
                    ro = RT["roff"](h, "sp", HPC * 128 * T, hp * 128 * T + 4 * D * T)
                    h.dma_start(out=Q_views[hpar][:, :].rearrange("p (r t) -> p r t", r=4),
                                in_=bass.AP(qkv_gath, ro,
                                            [[T, 128], [D * T, 4], [1, T]])).then_inc(sems[hls], 16)
                    ro = RT["roff"](h, "sp", HPC * 128 * T, hp * 128 * T)
                    h.dma_start(out=K_views[hpar][:, :].rearrange("p (r t) -> p r t", r=4),
                                in_=bass.AP(qkv_gath, ro,
                                            [[T, 128], [D * T, 4], [1, T]])).then_inc(sems[hls], 16)
                E.emit("sync", fld)
                qk_ev = E.bump(hls, 16 * 2)
                return (hls, qk_ev)

            done_pe = {}
            loads = {0: head_load(0, ent["pe"])}
            for hp in range(HPC):
                if pend_o:
                    fcc_fn, o0, o1 = pend_o.pop(0)
                    E.wait("gpsimd", "os0", o0)
                    E.wait("gpsimd", "os1", o1)
                    E.emit("gpsimd", fcc_fn); E.bump("cc")
                if hp + 1 < HPC:
                    loads[hp + 1] = head_load(hp + 1, done_pe.get(hp - 1, ent["pe"]))
                E.wait(veng, "pe", done_pe.get(hp - 1, ent["pe"]))
                def fldv(h, hp=hp):
                    for r in range(4):
                        ro = RT["roff"](h, vtag, HPC * 128, hp * 128 + VBASE + r * D * T)
                        h.dma_start(out=V_view[:, r * T:(r + 1) * T].rearrange(
                                        "p (b f) -> p b f", b=T // 128),
                                    in_=bass.AP(qkv_gath, ro,
                                                [[D, 128], [128 * D, T // 128], [1, 128]])
                                    ).then_inc(sems[vsem], 16)
                E.emit(veng, fldv)
                v_ev = E.bump(vsem, 16 * 4)
                hls, qk_ev = loads[hp]
                E.wait("tensor", hls, qk_ev)
                hpar = hp % 2
                Q_view, K_view = Q_views[hpar], K_views[hpar]
                for qt in range(QT):
                    nkb = 4 * (qt + 1)
                    bq = (hp * QT + qt) % 2
                    exp_evt, zz_evt, mm_evt = {}, {}, {}
                    bias_act = [0]
                    av_last = 0
                    for g in range(nkb + 6):
                        e = g - 6
                        if 0 <= e < nkb:
                            rs = ring_ctr[0] % 4
                            ring_ctr[0] += 1
                            diag = e >= 4 * qt
                            E.wait("scalar", "pe", mm_evt[e])
                            E.wait("scalar", "act", bias_act[0])
                            if len(av_ring) >= 4:
                                E.wait("scalar", "pe", av_ring[-4])
                            def fex(h, e=e, rs=rs, bq=bq):
                                h.activation(out=pring[:, rs * 512:(rs + 1) * 512],
                                             in_=SBANKS[e % 6], func=AF.Exp,
                                             bias=bias_sb[:, bq:bq + 1],
                                             scale=C_SM).then_inc(sems["act"], 1)
                            E.emit("scalar", fex)
                            ev = E.bump("act")
                            exp_evt[e] = (ev, rs)
                            last_bank_exp[e % 6] = ev
                            if diag:
                                pat = e - 4 * qt
                                E.wait("vector", "act", ev)
                                E.wait("vector", "msk", 16)
                                def fzz(h, rs=rs, pat=pat):
                                    h.tensor_mul(out=pring[:, rs * 512:(rs + 1) * 512],
                                                 in0=pring[:, rs * 512:(rs + 1) * 512],
                                                 in1=mask01_sb[:, pat * 512:(pat + 1) * 512]
                                                 ).then_inc(sems["dve"], 1)
                                E.emit("vector", fzz)
                                zz_evt[e] = E.bump("dve")
                        if g < nkb:
                            E.wait("tensor", "act", last_bank_exp[g % 6])
                            if first_scores[0]:
                                E.wait("tensor", "dve", xn_dve)
                                E.wait_snap("tensor", a4ent, ("dve", "act"))
                                first_scores[0] = False
                            def fsc(h, kb=g, qt=qt, si=g % 6, KV=K_view, QV=Q_view):
                                h.matmul(SBANKS[si], KV[:, kb * 128:(kb + 1) * 128],
                                         QV[:, qt * 512:(qt + 1) * 512],
                                         start=True, stop=True).then_inc(sems["pe"], 1)
                            E.emit("tensor", fsc)
                            mm_evt[g] = E.bump("pe")
                            if g == 1:
                                E.wait("vector", "pe", mm_evt[1])
                                def fpr(h, w=E.ctr["dve"]):
                                    h.wait_ge(sems["dve"], w)
                                    h.tensor_reduce(out=Mp_sb[:, 0:1], in_=ps[0][:, 0:1024],
                                                    axis=AX.X, op=ALU.max).then_inc(sems["dve"], 1)
                                E.emit("vector", fpr); pr_evt = E.bump("dve")
                            if g == 2:
                                E.wait("tensor", "dve", pr_evt)
                                if fr3_guard[0]:
                                    E.wait("tensor", "dve", fr3_guard[0])
                                E.wait("tensor", "ld", ent["ld"])
                                def ftr(h):
                                    h.transpose(ps[3][0:1, 512:640], Mp_sb[:, 0:1],
                                                ident_sb[:, :]).then_inc(sems["pe"], 1)
                                E.emit("tensor", ftr); tr_evt = E.bump("pe")
                                E.wait("vector", "pe", tr_evt)
                                def fmu(h):
                                    h.tensor_reduce(out=mu_sb[0:1, 0:1],
                                                    in_=ps[3][0:1, 512:640], axis=AX.X,
                                                    op=ALU.max).then_inc(sems["dve"], 1)
                                E.emit("vector", fmu); E.bump("dve")
                                E.wait("tensor", "dve", E.ctr["dve"])
                                def fb4(h):
                                    h.matmul(ps[3][:, 640:641], ones_r_f[:, :],
                                             mu_sb[0:1, 0:1], start=True,
                                             stop=True).then_inc(sems["pe"], 1)
                                E.emit("tensor", fb4); b4 = E.bump("pe")
                                E.wait("scalar", "pe", b4)
                                def fbi(h, bq=bq):
                                    h.activation(out=bias_sb[:, bq:bq + 1],
                                                 in_=ps[3][:, 640:641], func=AF.Copy,
                                                 scale=-C_SM).then_inc(sems["act"], 1)
                                E.emit("scalar", fbi); bias_act[0] = E.bump("act")
                        if 0 <= e < nkb:
                            ev, rs = exp_evt[e]
                            E.wait("tensor", "act", ev)
                            if e in zz_evt:
                                E.wait("tensor", "dve", zz_evt[e])
                            if e == 0 and fr3_guard[0]:
                                E.wait("tensor", "dve", fr3_guard[0])
                            if e == 0 and qt == 0:
                                E.wait("tensor", vsem, v_ev)
                            def fav(h, kb=e, rs=rs, first=(e == 0), last=(e == nkb - 1),
                                    VV=V_view):
                                h.matmul(ps[3][:, 0:512], VV[:, kb * 128:(kb + 1) * 128],
                                         pring[:, rs * 512:(rs + 1) * 512],
                                         start=first, stop=last)
                                h.matmul(ps[3][0:1, 512:1024], ones_c_bf[:, :],
                                         pring[:, rs * 512:(rs + 1) * 512],
                                         start=first, stop=last).then_inc(sems["pe"], 1)
                            E.emit("tensor", fav)
                            av_ring.append(E.bump("pe"))
                            av_last = av_ring[-1]
                    E.wait("vector", "pe", av_last)
                    def fr1(h, w=E.ctr["dve"]):
                        h.wait_ge(sems["dve"], w)
                        h.reciprocal(out=rec_sb[:, :],
                                     in_=ps[3][0:1, 512:1024]).then_inc(sems["dve"], 1)
                    E.emit("vector", fr1); E.bump("dve")
                    E.wait("tensor", "dve", E.ctr["dve"])
                    def fr2(h):
                        h.matmul(ps[2][:, 0:512], ones_r_f[:, :], rec_sb[:, :],
                                 start=True, stop=True).then_inc(sems["pe"], 1)
                    E.emit("tensor", fr2); r2 = E.bump("pe")
                    E.wait("scalar", "pe", r2)
                    E.wait("scalar", "dve", fr3_guard[0])
                    def frc(h):
                        h.copy(out=recb_sb[:, :], in_=ps[2][:, 0:512]).then_inc(sems["act"], 1)
                    E.emit("scalar", frc)
                    last_bank_exp[4] = E.bump("act")
                    ix = hp * QT + qt
                    opar2 = ix % 2
                    E.wait("vector", "act", last_bank_exp[4])
                    E.wait("vector", f"os{opar2}", E.ctr[f"os{opar2}"])
                    E.wait("vector", "vs0", E.ctr["vs0"])
                    E.wait("vector", "vs1", E.ctr["vs1"])
                    def fr3(h, opar2=opar2):
                        h.tensor_mul(out=ost[opar2][:, :], in0=ps[3][:, 0:512],
                                     in1=recb_sb[:, :]).then_inc(sems["dve"], 1)
                    E.emit("vector", fr3); E.bump("dve")
                    fr3_guard[0] = E.ctr["dve"]
                    E.wait("sync", "dve", E.ctr["dve"])
                    def fr4(h, hp=hp, qt=qt, opar2=opar2):
                        h.dma_start(out=bass.AP(o_local, hp * 128 * S + qt * 512, [[S, 128], [1, 512]]),
                                    in_=ost[opar2][:, :]).then_inc(sems[f"os{opar2}"], 16)
                    E.emit("sync", fr4); E.bump(f"os{opar2}", 16)
                done_pe[hp] = E.ctr["pe"]
                E.wait("gpsimd", "ota", ent["ota"])
                E.wait("gpsimd", "otb", ent["otb"])
                def fcc2(h, hp=hp):
                    h.collective_compute("AllGather", ALU.bypass,
                                         replica_groups=[[0, 1, 2, 3], [4, 5, 6, 7]],
                                         ins=[o_local[hp * 128:(hp + 1) * 128, :].opt()],
                                         outs=[o_gath[hp * 4 * 128:(hp + 1) * 4 * 128, :].opt()]
                                         ).then_inc(sems["cc"])
                pend_o.append((fcc2, E.ctr["os0"], E.ctr["os1"]))
            for fcc_fn, o0, o1 in pend_o:
                E.wait("gpsimd", "os0", o0)
                E.wait("gpsimd", "os1", o1)
                E.emit("gpsimd", fcc_fn); E.bump("cc")
            cc_o = E.ctr["cc"]
            osem = "ota" if ly == 0 else "otb"
            E.wait(veng, "cc", cc_o)
            E.wait(veng, "pe", E.ctr["pe"])
            def fot(h):
                for r in range(4):
                    ro = RT["roff"](h, vtag, T, r * 128 * S)
                    h.dma_start(out=xn_sb[:, (r * HPC) * T:(r * HPC + HPC) * T].rearrange(
                                    "p (hh n) -> p hh n", hh=HPC),
                                in_=bass.AP(o_gath, ro,
                                            [[S, 128], [4 * 128 * S, HPC], [1, T]])).then_inc(sems[osem], 16)
            E.emit(veng, fot)
            ot_ld = E.bump(osem, 16 * 4)
            E.wait("vector", osem, ot_ld)
            E.wait("vector", "ld", ent["ld"])
            for hh in range(NH):
                def fbv(h, hh=hh, ly=ly):
                    h.tensor_scalar_add(out=xn_sb[:, hh * T:(hh + 1) * T],
                                        in0=xn_sb[:, hh * T:(hh + 1) * T],
                                        scalar1=bvh_sb[ly][:, hh:hh + 1]).then_inc(sems["dve"], 1)
                E.emit("vector", fbv); E.bump("dve")
            ot_dve = E.ctr["dve"]
            op_pe, op_dve = {}, {}
            for db in range(DT):
                par = db % 2
                wsem = f"wc{par}"
                if db >= 2:
                    E.wait("gpsimd", "pe", op_pe[db - 2])
                else:
                    E.wait("gpsimd", "pe", pe_evt[2 * DT - 2 + db])
                wev = wcast(wbuf[par], wo_h[ly], db * SLAB, DT, 128, wsem)
                E.wait("tensor", wsem, wev)
                E.wait("tensor", osem, ot_ld)
                E.wait("tensor", "dve", ot_dve)
                if db >= 2:
                    E.wait("tensor", "dve", op_dve[db - 2])
                def fmm(h, par=par):
                    for ht in range(NH):
                        for hf in range(TH):
                            ins = h.matmul(ps[par][:, hf * 512:hf * 512 + 512],
                                           wbuf[par][:, ht * 128:(ht + 1) * 128],
                                           xn_sb[:, ht * T + hf * 512: ht * T + hf * 512 + 512],
                                           start=(ht == 0), stop=(ht == NH - 1))
                    ins.then_inc(sems["pe"], 1)
                E.emit("tensor", fmm); op_pe[db] = E.bump("pe")
                E.wait("vector", "pe", op_pe[db])
                def fres(h, db=db, par=par):
                    h.scalar_tensor_tensor(out=x_sb[:, db * T:(db + 1) * T], in0=ps[par][:, 0:T],
                                           scalar=bo_sb[ly][:, db:db + 1],
                                           in1=x_sb[:, db * T:(db + 1) * T],
                                           op0=ALU.add, op1=ALU.add).then_inc(sems["dve"], 1)
                E.emit("vector", fres); op_dve[db] = E.bump("dve")
            return op_dve

        def p_ffn(second, xn_evts):
            xn_dve = xn_evts[-1]
            ent = E.snap()
            g_pe, g_act, g_dve = {}, {}, {}
            d_pe_g, d_dve_g = {}, {}
            down_guard = 0
            for hq in range(8):
                for fb in range(FQ):
                    gb = hq * FQ + fb
                    par = fb % 2
                    weng = "gpsimd" if not second else "sync"
                    if gb >= 2:
                        E.wait(weng, "pe", g_pe[(gb - 2, "u")])
                    else:
                        E.wait_snap(weng, ent, ("pe", "fwb0", "fwb1"))
                    if not second:
                        E.wait("gpsimd", f"fwb{par}", E.ctr[f"fwb{par}"])
                        cg = wcast(wbuf[par], wg_h, gb * SLAB, DT, 128, f"wc{par}")
                        cu = wcast(uwbuf[par], wu_h, gb * SLAB, DT, 128, f"uc{par}")
                        E.wait("sync", f"wc{par}", cg)
                        E.wait("sync", f"uc{par}", cu)
                        def fwb(h, gb=gb, par=par):
                            h.dma_start(out=bass.AP(wg_bf, gb * SLAB, [[SLAB // 128, 128], [1, SLAB // 128]]),
                                        in_=wbuf[par][:, :]).then_inc(sems[f"fwb{par}"], 16)
                            h.dma_start(out=bass.AP(wu_bf, gb * SLAB, [[SLAB // 128, 128], [1, SLAB // 128]]),
                                        in_=uwbuf[par][:, :]).then_inc(sems[f"fwb{par}"], 16)
                        E.emit("sync", fwb); E.bump(f"fwb{par}", 32)
                        E.wait("tensor", f"wc{par}", cg)
                        E.wait("tensor", f"uc{par}", cu)
                    else:
                        def fw(h, gb=gb, par=par):
                            h.dma_start(out=wbuf[par][:, :].rearrange("p (t f) -> p t f", t=DT),
                                        in_=bass.AP(wg_bf, gb * SLAB,
                                                    [[SLAB // 128, 128], [128, DT], [1, 128]])
                                        ).then_inc(sems[f"wl{par}"], 16)
                            h.dma_start(out=uwbuf[par][:, :].rearrange("p (t f) -> p t f", t=DT),
                                        in_=bass.AP(wu_bf, gb * SLAB,
                                                    [[SLAB // 128, 128], [128, DT], [1, 128]])
                                        ).then_inc(sems[f"wl{par}"], 16)
                        E.emit("sync", fw)
                        wlev = E.bump(f"wl{par}", 32)
                        E.wait("tensor", f"wl{par}", wlev)
                    E.wait("tensor", "dve", xn_dve)
                    E.wait("tensor", "dve", down_guard)
                    if gb >= 2:
                        E.wait("tensor", "act", g_act[gb - 2])
                        E.wait("tensor", "dve", g_dve[gb - 2])
                    def fg(h, par=par):
                        for dt in range(DT):
                            for hf in range(TH):
                                ins = h.matmul(ps[par][:, hf * 512:hf * 512 + 512],
                                               wbuf[par][:, dt * 128:(dt + 1) * 128],
                                               xn_sb[:, dt * T + hf * 512: dt * T + hf * 512 + 512],
                                               start=(dt == 0), stop=(dt == DT - 1))
                        ins.then_inc(sems["pe"], 1)
                    E.emit("tensor", fg); g_pe[(gb, "g")] = E.bump("pe")
                    def fu(h, par=par):
                        for dt in range(DT):
                            for hf in range(TH):
                                ins = h.matmul(ps[2 + par][:, hf * 512:hf * 512 + 512],
                                               uwbuf[par][:, dt * 128:(dt + 1) * 128],
                                               xn_sb[:, dt * T + hf * 512: dt * T + hf * 512 + 512],
                                               start=(dt == 0), stop=(dt == DT - 1))
                        ins.then_inc(sems["pe"], 1)
                    E.emit("tensor", fu); g_pe[(gb, "u")] = E.bump("pe")
                    E.wait("scalar", "pe", g_pe[(gb, "g")])
                    if gb >= 2:
                        E.wait("scalar", "dve", g_dve[gb - 2])
                    else:
                        E.wait_snap("scalar", ent, ("dve", "qs0", "qs1"))
                    def fsi(h, par=par):
                        h.activation(out=silustage[par][:, 0:T], in_=ps[par][:, 0:T],
                                     func=AF.Silu).then_inc(sems["act"], 1)
                    E.emit("scalar", fsi); g_act[gb] = E.bump("act")
                    E.wait("vector", "pe", g_pe[(gb, "u")])
                    E.wait("vector", "act", g_act[gb])
                    def fgm(h, fb=fb, par=par):
                        h.tensor_mul(out=gated_view[:, fb * T:(fb + 1) * T], in0=ps[2 + par][:, 0:T],
                                     in1=silustage[par][:, 0:T]).then_inc(sems["dve"], 1)
                    E.emit("vector", fgm); g_dve[gb] = E.bump("dve")
                gate_dve = E.ctr["dve"]
                for db in range(DT):
                    par = db % 2
                    sl = hq * DT + db
                    weng = "gpsimd" if not second else "sync"
                    if sl >= 2:
                        E.wait(weng, "pe", d_pe_g[sl - 2])
                    else:
                        E.wait_snap(weng, ent, ("pe", "dwb0", "dwb1"))
                    if not second:
                        E.wait("gpsimd", f"dwb{par}", E.ctr[f"dwb{par}"])
                        cd = wcast(dwbuf[par], wd_h, sl * DSLAB, FQ, 128, f"dc{par}")
                        E.wait("sync", f"dc{par}", cd)
                        def fwb(h, sl=sl, par=par):
                            h.dma_start(out=bass.AP(wd_bf, sl * DSLAB,
                                                    [[DSLAB // 128, 128], [1, DSLAB // 128]]),
                                        in_=dwbuf[par][:, :]).then_inc(sems[f"dwb{par}"], 16)
                        E.emit("sync", fwb); E.bump(f"dwb{par}", 16)
                        E.wait("tensor", f"dc{par}", cd)
                    else:
                        def fw(h, sl=sl, par=par):
                            h.dma_start(out=dwbuf[par][:, :].rearrange("p (t f) -> p t f", t=FQ),
                                        in_=bass.AP(wd_bf, sl * DSLAB,
                                                    [[DSLAB // 128, 128], [128, FQ], [1, 128]])
                                        ).then_inc(sems[f"dl{par}"], 16)
                        E.emit("sync", fw)
                        dlev = E.bump(f"dl{par}", 16)
                        E.wait("tensor", f"dl{par}", dlev)
                    E.wait("tensor", "dve", gate_dve)
                    if sl >= 2:
                        E.wait("tensor", "dve", d_dve_g.get(sl - 2, 0))
                    def fdm(h, par=par):
                        for hb in range(FQ):
                            for hf in range(TH):
                                ins = h.matmul(ps[par][:, hf * 512:hf * 512 + 512],
                                               dwbuf[par][:, hb * 128:(hb + 1) * 128],
                                               gated_view[:, hb * T + hf * 512: hb * T + hf * 512 + 512],
                                               start=(hb == 0), stop=(hb == FQ - 1))
                        ins.then_inc(sems["pe"], 1)
                    E.emit("tensor", fdm); d_pe_g[sl] = E.bump("pe")
                    E.wait("vector", "pe", d_pe_g[sl])
                    def fres(h, db=db, par=par):
                        h.scalar_tensor_tensor(out=x_sb[:, db * T:(db + 1) * T], in0=ps[par][:, 0:T],
                                               scalar=0.0, in1=x_sb[:, db * T:(db + 1) * T],
                                               op0=ALU.add, op1=ALU.add).then_inc(sems["dve"], 1)
                    E.emit("vector", fres); d_dve_g[sl] = E.bump("dve")
                down_guard = E.ctr["dve"]
            return {db: d_dve_g[7 * DT + db] for db in range(DT)}

        fold_guard = [0, 0]
        last_ln_pe = [0]

        p_init()
        xn1 = p_ln()
        prevA = p_attn(0, xn1)
        xn2 = p_ln(prevA)
        prevF = p_ffn(False, xn2)
        xn3 = p_ln(prevF)
        prevB = p_attn(1, xn3)
        xn4 = p_ln(prevB)
        p_ffn(True, xn4)

        E.wait("sync", "dve", E.ctr["dve"])
        def ffin(h):
            h.dma_start(out=yT_h.ap().rearrange("(t p) n -> p t n", p=128),
                        in_=x_sb[:, :].rearrange("p (t n) -> p t n", t=DT)).then_inc(sems["out"], 16)
        E.emit("sync", ffin)
        E.bump("out", 16)
        E.wait("sync", "out", 16)

        with nc.Block() as block:
            @block.sync
            def _(h):
                for f in E.q["sync"]:
                    f(h)

            @block.tensor
            def _(h):
                for f in E.q["tensor"]:
                    f(h)

            @block.scalar
            def _(h):
                for f in E.q["scalar"]:
                    f(h)

            @block.vector
            def _(h):
                for f in E.q["vector"]:
                    f(h)

            @block.gpsimd
            def _(h):
                for f in E.q["gpsimd"]:
                    f(h)

    return nc



def make_in_maps(cfg, inputs):
    shared = host_inputs(cfg, inputs)
    x = np.asarray(inputs["L0_attn_a_x"], dtype=np.float32)
    in_maps = []
    for c in range(8):
        g, r = c // 4, c % 4
        xs = np.ascontiguousarray(x[g, r * cfg.T:(r + 1) * cfg.T, :].T)
        m = {"xT": xs}
        m.update(shared)
        in_maps.append(m)
    return in_maps


def assemble_out(cfg, results):
    out = np.empty((cfg.B, cfg.S, cfg.D), dtype=np.float32)
    for c in range(8):
        g, r = c // 4, c % 4
        out[g, r * cfg.T:(r + 1) * cfg.T, :] = results[c]["yT"].T
    return out


def kernel(**inputs):
    cfg = CFG
    nc = build_program(cfg)
    in_maps = make_in_maps(cfg, inputs)
    res = run_bass_kernel_spmd(nc, in_maps, core_ids=list(range(8)))
    return assemble_out(cfg, res.results)
